# version 06
# baseline (speedup 1.0000x reference)
import numpy as np
import ml_dtypes
import concourse.bass as bass
import concourse.mybir as mybir
from concourse.bass_utils import run_bass_kernel_spmd

F32 = mybir.dt.float32
BF16 = mybir.dt.bfloat16
I32 = mybir.dt.int32
AF = mybir.ActivationFunctionType
ALU = mybir.AluOpType
AX = mybir.AxisListType

NCORES = 8
NEG = -1e9
EPS = 1e-5

REAL = dict(DT=32, HD=128, TBC=8, IT=86, NH=32)
TINY = dict(DT=16, HD=128, TBC=8, IT=40, NH=16)


def cfg_derive(cfg):
    c = dict(cfg)
    c["D"] = c["DT"] * 128
    c["NH"] = c.get("NH", 32)
    c["NG"] = 8
    assert c["NH"] % 8 == 0
    assert c["NH"] * c["HD"] == c["D"]
    c["KVD"] = 2 * c["HD"] * c["NG"]
    c["KVT"] = c["KVD"] // 128
    assert c["KVD"] % 128 == 0
    c["TC"] = c["TBC"] * 128
    c["SEQ"] = 2 * c["TC"]
    c["INTER"] = c["IT"] * 128
    assert (c["D"] // 2) % (NCORES * 32) == 0
    c["GPC"] = c["D"] // 2 // NCORES
    c["GGRP"] = c["GPC"] // 32
    c["HPT"] = 2 * c["GPC"] // 128
    assert (2 * c["GPC"]) % 128 == 0 and c["HPT"] >= 1
    c["OPC"] = c["D"] // NCORES
    c["OTC"] = c["OPC"] // 128
    assert c["OPC"] % 128 == 0
    c["PC"] = c["INTER"] // 2 // 128
    assert (c["INTER"] // 2) % 128 == 0
    c["DGRP"] = c["INTER"] // 64
    it = c["IT"]
    NCH = 16
    base, rem = it // NCH, it % NCH
    c["ICH"] = [base + (1 if i < rem else 0) for i in range(NCH)]
    return c


def mk(t, off, dims):
    return bass.AP(t, off, dims)


def build_nc(cfg, dbg=False):
    import contextlib

    c = cfg_derive(cfg)
    DT, HD, TBC, IT = c["DT"], c["HD"], c["TBC"], c["IT"]
    D, NH, NG, KVT, TC = c["D"], c["NH"], c["NG"], c["KVT"], c["TC"]
    KVD = c["KVD"]
    INTER, GPC, GGRP, HPT = c["INTER"], c["GPC"], c["GGRP"], c["HPT"]
    OPC, OTC, PC, ICH, DGRP = c["OPC"], c["OTC"], c["PC"], c["ICH"], c["DGRP"]
    TT = TBC
    NTC = max(TC // 512, 1)
    TCK = TC // NTC
    SCL = 1.0 / float(np.sqrt(HD))
    ALLT = NCORES * TC

    nc = bass.Bass("TRN2", target_bir_lowering=False, debug=False,
                   detect_race_conditions=True)

    x_own = nc.dram_tensor("x_own", [TC, D], F32, kind="ExternalInput")
    wq_blk = nc.dram_tensor("wq_blk", [DT, D, 128], F32, kind="ExternalInput")
    wkv_blk = nc.dram_tensor("wkv_blk", [KVT, D, 128], F32, kind="ExternalInput")
    wo_blk = nc.dram_tensor("wo_blk", [DT, D, 128], F32, kind="ExternalInput")
    bo_st = nc.dram_tensor("bo_st", [128, DT], F32, kind="ExternalInput")
    ident_d = nc.dram_tensor("ident_d", [128, 128], BF16, kind="ExternalInput")
    cmask_d = nc.dram_tensor("cmask_d", [128, 256], F32, kind="ExternalInput")
    kvoffs_d = nc.dram_tensor("kvoffs_d", [1, 4], I32, kind="ExternalInput")
    gate_q = nc.dram_tensor("gate_q", [INTER, GPC], I32, kind="ExternalInput")
    up_q = nc.dram_tensor("up_q", [INTER, GPC], I32, kind="ExternalInput")
    down_q = nc.dram_tensor("down_q", [OPC, INTER // 2], I32, kind="ExternalInput")
    g_sc_d = nc.dram_tensor("g_sc_d", [INTER, GGRP], F32, kind="ExternalInput")
    g_zr_d = nc.dram_tensor("g_zr_d", [INTER, GGRP], F32, kind="ExternalInput")
    u_sc_d = nc.dram_tensor("u_sc_d", [INTER, GGRP], F32, kind="ExternalInput")
    u_zr_d = nc.dram_tensor("u_zr_d", [INTER, GGRP], F32, kind="ExternalInput")
    d_sc_d = nc.dram_tensor("d_sc_d", [OPC, DGRP], F32, kind="ExternalInput")
    d_zr_d = nc.dram_tensor("d_zr_d", [OPC, DGRP], F32, kind="ExternalInput")
    out_d = nc.dram_tensor("out", [TC, D], F32, kind="ExternalOutput")

    kv_in = nc.dram_tensor("kv_in", [KVD, TC], BF16)
    kv_ga = nc.dram_tensor("kv_ga", [NCORES, KVD, TC], BF16)
    wg_pc = nc.dram_tensor("wg_pc", [IT, 2 * GPC, 128], BF16)
    wu_pc = nc.dram_tensor("wu_pc", [IT, 2 * GPC, 128], BF16)
    wg_ga = nc.dram_tensor("wg_ga", [NCORES, IT, 2 * GPC, 128], BF16)
    wu_ga = nc.dram_tensor("wu_ga", [NCORES, IT, 2 * GPC, 128], BF16)
    wd_pc = nc.dram_tensor("wd_pc", [INTER, OPC], BF16)
    wd_ga = nc.dram_tensor("wd_ga", [NCORES, INTER, OPC], BF16)
    x1_sp = nc.dram_tensor("x1_sp", [TC, D], BF16)
    if dbg:
        d_xf = nc.dram_tensor("d_xf", [128, DT * TC], BF16, kind="ExternalOutput")
        d_q = nc.dram_tensor("d_q", [128, DT * TC], BF16, kind="ExternalOutput")
        d_kv = nc.dram_tensor("d_kv", [128, KVT * TC], BF16, kind="ExternalOutput")
        d_kk = nc.dram_tensor("d_kk", [HD, ALLT], BF16, kind="ExternalOutput")
        d_vt = nc.dram_tensor("d_vt", [128, ALLT * HD // 128], BF16, kind="ExternalOutput")
        d_ctx = nc.dram_tensor("d_ctx", [128, DT * TC], BF16, kind="ExternalOutput")
        d_y = nc.dram_tensor("d_y", [128, DT * TC], BF16, kind="ExternalOutput")
        d_xf2 = nc.dram_tensor("d_xf2", [128, DT * TC], BF16, kind="ExternalOutput")
        d_acc = nc.dram_tensor("d_acc", [128, DT * TC], BF16, kind="ExternalOutput")
        d_wg = nc.dram_tensor("d_wg", [2 * GPC, 128], BF16, kind="ExternalOutput")

    KBy = 1024
    SM = 16512

    def at(name, shape, dtype, off):
        return nc.alloc_sbuf_tensor_at(name, shape, dtype, offset=off)

    ident = at("ident", [128, 128], BF16, SM)
    cmask = at("cmask", [128, 256], F32, SM + 256)
    bo_sb = at("bo_sb", [128, DT], F32, SM + 1280)
    kvoff = at("kvoff", [1, 4], I32, SM + 1472)
    st_s1 = at("st_s1", [128, 16], F32, SM + 1536)
    st_s2 = at("st_s2", [128, 16], F32, SM + 1600)
    st_a = at("st_a", [128, 16], F32, SM + 1664)
    st_b = at("st_b", [128, 16], F32, SM + 1728)
    st_t0 = at("st_t0", [128, 16], F32, SM + 1792)
    st_t1 = at("st_t1", [128, 16], F32, SM + 1856)
    sm_nm = at("sm_nm", [128, 1], F32, SM + 1920)
    sm_ns = at("sm_ns", [128, 1], F32, SM + 1952)
    sm_den = at("sm_den", [128, 1], F32, SM + 1984)
    sm_rd = at("sm_rd", [128, 1], F32, SM + 2016)
    eps_sb = at("eps_sb", [128, 1], F32, SM + 2048)
    hs_buf = at("hs_buf", [128, 512], F32, SM + 2080)
    p_sb = at("p_sb", [128, 2 * TT * 128], BF16, SM + 4128)
    pt_sb = at("pt_sb", [128, 2 * TT * 128], BF16, SM + 4128 + 2 * TT * 128 * 2)
    junk = at("junk", [128, D], BF16, SM + 4128 + 4 * TT * 128 * 2)
    RA = SM + 4128 + 4 * TT * 128 * 2 + 2 * D
    RA = (RA + 31) // 32 * 32

    xf = at("xf", [128, DT * TC], BF16, RA)
    _o = RA
    dq_gq = at("dq_gq", [128, 4 * max(GPC, 256)], I32, _o)
    _o += 4 * max(GPC, 256) * 4
    dq_nib = at("dq_nib", [128, 2 * max(GPC, 256)], I32, _o)
    _o += 2 * max(GPC, 256) * 4
    dq_w = at("dq_w", [128, 2 * max(GPC, 256)], BF16, _o)
    _o += 2 * max(GPC, 256) * 2
    dq_zs = at("dq_zs", [128, 64], F32, _o)
    _o += 256
    g_sc = at("g_sc", [128, IT * GGRP], F32, _o); _o += IT * GGRP * 4
    g_zr = at("g_zr", [128, IT * GGRP], F32, _o); _o += IT * GGRP * 4
    u_sc = at("u_sc", [128, IT * GGRP], F32, _o); _o += IT * GGRP * 4
    u_zr = at("u_zr", [128, IT * GGRP], F32, _o); _o += IT * GGRP * 4
    d_sc = at("d_sc", [128, OTC * DGRP], F32, _o); _o += OTC * DGRP * 4
    d_zr = at("d_zr", [128, OTC * DGRP], F32, _o); _o += OTC * DGRP * 4
    dq_dq = at("dq_dq", [128, OTC * 2 * 128], I32, _o); _o += OTC * 2 * 128 * 4
    stg_g = at("stg_g", [128, HPT * 16 * 128], BF16, _o)
    stg_d = at("stg_d", [128, 2 * 16 * OPC], BF16, _o)
    _o += max(HPT * 16 * 128 * 2, 2 * 16 * OPC * 2)
    kv_k = at("kv_k", [HD, ALLT], BF16, RA)
    kv_vw = at("kv_vw", [HD, ALLT], BF16, RA + ALLT * 2)
    kv_vt = at("kv_vt", [128, (ALLT // 128) * HD], BF16, RA + ALLT * 2)

    KVAREA = 2 * ALLT * 2
    XFSZ = DT * TC * 2
    HIT = min(DT, max(0, (XFSZ - KVAREA) // (TC * 2)))
    LOT = DT - HIT
    ctx_hi = at("ctx_hi", [128, max(HIT, 1) * TC], BF16, RA + KVAREA)
    MID = RA + max(KVAREA, XFSZ)
    x0t = at("x0t", [128, 2 * D], F32, MID)
    kvs = at("kvs", [128, KVT * TC], BF16, MID)
    ctx_lo = at("ctx_lo", [128, LOT * TC], BF16, MID)
    c_x0 = at("c_x0", [128, D], F32, MID)
    c_yt = at("c_yt", [128, D], BF16, MID + D * 4)
    c_x1 = at("c_x1", [128, D], BF16, MID + D * 4 + D * 2)
    c_ot = at("c_ot", [128, D], F32, MID)
    h_sb = at("h_sb", [128, max(ICH) * TC], BF16, MID)
    _f = MID + max(ICH) * TC * 2
    wg_b = at("wg_b", [128, D], BF16, _f); _f += D * 2
    wu_b = at("wu_b", [128, D], BF16, _f); _f += D * 2
    wd_b = at("wd_b", [128, max(ICH) * OPC], BF16, _f)
    _f += max(ICH) * OPC * 2
    MIDSZ = max(LOT * TC * 2, 2 * D * 4, KVT * TC * 2,
                D * 4 + 2 * D * 2, _f - MID)

    def ctx_at(dt, a, b, p0=0, p1=128):
        if dt < LOT:
            return ctx_lo[p0:p1, dt * TC + a:dt * TC + b]
        return ctx_hi[p0:p1, (dt - LOT) * TC + a:(dt - LOT) * TC + b]
    QB = MID + MIDSZ
    QB = (QB + 31) // 32 * 32
    qbuf = at("qbuf", [128, DT * TC], BF16, QB)
    TL = QB + DT * TC * 2
    xlt = at("xlt", [128, D], BF16, TL)
    wqb = at("wqb", [128, 2 * D], BF16, TL + D * 2)
    assert TL + D * 2 + 2 * D * 2 <= 229344, (
        f"sbuf overflow end={TL + D * 2 + 2 * D * 2}")
    assert _o <= 229344

    pbig = nc.alloc_psum_tensor("pbig", [128, 2048], F32)
    pw0 = nc.alloc_psum_tensor("pw0", [128, 512], F32)
    pw1 = nc.alloc_psum_tensor("pw1", [128, 512], F32)
    pcx = nc.alloc_psum_tensor("pcx", [128, 512], F32)
    ptp = nc.alloc_psum_tensor("ptp", [128, 1024], BF16)

    stk = contextlib.ExitStack()

    class SemC:
        def __init__(self, name):
            self.h = stk.enter_context(nc.semaphore(name))
            self.n = 0

        def inc(self, instr, k=1):
            instr.then_inc(self.h, k)
            self.n += k

        def w(self, eng):
            if self.n:
                eng.wait_ge(self.h, self.n)

    dS, dG, sM, sV, sA, sCC, sF = (SemC(x) for x in
                                   ("dS", "dG", "sM", "sV", "sA", "sCC", "sF"))
    dW = [SemC("dW0"), SemC("dW1")]

    def dbg_store(blkh, dst, src_ap):
        if not dbg:
            return

        def _d(sy):
            sA.w(sy)
            sV.w(sy)
            sM.w(sy)
            if isinstance(dst, bass.AP):
                out = dst
            else:
                out = dst[:, :] if dst.shape[0] <= 128 else dst.ap()
            dS.inc(sy.dma_start(out=out, in_=src_ap), 16)
        blkh.sync(_d)

    with nc.Block() as blk:
        def p0(sy):
            for dst, src in ((ident, ident_d), (cmask, cmask_d),
                             (bo_sb, bo_st), (kvoff, kvoffs_d)):
                dS.inc(sy.dma_start(out=dst[:, :], in_=src[:, :]), 16)
        blk.sync(p0)

        def p0v(v):
            sV.inc(v.memset(eps_sb[:, :], float(EPS)))
        blk.vector(p0v)

        def ln_finish(col):
            def _v(v):
                sA.w(v)
                invn = 1.0 / float(D)
                m = st_t0[:, col:col + 1]
                var = st_t1[:, col:col + 1]
                sV.inc(v.tensor_scalar_mul(m, st_s1[:, col:col + 1], invn))
                sV.w(v)
                sV.inc(v.tensor_scalar_mul(var, st_s2[:, col:col + 1], invn))
                sV.w(v)
                sV.inc(v.tensor_mul(st_a[:, col:col + 1], m, m))
                sV.w(v)
                sV.inc(v.tensor_sub(var, var, st_a[:, col:col + 1]))
            blk.vector(_v)

            def _s(sc):
                sV.w(sc)
                sA.inc(sc.activation(st_t1[:, col:col + 1], st_t1[:, col:col + 1],
                                     AF.Ln, bias=eps_sb[:, 0:1]))
                sA.w(sc)
                sA.inc(sc.activation(st_a[:, col:col + 1], st_t1[:, col:col + 1],
                                     AF.Exp, scale=-0.5))
            blk.scalar(_s)

            def _v2(v):
                sA.w(v)
                sV.inc(v.scalar_tensor_tensor(
                    st_b[:, col:col + 1], st_t0[:, col:col + 1], -1.0,
                    st_a[:, col:col + 1], ALU.mult, ALU.mult))
            blk.vector(_v2)

        def tr_tile_to_feat(src_t, slicer, dst_t, dst_col_of_k, wait_extra=None):
            for g8 in range(0, DT, 8):
                n8 = min(8, DT - g8)

                def _t(t, g8=g8, n8=n8):
                    sA.w(t)
                    sV.w(t)
                    if wait_extra is not None:
                        wait_extra(t)
                    for k in range(n8):
                        sM.inc(t.transpose(ptp[:, k * 128:(k + 1) * 128],
                                           slicer(g8 + k), ident[:, :]))
                blk.tensor(_t)

                def _v(v, g8=g8, n8=n8):
                    sM.w(v)
                    c0 = dst_col_of_k(g8)
                    cs = dst_col_of_k(g8 + 1) - c0
                    dst = mk(dst_t, c0, [dst_t[:, :].ap[0], [cs, n8], [1, 128]])
                    src = mk(ptp, 0, [ptp[:, :].ap[0], [128, n8], [1, 128]])
                    sV.inc(v.tensor_copy(dst, src))
                blk.vector(_v)

        for tt in range(TT):
            pb = tt % 2

            def p1_load(sy, tt=tt, pb=pb):
                sA.w(sy)
                dS.inc(sy.dma_start(out=x0t[:, pb * D:(pb + 1) * D],
                                    in_=x_own[tt * 128:(tt + 1) * 128, :]), 16)
            blk.sync(p1_load)

            def p1_stats(sc, tt=tt, pb=pb):
                dS.w(sc)
                xin = x0t[:, pb * D:(pb + 1) * D]
                sA.inc(sc.activation(junk[:, 0:D], xin, AF.Copy,
                                     accum_out=st_s1[:, tt:tt + 1]))
                sA.w(sc)
                sA.inc(sc.activation(junk[:, 0:D], xin, AF.Square,
                                     accum_out=st_s2[:, tt:tt + 1]))
            blk.scalar(p1_stats)
            ln_finish(tt)

            def p1_aff(sc, tt=tt, pb=pb):
                sV.w(sc)
                sA.inc(sc.activation(xlt[:, :], x0t[:, pb * D:(pb + 1) * D],
                                     AF.Identity, scale=st_a[:, tt:tt + 1],
                                     bias=st_b[:, tt:tt + 1]))
            blk.scalar(p1_aff)

            tr_tile_to_feat(xlt, lambda k: xlt[:, k * 128:(k + 1) * 128],
                            xf, lambda k, tt=tt: k * TC + tt * 128)

        def gemm_blk(nblk, wsrc, rhs_sb, dst_sb, bias_fn=None, rhs_at=None):
            if rhs_at is None:
                def rhs_at(dt, a, b, rhs_sb=rhs_sb):
                    return rhs_sb[:, dt * TC + a:dt * TC + b]
            for w in range(nblk):
                pb = w % 2

                def g_load(g, w=w, pb=pb):
                    sA.w(g)
                    dW[pb].inc(g.dma_start(
                        out=mk(wqb, pb * D, [wqb[:, :].ap[0], [128, DT], [1, 128]]),
                        in_=mk(wsrc, w * D * 128,
                               [[128, 128], [128 * 128, DT], [1, 128]])), 16)
                blk.gpsimd(g_load)

                for tc in range(NTC):
                    ps = pw0 if (w * NTC + tc) % 2 == 0 else pw1

                    def g_mm(t, w=w, tc=tc, ps=ps, pb=pb):
                        dW[pb].w(t)
                        sA.w(t)
                        sV.w(t)
                        ins = None
                        for dt in range(DT):
                            ins = t.matmul(
                                ps[:, 0:TCK],
                                wqb[:, pb * D + dt * 128:pb * D + (dt + 1) * 128],
                                rhs_at(dt, tc * TCK, (tc + 1) * TCK),
                                start=(dt == 0), stop=(dt == DT - 1))
                        sM.inc(ins)
                    blk.tensor(g_mm)

                    def g_ev(sc, w=w, tc=tc, ps=ps):
                        sM.w(sc)
                        fn = AF.Copy
                        kw = {}
                        if bias_fn is not None:
                            kw["bias"] = bias_fn(w)
                            fn = AF.Identity
                        sA.inc(sc.activation(
                            dst_sb[:, w * TC + tc * TCK:w * TC + (tc + 1) * TCK],
                            ps[:, 0:TCK], fn, **kw))
                    blk.scalar(g_ev)

        if dbg:
            dbg_store(blk, d_xf, xf[:, :])
        gemm_blk(DT, wq_blk, xf, qbuf)
        gemm_blk(KVT, wkv_blk, xf, kvs)
        if dbg:
            dbg_store(blk, d_q, qbuf[:, :])
            dbg_store(blk, d_kv, kvs[:, :])

        def p2_store_ag(g):
            sA.w(g)
            dG.inc(g.dma_start(
                out=mk(kv_in, 0, [[TC, 128], [128 * TC, KVT], [1, TC]]),
                in_=mk(kvs, 0, [kvs[:, :].ap[0], [TC, KVT], [1, TC]])), 16)
            dG.w(g)
            sCC.inc(g.collective_compute(
                "AllGather", ALU.bypass, replica_groups=[list(range(NCORES))],
                ins=[kv_in[:, :].opt()], outs=[kv_ga.ap().opt()]))
        blk.gpsimd(p2_store_ag)
        CC_KV = sCC.n

        def dq_gateup(qsrc, scs, zrs, piece):
            for it4 in range(0, IT, 4):
                nb = min(4, IT - it4)

                def d_load(sy, it4=it4, nb=nb):
                    sV.w(sy)
                    sM.w(sy)
                    dS.inc(sy.dma_start(
                        out=mk(dq_gq, 0, [dq_gq[:, :].ap[0], [GPC, nb], [1, GPC]]),
                        in_=mk(qsrc, it4 * 128 * GPC,
                               [[GPC, 128], [128 * GPC, nb], [1, GPC]])), 16)
                blk.sync(d_load)

                for k in range(nb):
                    it = it4 + k

                    def d_vec(v, it=it, k=k, scs=scs, zrs=zrs):
                        dS.w(v)
                        sA.w(v)
                        q = dq_gq[:, k * GPC:(k + 1) * GPC]
                        v.tensor_mul(dq_zs[:, 0:GGRP],
                                     zrs[:, it * GGRP:(it + 1) * GGRP],
                                     scs[:, it * GGRP:(it + 1) * GGRP])
                        v.tensor_scalar(dq_nib[:, 0:GPC], q, 4, 15,
                                        ALU.logical_shift_right, ALU.bitwise_and)
                        sV.inc(v.tensor_scalar(dq_nib[:, GPC:2 * GPC], q, 15, None,
                                               ALU.bitwise_and))
                        sV.w(v)
                        scb = mk(scs, it * GGRP,
                                 [scs[:, :].ap[0], [0, 2], [1, GGRP], [0, 32]])
                        zsb = mk(dq_zs, 0,
                                 [dq_zs[:, :].ap[0], [0, 2], [1, GGRP], [0, 32]])
                        nv = mk(dq_nib, 0,
                                [dq_nib[:, :].ap[0], [GPC, 2], [32, GGRP], [1, 32]])
                        wv = mk(dq_w, 0,
                                [dq_w[:, :].ap[0], [GPC, 2], [32, GGRP], [1, 32]])
                        sV.inc(v.tensor_tensor(wv, nv, scb, ALU.mult))
                        sV.w(v)
                        sV.inc(v.tensor_tensor(wv, wv, zsb, ALU.subtract))
                    blk.vector(d_vec)

                    def d_tr(t):
                        sV.w(t)
                        sA.w(t)
                        for rt in range(HPT):
                            sM.inc(t.transpose(ptp[:, rt * 128:(rt + 1) * 128],
                                               dq_w[:, rt * 128:(rt + 1) * 128],
                                               ident[:, :]))
                    blk.tensor(d_tr)

                    def d_ev(sc, it=it):
                        sM.w(sc)
                        if it % 16 == 0 and it > 0:
                            dS.w(sc)
                        kb = it % 16
                        dst = mk(stg_g, kb * 128,
                                 [stg_g[:, :].ap[0], [16 * 128, HPT], [1, 128]])
                        src = mk(ptp, 0, [ptp[:, :].ap[0], [128, HPT], [1, 128]])
                        sA.inc(sc.activation(dst, src, AF.Copy))
                    blk.scalar(d_ev)

                    if (it + 1) % 16 == 0 or it == IT - 1:
                        nkt = (it % 16) + 1
                        kb0 = it - nkt + 1

                        def d_fl(sy, nkt=nkt, kb0=kb0, piece=piece):
                            sA.w(sy)
                            for rt in range(HPT):
                                dS.inc(sy.dma_start(
                                    out=mk(piece,
                                           kb0 * (2 * GPC * 128) + rt * 128 * 128,
                                           [[128, 128], [2 * GPC * 128, nkt], [1, 128]]),
                                    in_=mk(stg_g, rt * 16 * 128,
                                           [stg_g[:, :].ap[0], [128, nkt], [1, 128]])), 16)
                        blk.sync(d_fl)

        def ag_weight(piece, gaout):
            def _r(sy):
                sy.wait_ge(dS.h, dS.n)
                sF.inc(sy.nop())
            blk.sync(_r)

            def _g(gg):
                sF.w(gg)
                sCC.inc(gg.collective_compute(
                    "AllGather", ALU.bypass, replica_groups=[list(range(NCORES))],
                    ins=[piece.ap().opt()], outs=[gaout.ap().opt()]))
            blk.gpsimd(_g)

        for engsec in (blk.sync, blk.scalar, blk.vector, blk.tensor):
            def _fence(e):
                dG.w(e)
                sA.w(e)
                sM.w(e)
                sV.w(e)
            engsec(_fence)

        def p3_scales(sy):
            for dst, src, nt, g in ((g_sc, g_sc_d, IT, GGRP), (g_zr, g_zr_d, IT, GGRP),
                                    (u_sc, u_sc_d, IT, GGRP), (u_zr, u_zr_d, IT, GGRP),
                                    (d_sc, d_sc_d, OTC, DGRP), (d_zr, d_zr_d, OTC, DGRP)):
                dS.inc(sy.dma_start(
                    out=mk(dst, 0, [dst[:, :].ap[0], [g, nt], [1, g]]),
                    in_=mk(src, 0, [[g, 128], [128 * g, nt], [1, g]])), 16)
        blk.sync(p3_scales)

        dq_gateup(gate_q, g_sc, g_zr, wg_pc)
        ag_weight(wg_pc, wg_ga)
        dq_gateup(up_q, u_sc, u_zr, wu_pc)
        ag_weight(wu_pc, wu_ga)

        for pp in range(0, PC, 2):
            npc = min(2, PC - pp)

            def dd_load(sy, pp=pp, npc=npc):
                sV.w(sy)
                sM.w(sy)
                dS.inc(sy.dma_start(
                    out=mk(dq_dq, 0,
                           [dq_dq[:, :].ap[0], [npc * 128, OTC], [1, npc * 128]]),
                    in_=mk(down_q, pp * 128,
                           [[INTER // 2, 128], [128 * (INTER // 2), OTC],
                            [1, npc * 128]])), 16)
            blk.sync(dd_load)

            for ot in range(OTC):
                def dd_vec(v, pp=pp, npc=npc, ot=ot):
                    dS.w(v)
                    sA.w(v)
                    q = dq_dq[:, ot * npc * 128:(ot + 1) * npc * 128]
                    gofs = ot * DGRP + 4 * pp
                    ng = 4 * npc
                    v.tensor_mul(dq_zs[:, 0:ng], d_zr[:, gofs:gofs + ng],
                                 d_sc[:, gofs:gofs + ng])
                    v.tensor_scalar(dq_nib[:, 0:npc * 128], q, 4, 15,
                                    ALU.logical_shift_right, ALU.bitwise_and)
                    sV.inc(v.tensor_scalar(dq_nib[:, npc * 128:2 * npc * 128], q,
                                           15, None, ALU.bitwise_and))
                    sV.w(v)
                    scb = mk(d_sc, gofs, [d_sc[:, :].ap[0], [0, 2], [1, ng], [0, 32]])
                    zsb = mk(dq_zs, 0, [dq_zs[:, :].ap[0], [0, 2], [1, ng], [0, 32]])
                    nv = mk(dq_nib, 0,
                            [dq_nib[:, :].ap[0], [npc * 128, 2], [32, ng], [1, 32]])
                    wv = mk(dq_w, 0,
                            [dq_w[:, :].ap[0], [npc * 128, 2], [32, ng], [1, 32]])
                    sV.inc(v.tensor_tensor(wv, nv, scb, ALU.mult))
                    sV.w(v)
                    sV.inc(v.tensor_tensor(wv, wv, zsb, ALU.subtract))
                blk.vector(dd_vec)

                def dd_tr(t, npc=npc):
                    sV.w(t)
                    sA.w(t)
                    for j in range(2 * npc):
                        sM.inc(t.transpose(ptp[:, j * 128:(j + 1) * 128],
                                           dq_w[:, j * 128:(j + 1) * 128],
                                           ident[:, :]))
                blk.tensor(dd_tr)

                def dd_ev(sc, pp=pp, npc=npc, ot=ot):
                    sM.w(sc)
                    if pp % 16 == 0 and pp > 0 and ot == 0:
                        dS.w(sc)
                    for j in range(npc):
                        slot = (pp + j) % 16
                        sA.inc(sc.activation(
                            stg_d[:, slot * OPC + ot * 128:slot * OPC + ot * 128 + 128],
                            ptp[:, j * 128:(j + 1) * 128], AF.Copy))
                        sA.inc(sc.activation(
                            stg_d[:, (16 + slot) * OPC + ot * 128:
                                  (16 + slot) * OPC + ot * 128 + 128],
                            ptp[:, (npc + j) * 128:(npc + j + 1) * 128], AF.Copy))
                blk.scalar(dd_ev)

            if (pp + npc) % 16 == 0 or pp + npc == PC:
                nkk = ((pp + npc - 1) % 16) + 1
                kb0 = pp + npc - nkk

                def dd_fl(sy, nkk=nkk, kb0=kb0):
                    sA.w(sy)
                    dS.inc(sy.dma_start(
                        out=mk(wd_pc, kb0 * 128 * OPC,
                               [[OPC, 128], [128 * OPC, nkk], [1, OPC]]),
                        in_=mk(stg_d, 0,
                               [stg_d[:, :].ap[0], [OPC, nkk], [1, OPC]])), 16)
                    dS.inc(sy.dma_start(
                        out=mk(wd_pc, (PC + kb0) * 128 * OPC,
                               [[OPC, 128], [128 * OPC, nkk], [1, OPC]]),
                        in_=mk(stg_d, 16 * OPC,
                               [stg_d[:, :].ap[0], [OPC, nkk], [1, OPC]])), 16)
                blk.sync(dd_fl)

        ag_weight(wd_pc, wd_ga)
        CC_ALL = sCC.n

        def attn_group(gslot):
            def kv_load(g, gslot=gslot):
                g.wait_ge(sCC.h, CC_KV)
                sM.w(g)
                dS.w(g)
                with g.register(f"kr{gslot}") as kr, g.register(f"vr{gslot}") as vr:
                    g.reg_load(kr, kvoff[0:1, 2 * gslot:2 * gslot + 1])
                    g.reg_load(vr, kvoff[0:1, 2 * gslot + 1:2 * gslot + 2])
                    kreg = g.snap(kr, min_val=0, max_val=KVD - HD)
                    vreg = g.snap(vr, min_val=0, max_val=KVD - HD)
                    for dst, reg in ((kv_k, kreg), (kv_vw, vreg)):
                        dG.inc(g.dma_start(
                            out=mk(dst, 0, [dst[:, :].ap[0], [TC, NCORES], [1, TC]]),
                            in_=kv_ga[:, bass.ds(reg, HD), :].transpose([1, 0, 2]),
                            allow_slow_non_contiguous=True), 16)
            blk.gpsimd(kv_load)

            for b8 in range(0, ALLT // 128, 8):
                n8 = min(8, ALLT // 128 - b8)

                def v_tr(t, b8=b8, n8=n8):
                    dG.w(t)
                    sV.w(t)
                    sA.w(t)
                    for k in range(n8):
                        sM.inc(t.transpose(
                            ptp[:, k * 128:k * 128 + HD],
                            kv_vw[:, (b8 + k) * 128:(b8 + k + 1) * 128],
                            ident[0:HD, 0:HD]))
                blk.tensor(v_tr)

                def v_ev(v, b8=b8, n8=n8):
                    sM.w(v)
                    dst = mk(kv_vt, b8 * HD, [kv_vt[:, :].ap[0], [HD, n8], [1, HD]])
                    src = mk(ptp, 0, [ptp[:, :].ap[0], [128, n8], [1, HD]])
                    sV.inc(v.tensor_copy(dst, src))
                blk.vector(v_ev)

            for hh in range(NH // 2):
                h = gslot * (NH // 2) + hh
                bs = hh // (NH // 8)
                hp0 = (h * HD) % 128
                hti = (h * HD) // 128
                for j in range(TT):
                    ke = j + 1

                    def s_mm(t, bs=bs, j=j, ke=ke, hp0=hp0, hti=hti):
                        sA.w(t)
                        sV.w(t)
                        dG.w(t)
                        ins = None
                        lhs = qbuf[hp0:hp0 + HD,
                                   hti * TC + j * 128:hti * TC + (j + 1) * 128]
                        for s in range(2):
                            kbase = (2 * bs + s) * TC
                            done = 0
                            while done < ke * 128:
                                cur = s * ke * 128 + done
                                n = min(512 - (cur % 512), ke * 128 - done)
                                ins = t.matmul(
                                    pbig[:, cur:cur + n],
                                    lhs, kv_k[:, kbase + done:kbase + done + n],
                                    start=True, stop=True)
                                done += n
                        sM.inc(ins)
                    blk.tensor(s_mm)

                    def s_mask(v, ke=ke):
                        sM.w(v)
                        v.tensor_add(pbig[:, (ke - 1) * 128:ke * 128],
                                     pbig[:, (ke - 1) * 128:ke * 128],
                                     cmask[:, 0:128])
                        sV.inc(v.tensor_add(pbig[:, (2 * ke - 1) * 128:2 * ke * 128],
                                            pbig[:, (2 * ke - 1) * 128:2 * ke * 128],
                                            cmask[:, 128:256]))
                        sV.w(v)
                        sV.inc(v.reduce_max(sm_nm[:, 0:1], pbig[:, 0:2 * ke * 128],
                                            axis=AX.X, negate=True))
                        sV.w(v)
                        sV.inc(v.tensor_scalar_mul(sm_ns[:, 0:1], sm_nm[:, 0:1],
                                                   float(SCL)))
                    blk.vector(s_mask)

                    def s_exp(sc, ke=ke):
                        sV.w(sc)
                        sA.inc(sc.activation(p_sb[:, 0:2 * ke * 128],
                                             pbig[:, 0:2 * ke * 128], AF.Exp,
                                             bias=sm_ns[:, 0:1], scale=float(SCL),
                                             accum_out=sm_den[:, 0:1]))
                    blk.scalar(s_exp)

                    def s_rd(sc):
                        sA.w(sc)
                        sA.inc(sc.activation(sm_rd[:, 0:1], sm_den[:, 0:1], AF.Ln))
                        sA.w(sc)
                        sA.inc(sc.activation(sm_rd[:, 0:1], sm_rd[:, 0:1], AF.Exp,
                                             scale=-1.0))
                    blk.scalar(s_rd)

                    def s_norm(v, ke=ke):
                        sA.w(v)
                        sV.inc(v.tensor_scalar_mul(p_sb[:, 0:2 * ke * 128],
                                                   p_sb[:, 0:2 * ke * 128],
                                                   sm_rd[:, 0:1]))
                    blk.vector(s_norm)

                    nk = 2 * ke
                    for q8 in range(0, nk, 8):
                        n8 = min(8, nk - q8)

                        def p_tr(t, q8=q8, n8=n8):
                            sV.w(t)
                            sA.w(t)
                            for k in range(n8):
                                sM.inc(t.transpose(
                                    ptp[:, k * 128:(k + 1) * 128],
                                    p_sb[:, (q8 + k) * 128:(q8 + k + 1) * 128],
                                    ident[:, :]))
                        blk.tensor(p_tr)

                        def p_ev(sc, q8=q8, n8=n8):
                            sM.w(sc)
                            sA.inc(sc.activation(
                                pt_sb[:, q8 * 128:(q8 + n8) * 128],
                                ptp[:, 0:n8 * 128], AF.Copy))
                        blk.scalar(p_ev)

                    def av_mm(t, bs=bs, j=j, ke=ke):
                        sA.w(t)
                        sV.w(t)
                        kti = 0
                        ins = None
                        for s in range(2):
                            tb = (2 * bs + s) * TT
                            for kt in range(ke):
                                ins = t.matmul(
                                    pcx[0:HD, 0:128],
                                    kv_vt[:, (tb + kt) * HD:(tb + kt + 1) * HD],
                                    pt_sb[:, (s * ke + kt) * 128:(s * ke + kt + 1) * 128],
                                    start=(kti == 0), stop=(kti == 2 * ke - 1))
                                kti += 1
                        sM.inc(ins)
                    blk.tensor(av_mm)

                    def av_ev(v, j=j, hp0=hp0, hti=hti):
                        sM.w(v)
                        sV.inc(v.tensor_copy(
                            ctx_at(hti, j * 128, (j + 1) * 128, hp0, hp0 + HD),
                            pcx[0:HD, 0:128]))
                    blk.vector(av_ev)

        attn_group(0)
        if dbg:
            dbg_store(blk, d_kk, kv_k[:, :])
            dbg_store(blk, d_vt, kv_vt[:, :])
        attn_group(1)
        if dbg:
            dbg_store(blk, d_ctx[:, 0:LOT * TC], ctx_lo[:, :])
            if HIT:
                dbg_store(blk, d_ctx[:, LOT * TC:DT * TC], ctx_hi[:, 0:HIT * TC])

        gemm_blk(DT, wo_blk, None, qbuf, bias_fn=lambda w: bo_sb[:, w:w + 1],
                 rhs_at=ctx_at)

        if dbg:
            dbg_store(blk, d_y, qbuf[:, :])

        def xln2_slice(k):
            r = k // HPT
            lt = k % HPT
            base = 2 * GPC * r
            assert HPT >= 2 and HPT % 2 == 0
            half = HPT // 2
            if lt < half:
                off = base + 256 * lt
            else:
                off = base + 256 * (lt - half) + 1
            return mk(xlt, off, [xlt[:, :].ap[0], [2, 128]])

        for tt in range(TT):
            def p7_tr_gate(t):
                sA.w(t)
            blk.tensor(p7_tr_gate)
            tr_tile_to_feat(qbuf, lambda k, tt=tt: qbuf[:, k * TC + tt * 128:
                                                        k * TC + (tt + 1) * 128],
                            c_yt, lambda k: k * 128)

            def p7_stats(sc, tt=tt):
                sV.w(sc)
                sA.inc(sc.activation(junk[:, 0:D], c_yt[:, :], AF.Copy,
                                     accum_out=st_s1[:, 8:9]))
                sA.w(sc)
                sA.inc(sc.activation(junk[:, 0:D], c_yt[:, :], AF.Square,
                                     accum_out=st_s2[:, 8:9]))
            blk.scalar(p7_stats)
            ln_finish(8)

            def p7_aff(sc):
                sV.w(sc)
                sA.inc(sc.activation(junk[:, 0:D], c_yt[:, :], AF.Identity,
                                     scale=st_a[:, 8:9], bias=st_b[:, 8:9]))
            blk.scalar(p7_aff)

            def p7_x0(sy, tt=tt):
                sV.w(sy)
                dS.inc(sy.dma_start(out=c_x0[:, :],
                                    in_=x_own[tt * 128:(tt + 1) * 128, :]), 16)
            blk.sync(p7_x0)

            def p7_res(v):
                sA.w(v)
                dS.w(v)
                sV.inc(v.scalar_tensor_tensor(
                    c_x1[:, :], junk[:, 0:D], 1.0, c_x0[:, :],
                    ALU.mult, ALU.add, accum_out=st_s1[:, 9:10]))
            blk.vector(p7_res)

            def p7_sp(sy, tt=tt):
                sV.w(sy)
                dS.inc(sy.dma_start(out=x1_sp[tt * 128:(tt + 1) * 128, :],
                                    in_=c_x1[:, :]), 16)
            blk.sync(p7_sp)

            def p8_stats(sc):
                sV.w(sc)
                sA.inc(sc.activation(junk[:, 0:D], c_x1[:, :], AF.Square,
                                     accum_out=st_s2[:, 9:10]))
            blk.scalar(p8_stats)
            ln_finish(9)

            def p8_aff(sc):
                sV.w(sc)
                sA.inc(sc.activation(xlt[:, :], c_x1[:, :], AF.Identity,
                                     scale=st_a[:, 9:10], bias=st_b[:, 9:10]))
            blk.scalar(p8_aff)

            tr_tile_to_feat(xlt, xln2_slice,
                            xf, lambda k, tt=tt: k * TC + tt * 128)

        if dbg:
            dbg_store(blk, d_xf2, xf[:, :])

            def _dwg(sy):
                sy.wait_ge(sCC.h, CC_ALL)
                dS.inc(sy.dma_start(out=d_wg[:, :], in_=mk(
                    wg_ga, 0 * (IT * 2 * GPC * 128) + 0 * (2 * GPC * 128),
                    [[128, 2 * GPC], [1, 128]])), 16)
            blk.sync(_dwg)

        def ffn_gate_wait(eng):
            eng.wait_ge(sCC.h, CC_ALL)

        def p9_fence(sy):
            sy.wait_ge(dS.h, dS.n)
        blk.sync(p9_fence)

        kk_glob = 0
        for ic in range(len(ICH)):
            nich = ICH[ic]
            if nich == 0:
                continue
            ic_base = kk_glob
            for kkl in range(nich):
                kg = ic_base + kkl

                def f_wload(sy, kg=kg):
                    ffn_gate_wait(sy)
                    sM.w(sy)
                    for wt, ga in ((wg_b, wg_ga), (wu_b, wu_ga)):
                        for rt in range(HPT):
                            dS.inc(sy.dma_start(
                                out=mk(wt, rt * 128,
                                       [wt[:, :].ap[0], [HPT * 128, NCORES], [1, 128]]),
                                in_=mk(ga, kg * (2 * GPC * 128) + rt * 128 * 128,
                                       [[128, 128], [IT * 2 * GPC * 128, NCORES],
                                        [1, 128]])), 16)
                blk.sync(f_wload)

                for tc in range(NTC):
                    def f_gmm(t, kg=kg, tc=tc):
                        dS.w(t)
                        sV.w(t)
                        sA.w(t)
                        insu = None
                        for dt in range(DT):
                            t.matmul(
                                pw0[:, 0:TCK],
                                wg_b[:, dt * 128:(dt + 1) * 128],
                                xf[:, dt * TC + tc * TCK:dt * TC + (tc + 1) * TCK],
                                start=(dt == 0), stop=(dt == DT - 1))
                        for dt in range(DT):
                            insu = t.matmul(
                                pw1[:, 0:TCK],
                                wu_b[:, dt * 128:(dt + 1) * 128],
                                xf[:, dt * TC + tc * TCK:dt * TC + (tc + 1) * TCK],
                                start=(dt == 0), stop=(dt == DT - 1))
                        sM.inc(insu)
                    blk.tensor(f_gmm)

                    def f_silu(sc, tc=tc):
                        sM.w(sc)
                        sA.inc(sc.activation(hs_buf[:, 0:TCK], pw0[:, 0:TCK],
                                             AF.Silu))
                    blk.scalar(f_silu)

                    def f_hmul(v, kkl=kkl, tc=tc):
                        sA.w(v)
                        sM.w(v)
                        sV.inc(v.tensor_mul(
                            h_sb[:, kkl * TC + tc * TCK:kkl * TC + (tc + 1) * TCK],
                            hs_buf[:, 0:TCK], pw1[:, 0:TCK]))
                    blk.vector(f_hmul)

            for r in range(NCORES):
                def f_dload(sy, ic=ic, r=r, ic_base=ic_base, nich=nich):
                    ffn_gate_wait(sy)
                    sM.w(sy)
                    dS.inc(sy.dma_start(
                        out=mk(wd_b, 0, [wd_b[:, :].ap[0], [OPC, nich], [1, OPC]]),
                        in_=mk(wd_ga, r * INTER * OPC + ic_base * 128 * OPC,
                               [[OPC, 128], [128 * OPC, nich], [1, OPC]])), 16)
                blk.sync(f_dload)

                for tc in range(NTC):
                    def f_dmm(t, r=r, tc=tc, nich=nich):
                        dS.w(t)
                        sV.w(t)
                        for ot in range(OTC):
                            for kkl in range(nich):
                                ins = t.matmul(
                                    pbig[:, ot * TCK:(ot + 1) * TCK],
                                    wd_b[:, kkl * OPC + ot * 128:kkl * OPC + (ot + 1) * 128],
                                    h_sb[:, kkl * TC + tc * TCK:kkl * TC + (tc + 1) * TCK],
                                    start=(kkl == 0), stop=(kkl == nich - 1))
                        sM.inc(ins)
                    blk.tensor(f_dmm)

                    def f_acc(v, ic=ic, r=r, tc=tc):
                        sM.w(v)
                        for ot in range(OTC):
                            dst = qbuf[:, (r * OTC + ot) * TC + tc * TCK:
                                       (r * OTC + ot) * TC + (tc + 1) * TCK]
                            if ic == 0:
                                ins = v.tensor_copy(dst, pbig[:, ot * TCK:(ot + 1) * TCK])
                            else:
                                ins = v.tensor_add(dst, dst,
                                                   pbig[:, ot * TCK:(ot + 1) * TCK])
                        sV.inc(ins)
                    blk.vector(f_acc)
            kk_glob += nich

        if dbg:
            dbg_store(blk, d_acc, qbuf[:, :])

        for tt in range(TT):
            tr_tile_to_feat(qbuf, lambda k, tt=tt: qbuf[:, k * TC + tt * 128:
                                                        k * TC + (tt + 1) * 128],
                            c_yt, lambda k: k * 128)

            def f_stats(sc):
                sV.w(sc)
                sA.inc(sc.activation(junk[:, 0:D], c_yt[:, :], AF.Copy,
                                     accum_out=st_s1[:, 10:11]))
                sA.w(sc)
                sA.inc(sc.activation(junk[:, 0:D], c_yt[:, :], AF.Square,
                                     accum_out=st_s2[:, 10:11]))
            blk.scalar(f_stats)
            ln_finish(10)

            def f_aff(sc):
                sV.w(sc)
                sA.inc(sc.activation(junk[:, 0:D], c_yt[:, :], AF.Identity,
                                     scale=st_a[:, 10:11], bias=st_b[:, 10:11]))
            blk.scalar(f_aff)

            def f_x1(sy, tt=tt):
                sV.w(sy)
                dS.inc(sy.dma_start(out=c_x1[:, :],
                                    in_=x1_sp[tt * 128:(tt + 1) * 128, :]), 16)
            blk.sync(f_x1)

            def f_out(v):
                sA.w(v)
                dS.w(v)
                sV.inc(v.scalar_tensor_tensor(
                    c_ot[:, :], junk[:, 0:D], 1.0, c_x1[:, :],
                    ALU.mult, ALU.add))
            blk.vector(f_out)

            def f_store(sy, tt=tt):
                sV.w(sy)
                dS.inc(sy.dma_start(out=out_d[tt * 128:(tt + 1) * 128, :],
                                    in_=c_ot[:, :]), 16)
            blk.sync(f_store)

        def fin(sy):
            dS.w(sy)
        blk.sync(fin)

    stk.close()
    return nc



def stage_inputs(cfg, inp):
    c = cfg_derive(cfg)
    D, HD, TC, TBC = c["D"], c["HD"], c["TC"], c["TBC"]
    DT, KVT, IT = c["DT"], c["KVT"], c["IT"]
    INTER, GPC, GGRP = c["INTER"], c["GPC"], c["GGRP"]
    OPC, DGRP = c["OPC"], c["DGRP"]

    x = np.asarray(inp["L0_attn_x"])
    wq = np.asarray(inp["L0_attn_Wq"])
    wkv = np.asarray(inp["L0_attn_Wkv"])
    wo = np.asarray(inp["L0_attn_Wo"])
    bo = np.asarray(inp["L0_attn_bo"])
    gq = np.asarray(inp["L0_ffn_gate_Wq"])
    uq = np.asarray(inp["L0_ffn_up_Wq"])
    dq = np.asarray(inp["L0_ffn_down_Wq"])
    gsc = np.asarray(inp["L0_ffn_gate_scale"])
    gzr = np.asarray(inp["L0_ffn_gate_zero"])
    usc = np.asarray(inp["L0_ffn_up_scale"])
    uzr = np.asarray(inp["L0_ffn_up_zero"])
    dsc = np.asarray(inp["L0_ffn_down_scale"])
    dzr = np.asarray(inp["L0_ffn_down_zero"])

    CC = np.ascontiguousarray
    wq_blk = CC(wq.reshape(D, DT, 128).transpose(1, 0, 2))
    wkv_blk = CC(wkv.reshape(D, KVT, 128).transpose(1, 0, 2))
    wo_blk = CC(wo.reshape(D, DT, 128).transpose(1, 0, 2))
    bo_st = CC(bo.reshape(DT, 128).T)
    ident = np.eye(128, dtype=ml_dtypes.bfloat16)
    perm_i = np.concatenate([np.arange(0, INTER, 2), np.arange(1, INTER, 2)])
    gq_p = gq[perm_i]
    uq_p = uq[perm_i]
    gsc_p = gsc[perm_i]
    gzr_p = gzr[perm_i]
    usc_p = usc[perm_i]
    uzr_p = uzr[perm_i]

    tri = np.triu(np.full((128, 128), NEG, np.float32), 1)
    full = np.full((128, 128), NEG, np.float32)
    zero = np.zeros((128, 128), np.float32)

    in_maps = []
    for cid in range(NCORES):
        bp, par = cid // 2, cid % 2
        xo = CC(x[bp].reshape(2 * TBC, 128, D)[par::2].reshape(TC, D))
        cm = np.concatenate([tri, full] if par == 0 else [zero, tri],
                            axis=1).astype(np.float32)
        kvo = (np.array([0, HD, 2 * HD, 3 * HD], np.int32)
               + np.int32(4 * HD * bp)).reshape(1, 4)
        ngr = GGRP
        m = dict(
            x_own=xo,
            wq_blk=wq_blk, wkv_blk=wkv_blk, wo_blk=wo_blk, bo_st=bo_st,
            ident_d=ident, cmask_d=CC(cm), kvoffs_d=kvo,
            gate_q=CC(gq_p[:, GPC * cid:GPC * (cid + 1)]),
            up_q=CC(uq_p[:, GPC * cid:GPC * (cid + 1)]),
            down_q=CC(dq[OPC * cid:OPC * (cid + 1), :]),
            g_sc_d=CC(gsc_p[:, ngr * cid:ngr * (cid + 1)]),
            g_zr_d=CC(gzr_p[:, ngr * cid:ngr * (cid + 1)]),
            u_sc_d=CC(usc_p[:, ngr * cid:ngr * (cid + 1)]),
            u_zr_d=CC(uzr_p[:, ngr * cid:ngr * (cid + 1)]),
            d_sc_d=CC(dsc[OPC * cid:OPC * (cid + 1), :]),
            d_zr_d=CC(dzr[OPC * cid:OPC * (cid + 1), :]),
        )
        in_maps.append(m)
    return in_maps


def assemble_out(cfg, shards):
    c = cfg_derive(cfg)
    D, TC, TBC = c["D"], c["TC"], c["TBC"]
    out = np.empty((NCORES // 2, 2 * TC, D), np.float32)
    for cid in range(NCORES):
        bp, par = cid // 2, cid % 2
        blkv = out[bp].reshape(2 * TBC, 128, D)
        blkv[par::2] = shards[cid].reshape(TBC, 128, D)
    return out


def kernel(**inputs):
    cfg = REAL
    in_maps = stage_inputs(cfg, inputs)
    nc = build_nc(cfg)
    res = run_bass_kernel_spmd(nc, in_maps, core_ids=list(range(NCORES)))
    shards = [np.asarray(res.results[i]["out"]) for i in range(NCORES)]
    return assemble_out(cfg, shards)


# version 07
# speedup vs baseline: 1.1558x; 1.1558x over previous
import numpy as np
import ml_dtypes
import concourse.bass as bass
import concourse.mybir as mybir
from concourse.bass_utils import run_bass_kernel_spmd

F32 = mybir.dt.float32
BF16 = mybir.dt.bfloat16
I32 = mybir.dt.int32
AF = mybir.ActivationFunctionType
ALU = mybir.AluOpType
AX = mybir.AxisListType

NCORES = 8
NEG = -1e9
EPS = 1e-5

REAL = dict(DT=32, HD=128, TBC=8, IT=86, NH=32)
TINY = dict(DT=16, HD=128, TBC=8, IT=40, NH=16)


def cfg_derive(cfg):
    c = dict(cfg)
    c["D"] = c["DT"] * 128
    c["NH"] = c.get("NH", 32)
    c["NG"] = 8
    assert c["NH"] % 8 == 0
    assert c["NH"] * c["HD"] == c["D"]
    c["KVD"] = 2 * c["HD"] * c["NG"]
    c["KVT"] = c["KVD"] // 128
    assert c["KVD"] % 128 == 0
    c["TC"] = c["TBC"] * 128
    c["SEQ"] = 2 * c["TC"]
    c["INTER"] = c["IT"] * 128
    assert (c["D"] // 2) % (NCORES * 32) == 0
    c["GPC"] = c["D"] // 2 // NCORES
    c["GGRP"] = c["GPC"] // 32
    c["HPT"] = 2 * c["GPC"] // 128
    assert (2 * c["GPC"]) % 128 == 0 and c["HPT"] >= 1
    c["OPC"] = c["D"] // NCORES
    c["OTC"] = c["OPC"] // 128
    assert c["OPC"] % 128 == 0
    c["PC"] = c["INTER"] // 2 // 128
    assert (c["INTER"] // 2) % 128 == 0
    c["DGRP"] = c["INTER"] // 64
    it = c["IT"]
    NCH = 16
    base, rem = it // NCH, it % NCH
    c["ICH"] = [base + (1 if i < rem else 0) for i in range(NCH)]
    return c


def mk(t, off, dims):
    return bass.AP(t, off, dims)


def build_nc(cfg, dbg=False):
    import contextlib

    c = cfg_derive(cfg)
    DT, HD, TBC, IT = c["DT"], c["HD"], c["TBC"], c["IT"]
    D, NH, NG, KVT, TC = c["D"], c["NH"], c["NG"], c["KVT"], c["TC"]
    KVD = c["KVD"]
    INTER, GPC, GGRP, HPT = c["INTER"], c["GPC"], c["GGRP"], c["HPT"]
    OPC, OTC, PC, ICH, DGRP = c["OPC"], c["OTC"], c["PC"], c["ICH"], c["DGRP"]
    TT = TBC
    NTC = max(TC // 512, 1)
    TCK = TC // NTC
    SCL = 1.0 / float(np.sqrt(HD))
    ALLT = NCORES * TC

    nc = bass.Bass("TRN2", target_bir_lowering=False, debug=False,
                   detect_race_conditions=True)

    x_own = nc.dram_tensor("x_own", [TC, D], F32, kind="ExternalInput")
    wq_blk = nc.dram_tensor("wq_blk", [DT, D, 128], F32, kind="ExternalInput")
    wkv_blk = nc.dram_tensor("wkv_blk", [KVT, D, 128], F32, kind="ExternalInput")
    wo_blk = nc.dram_tensor("wo_blk", [DT, D, 128], F32, kind="ExternalInput")
    bo_st = nc.dram_tensor("bo_st", [128, DT], F32, kind="ExternalInput")
    ident_d = nc.dram_tensor("ident_d", [128, 128], BF16, kind="ExternalInput")
    cmask_d = nc.dram_tensor("cmask_d", [128, 256], F32, kind="ExternalInput")
    kvoffs_d = nc.dram_tensor("kvoffs_d", [1, 4], I32, kind="ExternalInput")
    gate_q = nc.dram_tensor("gate_q", [INTER, GPC], I32, kind="ExternalInput")
    up_q = nc.dram_tensor("up_q", [INTER, GPC], I32, kind="ExternalInput")
    down_q = nc.dram_tensor("down_q", [OPC, INTER // 2], I32, kind="ExternalInput")
    g_sc_d = nc.dram_tensor("g_sc_d", [INTER, GGRP], F32, kind="ExternalInput")
    g_zr_d = nc.dram_tensor("g_zr_d", [INTER, GGRP], F32, kind="ExternalInput")
    u_sc_d = nc.dram_tensor("u_sc_d", [INTER, GGRP], F32, kind="ExternalInput")
    u_zr_d = nc.dram_tensor("u_zr_d", [INTER, GGRP], F32, kind="ExternalInput")
    d_sc_d = nc.dram_tensor("d_sc_d", [OPC, DGRP], F32, kind="ExternalInput")
    d_zr_d = nc.dram_tensor("d_zr_d", [OPC, DGRP], F32, kind="ExternalInput")
    out_d = nc.dram_tensor("out", [TC, D], F32, kind="ExternalOutput")

    kv_in = nc.dram_tensor("kv_in", [KVD, TC], BF16)
    kv_ga = nc.dram_tensor("kv_ga", [NCORES, KVD, TC], BF16)
    wg_pc = nc.dram_tensor("wg_pc", [IT, 2 * GPC, 128], BF16)
    wu_pc = nc.dram_tensor("wu_pc", [IT, 2 * GPC, 128], BF16)
    wg_ga = nc.dram_tensor("wg_ga", [NCORES, IT, 2 * GPC, 128], BF16)
    wu_ga = nc.dram_tensor("wu_ga", [NCORES, IT, 2 * GPC, 128], BF16)
    wd_pc = nc.dram_tensor("wd_pc", [INTER, OPC], BF16)
    wd_ga = nc.dram_tensor("wd_ga", [NCORES, INTER, OPC], BF16)
    x1_sp = nc.dram_tensor("x1_sp", [TC, D], BF16)
    if dbg:
        d_xf = nc.dram_tensor("d_xf", [128, DT * TC], BF16, kind="ExternalOutput")
        d_q = nc.dram_tensor("d_q", [128, DT * TC], BF16, kind="ExternalOutput")
        d_kv = nc.dram_tensor("d_kv", [128, KVT * TC], BF16, kind="ExternalOutput")
        d_kk = nc.dram_tensor("d_kk", [HD, ALLT], BF16, kind="ExternalOutput")
        d_vt = nc.dram_tensor("d_vt", [128, ALLT * HD // 128], BF16, kind="ExternalOutput")
        d_ctx = nc.dram_tensor("d_ctx", [128, DT * TC], BF16, kind="ExternalOutput")
        d_y = nc.dram_tensor("d_y", [128, DT * TC], BF16, kind="ExternalOutput")
        d_xf2 = nc.dram_tensor("d_xf2", [128, DT * TC], BF16, kind="ExternalOutput")
        d_acc = nc.dram_tensor("d_acc", [128, DT * TC], BF16, kind="ExternalOutput")
        d_wg = nc.dram_tensor("d_wg", [2 * GPC, 128], BF16, kind="ExternalOutput")

    KBy = 1024
    SM = 16512

    def at(name, shape, dtype, off):
        return nc.alloc_sbuf_tensor_at(name, shape, dtype, offset=off)

    ident = at("ident", [128, 128], BF16, SM)
    cmask = at("cmask", [128, 256], F32, SM + 256)
    bo_sb = at("bo_sb", [128, DT], F32, SM + 1280)
    kvoff = at("kvoff", [1, 4], I32, SM + 1472)
    st_s1 = at("st_s1", [128, 16], F32, SM + 1536)
    st_s2 = at("st_s2", [128, 16], F32, SM + 1600)
    st_a = at("st_a", [128, 16], F32, SM + 1664)
    st_b = at("st_b", [128, 16], F32, SM + 1728)
    st_t0 = at("st_t0", [128, 16], F32, SM + 1792)
    st_t1 = at("st_t1", [128, 16], F32, SM + 1856)
    sm_nm = at("sm_nm", [128, 1], F32, SM + 1920)
    sm_ns = at("sm_ns", [128, 1], F32, SM + 1952)
    sm_den = at("sm_den", [128, 1], F32, SM + 1984)
    sm_rd = at("sm_rd", [128, 1], F32, SM + 2016)
    eps_sb = at("eps_sb", [128, 1], F32, SM + 2048)
    hs_buf = at("hs_buf", [128, 512], F32, SM + 2080)
    p_sb = at("p_sb", [128, 2 * TT * 128], BF16, SM + 4128)
    pt_sb = at("pt_sb", [128, 2 * TT * 128], BF16, SM + 4128 + 2 * TT * 128 * 2)
    junk = at("junk", [128, D], BF16, SM + 4128 + 4 * TT * 128 * 2)
    RA = SM + 4128 + 4 * TT * 128 * 2 + 2 * D
    RA = (RA + 31) // 32 * 32

    xf = at("xf", [128, DT * TC], BF16, RA)
    _o = RA
    dq_gq = at("dq_gq", [128, 4 * max(GPC, 256)], I32, _o)
    _o += 4 * max(GPC, 256) * 4
    dq_nib = at("dq_nib", [128, 2 * max(GPC, 256)], I32, _o)
    _o += 2 * max(GPC, 256) * 4
    dq_w = at("dq_w", [128, 2 * max(GPC, 256)], BF16, _o)
    _o += 2 * max(GPC, 256) * 2
    dq_zs = at("dq_zs", [128, 64], F32, _o)
    _o += 256
    g_sc = at("g_sc", [128, IT * GGRP], F32, _o); _o += IT * GGRP * 4
    g_zr = at("g_zr", [128, IT * GGRP], F32, _o); _o += IT * GGRP * 4
    u_sc = at("u_sc", [128, IT * GGRP], F32, _o); _o += IT * GGRP * 4
    u_zr = at("u_zr", [128, IT * GGRP], F32, _o); _o += IT * GGRP * 4
    d_sc = at("d_sc", [128, OTC * DGRP], F32, _o); _o += OTC * DGRP * 4
    d_zr = at("d_zr", [128, OTC * DGRP], F32, _o); _o += OTC * DGRP * 4
    dq_dq = at("dq_dq", [128, OTC * 2 * 128], I32, _o); _o += OTC * 2 * 128 * 4
    stg_g = at("stg_g", [128, HPT * 16 * 128], BF16, _o)
    stg_d = at("stg_d", [128, 2 * 16 * OPC], BF16, _o)
    _o += max(HPT * 16 * 128 * 2, 2 * 16 * OPC * 2)
    kv_k = at("kv_k", [HD, ALLT], BF16, RA)
    kv_vw = at("kv_vw", [HD, ALLT], BF16, RA + ALLT * 2)
    kv_vt = at("kv_vt", [128, (ALLT // 128) * HD], BF16, RA + ALLT * 2)

    KVAREA = 2 * ALLT * 2
    XFSZ = DT * TC * 2
    HIT = min(DT, max(0, (XFSZ - KVAREA) // (TC * 2)))
    LOT = DT - HIT
    ctx_hi = at("ctx_hi", [128, max(HIT, 1) * TC], BF16, RA + KVAREA)
    MID = RA + max(KVAREA, XFSZ)
    x0t = at("x0t", [128, 2 * D], F32, MID)
    kvs = at("kvs", [128, KVT * TC], BF16, MID)
    ctx_lo = at("ctx_lo", [128, LOT * TC], BF16, MID)
    c_x0 = at("c_x0", [128, D], F32, MID)
    c_yt = at("c_yt", [128, D], BF16, MID + D * 4)
    c_x1 = at("c_x1", [128, D], BF16, MID + D * 4 + D * 2)
    c_ot = at("c_ot", [128, D], F32, MID)
    h_sb = at("h_sb", [128, max(ICH) * TC], BF16, MID)
    _f = MID + max(ICH) * TC * 2
    wg_b = at("wg_b", [128, D], BF16, _f); _f += D * 2
    wu_b = at("wu_b", [128, D], BF16, _f); _f += D * 2
    wd_b = at("wd_b", [128, max(ICH) * OPC], BF16, _f)
    _f += max(ICH) * OPC * 2
    MIDSZ = max(LOT * TC * 2, 2 * D * 4, KVT * TC * 2,
                D * 4 + 2 * D * 2, _f - MID)

    def ctx_at(dt, a, b, p0=0, p1=128):
        if dt < LOT:
            return ctx_lo[p0:p1, dt * TC + a:dt * TC + b]
        return ctx_hi[p0:p1, (dt - LOT) * TC + a:(dt - LOT) * TC + b]
    QB = MID + MIDSZ
    QB = (QB + 31) // 32 * 32
    qbuf = at("qbuf", [128, DT * TC], BF16, QB)
    TL = QB + DT * TC * 2
    xlt = at("xlt", [128, D], BF16, TL)
    wqb = at("wqb", [128, 2 * D], BF16, TL + D * 2)
    hs_odd = at("hs_odd", [128, 512], F32, SM + 4128)
    wd_b2 = at("wd_b2", [128, max(ICH) * OPC], BF16, TL)
    wg_b2 = at("wg_b2", [128, D], BF16, TL + D * 2)
    wu_b2 = at("wu_b2", [128, D], BF16, TL + D * 2 + D * 2)
    assert max(ICH) * OPC * 2 <= D * 2
    assert TL + D * 2 + 2 * D * 2 <= 229344, (
        f"sbuf overflow end={TL + D * 2 + 2 * D * 2}")
    assert _o <= 229344

    pbig = nc.alloc_psum_tensor("pbig", [128, 2048], F32)
    pw0 = nc.alloc_psum_tensor("pw0", [128, 512], F32)
    pw1 = nc.alloc_psum_tensor("pw1", [128, 512], F32)
    pcx = nc.alloc_psum_tensor("pcx", [128, 512], F32)
    ptp = nc.alloc_psum_tensor("ptp", [128, 1024], BF16)

    stk = contextlib.ExitStack()

    class SemC:
        def __init__(self, name):
            self.h = stk.enter_context(nc.semaphore(name))
            self.n = 0

        def inc(self, instr, k=1):
            instr.then_inc(self.h, k)
            self.n += k

        def w(self, eng):
            if self.n:
                eng.wait_ge(self.h, self.n)

    dS, dG, sM, sV, sA, sCC, sF = (SemC(x) for x in
                                   ("dS", "dG", "sM", "sV", "sA", "sCC", "sF"))
    dW = [SemC("dW0"), SemC("dW1")]

    def dbg_store(blkh, dst, src_ap):
        if not dbg:
            return

        def _d(sy):
            sA.w(sy)
            sV.w(sy)
            sM.w(sy)
            if isinstance(dst, bass.AP):
                out = dst
            else:
                out = dst[:, :] if dst.shape[0] <= 128 else dst.ap()
            dS.inc(sy.dma_start(out=out, in_=src_ap), 16)
        blkh.sync(_d)

    with nc.Block() as blk:
        def p0(sy):
            for dst, src in ((ident, ident_d), (cmask, cmask_d),
                             (bo_sb, bo_st), (kvoff, kvoffs_d)):
                dS.inc(sy.dma_start(out=dst[:, :], in_=src[:, :]), 16)
        blk.sync(p0)

        def p0v(v):
            sV.inc(v.memset(eps_sb[:, :], float(EPS)))
        blk.vector(p0v)

        def ln_finish(col):
            def _v(v):
                sA.w(v)
                invn = 1.0 / float(D)
                m = st_t0[:, col:col + 1]
                var = st_t1[:, col:col + 1]
                sV.inc(v.tensor_scalar_mul(m, st_s1[:, col:col + 1], invn))
                sV.w(v)
                sV.inc(v.tensor_scalar_mul(var, st_s2[:, col:col + 1], invn))
                sV.w(v)
                sV.inc(v.tensor_mul(st_a[:, col:col + 1], m, m))
                sV.w(v)
                sV.inc(v.tensor_sub(var, var, st_a[:, col:col + 1]))
            blk.vector(_v)

            def _s(sc):
                sV.w(sc)
                sA.inc(sc.activation(st_t1[:, col:col + 1], st_t1[:, col:col + 1],
                                     AF.Ln, bias=eps_sb[:, 0:1]))
                sA.w(sc)
                sA.inc(sc.activation(st_a[:, col:col + 1], st_t1[:, col:col + 1],
                                     AF.Exp, scale=-0.5))
            blk.scalar(_s)

            def _v2(v):
                sA.w(v)
                sV.inc(v.scalar_tensor_tensor(
                    st_b[:, col:col + 1], st_t0[:, col:col + 1], -1.0,
                    st_a[:, col:col + 1], ALU.mult, ALU.mult))
            blk.vector(_v2)

        def tr_tile_to_feat(src_t, slicer, dst_t, dst_col_of_k, wait_extra=None):
            for g8 in range(0, DT, 8):
                n8 = min(8, DT - g8)

                def _t(t, g8=g8, n8=n8):
                    sA.w(t)
                    sV.w(t)
                    if wait_extra is not None:
                        wait_extra(t)
                    for k in range(n8):
                        sM.inc(t.transpose(ptp[:, k * 128:(k + 1) * 128],
                                           slicer(g8 + k), ident[:, :]))
                blk.tensor(_t)

                def _v(v, g8=g8, n8=n8):
                    sM.w(v)
                    c0 = dst_col_of_k(g8)
                    cs = dst_col_of_k(g8 + 1) - c0
                    dst = mk(dst_t, c0, [dst_t[:, :].ap[0], [cs, n8], [1, 128]])
                    src = mk(ptp, 0, [ptp[:, :].ap[0], [128, n8], [1, 128]])
                    sV.inc(v.tensor_copy(dst, src))
                blk.vector(_v)

        for tt in range(TT):
            pb = tt % 2

            def p1_load(sy, tt=tt, pb=pb):
                sA.w(sy)
                dS.inc(sy.dma_start(out=x0t[:, pb * D:(pb + 1) * D],
                                    in_=x_own[tt * 128:(tt + 1) * 128, :]), 16)
            blk.sync(p1_load)

            def p1_stats(sc, tt=tt, pb=pb):
                dS.w(sc)
                xin = x0t[:, pb * D:(pb + 1) * D]
                sA.inc(sc.activation(junk[:, 0:D], xin, AF.Copy,
                                     accum_out=st_s1[:, tt:tt + 1]))
                sA.w(sc)
                sA.inc(sc.activation(junk[:, 0:D], xin, AF.Square,
                                     accum_out=st_s2[:, tt:tt + 1]))
            blk.scalar(p1_stats)
            ln_finish(tt)

            def p1_aff(sc, tt=tt, pb=pb):
                sV.w(sc)
                sA.inc(sc.activation(xlt[:, :], x0t[:, pb * D:(pb + 1) * D],
                                     AF.Identity, scale=st_a[:, tt:tt + 1],
                                     bias=st_b[:, tt:tt + 1]))
            blk.scalar(p1_aff)

            tr_tile_to_feat(xlt, lambda k: xlt[:, k * 128:(k + 1) * 128],
                            xf, lambda k, tt=tt: k * TC + tt * 128)

        def gemm_blk(nblk, wsrc, rhs_sb, dst_sb, bias_fn=None, rhs_at=None):
            if rhs_at is None:
                def rhs_at(dt, a, b, rhs_sb=rhs_sb):
                    return rhs_sb[:, dt * TC + a:dt * TC + b]
            for w in range(nblk):
                pb = w % 2

                def g_load(g, w=w, pb=pb):
                    sA.w(g)
                    dW[pb].inc(g.dma_start(
                        out=mk(wqb, pb * D, [wqb[:, :].ap[0], [128, DT], [1, 128]]),
                        in_=mk(wsrc, w * D * 128,
                               [[128, 128], [128 * 128, DT], [1, 128]])), 16)
                blk.gpsimd(g_load)

                for tc in range(NTC):
                    ps = pw0 if (w * NTC + tc) % 2 == 0 else pw1

                    def g_mm(t, w=w, tc=tc, ps=ps, pb=pb):
                        dW[pb].w(t)
                        sA.w(t)
                        sV.w(t)
                        ins = None
                        for dt in range(DT):
                            ins = t.matmul(
                                ps[:, 0:TCK],
                                wqb[:, pb * D + dt * 128:pb * D + (dt + 1) * 128],
                                rhs_at(dt, tc * TCK, (tc + 1) * TCK),
                                start=(dt == 0), stop=(dt == DT - 1))
                        sM.inc(ins)
                    blk.tensor(g_mm)

                    def g_ev(sc, w=w, tc=tc, ps=ps):
                        sM.w(sc)
                        fn = AF.Copy
                        kw = {}
                        if bias_fn is not None:
                            kw["bias"] = bias_fn(w)
                            fn = AF.Identity
                        sA.inc(sc.activation(
                            dst_sb[:, w * TC + tc * TCK:w * TC + (tc + 1) * TCK],
                            ps[:, 0:TCK], fn, **kw))
                    blk.scalar(g_ev)

        if dbg:
            dbg_store(blk, d_xf, xf[:, :])
        gemm_blk(DT, wq_blk, xf, qbuf)
        gemm_blk(KVT, wkv_blk, xf, kvs)
        if dbg:
            dbg_store(blk, d_q, qbuf[:, :])
            dbg_store(blk, d_kv, kvs[:, :])

        def p2_store_ag(g):
            sA.w(g)
            dG.inc(g.dma_start(
                out=mk(kv_in, 0, [[TC, 128], [128 * TC, KVT], [1, TC]]),
                in_=mk(kvs, 0, [kvs[:, :].ap[0], [TC, KVT], [1, TC]])), 16)
            dG.w(g)
            sCC.inc(g.collective_compute(
                "AllGather", ALU.bypass, replica_groups=[list(range(NCORES))],
                ins=[kv_in[:, :].opt()], outs=[kv_ga.ap().opt()]))
        blk.gpsimd(p2_store_ag)
        CC_KV = sCC.n

        def dq_gateup(qsrc, scs, zrs, piece):
            for it4 in range(0, IT, 4):
                nb = min(4, IT - it4)

                def d_load(sy, it4=it4, nb=nb):
                    sV.w(sy)
                    sM.w(sy)
                    dS.inc(sy.dma_start(
                        out=mk(dq_gq, 0, [dq_gq[:, :].ap[0], [GPC, nb], [1, GPC]]),
                        in_=mk(qsrc, it4 * 128 * GPC,
                               [[GPC, 128], [128 * GPC, nb], [1, GPC]])), 16)
                blk.sync(d_load)

                for k in range(nb):
                    it = it4 + k

                    def d_vec(v, it=it, k=k, scs=scs, zrs=zrs):
                        dS.w(v)
                        sA.w(v)
                        q = dq_gq[:, k * GPC:(k + 1) * GPC]
                        v.tensor_mul(dq_zs[:, 0:GGRP],
                                     zrs[:, it * GGRP:(it + 1) * GGRP],
                                     scs[:, it * GGRP:(it + 1) * GGRP])
                        v.tensor_scalar(dq_nib[:, 0:GPC], q, 4, 15,
                                        ALU.logical_shift_right, ALU.bitwise_and)
                        sV.inc(v.tensor_scalar(dq_nib[:, GPC:2 * GPC], q, 15, None,
                                               ALU.bitwise_and))
                        sV.w(v)
                        scb = mk(scs, it * GGRP,
                                 [scs[:, :].ap[0], [0, 2], [1, GGRP], [0, 32]])
                        zsb = mk(dq_zs, 0,
                                 [dq_zs[:, :].ap[0], [0, 2], [1, GGRP], [0, 32]])
                        nv = mk(dq_nib, 0,
                                [dq_nib[:, :].ap[0], [GPC, 2], [32, GGRP], [1, 32]])
                        wv = mk(dq_w, 0,
                                [dq_w[:, :].ap[0], [GPC, 2], [32, GGRP], [1, 32]])
                        sV.inc(v.tensor_tensor(wv, nv, scb, ALU.mult))
                        sV.w(v)
                        sV.inc(v.tensor_tensor(wv, wv, zsb, ALU.subtract))
                    blk.vector(d_vec)

                    def d_tr(t):
                        sV.w(t)
                        sA.w(t)
                        for rt in range(HPT):
                            sM.inc(t.transpose(ptp[:, rt * 128:(rt + 1) * 128],
                                               dq_w[:, rt * 128:(rt + 1) * 128],
                                               ident[:, :]))
                    blk.tensor(d_tr)

                    def d_ev(sc, it=it):
                        sM.w(sc)
                        if it % 16 == 0 and it > 0:
                            dS.w(sc)
                        kb = it % 16
                        dst = mk(stg_g, kb * 128,
                                 [stg_g[:, :].ap[0], [16 * 128, HPT], [1, 128]])
                        src = mk(ptp, 0, [ptp[:, :].ap[0], [128, HPT], [1, 128]])
                        sA.inc(sc.activation(dst, src, AF.Copy))
                    blk.scalar(d_ev)

                    if (it + 1) % 16 == 0 or it == IT - 1:
                        nkt = (it % 16) + 1
                        kb0 = it - nkt + 1

                        def d_fl(sy, nkt=nkt, kb0=kb0, piece=piece):
                            sA.w(sy)
                            for rt in range(HPT):
                                dS.inc(sy.dma_start(
                                    out=mk(piece,
                                           kb0 * (2 * GPC * 128) + rt * 128 * 128,
                                           [[128, 128], [2 * GPC * 128, nkt], [1, 128]]),
                                    in_=mk(stg_g, rt * 16 * 128,
                                           [stg_g[:, :].ap[0], [128, nkt], [1, 128]])), 16)
                        blk.sync(d_fl)

        def ag_weight(piece, gaout):
            def _r(sy):
                sy.wait_ge(dS.h, dS.n)
                sF.inc(sy.nop())
            blk.sync(_r)

            def _g(gg):
                sF.w(gg)
                sCC.inc(gg.collective_compute(
                    "AllGather", ALU.bypass, replica_groups=[list(range(NCORES))],
                    ins=[piece.ap().opt()], outs=[gaout.ap().opt()]))
            blk.gpsimd(_g)

        for engsec in (blk.sync, blk.scalar, blk.vector, blk.tensor):
            def _fence(e):
                dG.w(e)
                sA.w(e)
                sM.w(e)
                sV.w(e)
            engsec(_fence)

        def p3_scales(sy):
            for dst, src, nt, g in ((g_sc, g_sc_d, IT, GGRP), (g_zr, g_zr_d, IT, GGRP),
                                    (u_sc, u_sc_d, IT, GGRP), (u_zr, u_zr_d, IT, GGRP),
                                    (d_sc, d_sc_d, OTC, DGRP), (d_zr, d_zr_d, OTC, DGRP)):
                dS.inc(sy.dma_start(
                    out=mk(dst, 0, [dst[:, :].ap[0], [g, nt], [1, g]]),
                    in_=mk(src, 0, [[g, 128], [128 * g, nt], [1, g]])), 16)
        blk.sync(p3_scales)

        dq_gateup(gate_q, g_sc, g_zr, wg_pc)
        ag_weight(wg_pc, wg_ga)
        dq_gateup(up_q, u_sc, u_zr, wu_pc)
        ag_weight(wu_pc, wu_ga)

        for pp in range(0, PC, 2):
            npc = min(2, PC - pp)

            def dd_load(sy, pp=pp, npc=npc):
                sV.w(sy)
                sM.w(sy)
                dS.inc(sy.dma_start(
                    out=mk(dq_dq, 0,
                           [dq_dq[:, :].ap[0], [npc * 128, OTC], [1, npc * 128]]),
                    in_=mk(down_q, pp * 128,
                           [[INTER // 2, 128], [128 * (INTER // 2), OTC],
                            [1, npc * 128]])), 16)
            blk.sync(dd_load)

            for ot in range(OTC):
                def dd_vec(v, pp=pp, npc=npc, ot=ot):
                    dS.w(v)
                    sA.w(v)
                    q = dq_dq[:, ot * npc * 128:(ot + 1) * npc * 128]
                    gofs = ot * DGRP + 4 * pp
                    ng = 4 * npc
                    v.tensor_mul(dq_zs[:, 0:ng], d_zr[:, gofs:gofs + ng],
                                 d_sc[:, gofs:gofs + ng])
                    v.tensor_scalar(dq_nib[:, 0:npc * 128], q, 4, 15,
                                    ALU.logical_shift_right, ALU.bitwise_and)
                    sV.inc(v.tensor_scalar(dq_nib[:, npc * 128:2 * npc * 128], q,
                                           15, None, ALU.bitwise_and))
                    sV.w(v)
                    scb = mk(d_sc, gofs, [d_sc[:, :].ap[0], [0, 2], [1, ng], [0, 32]])
                    zsb = mk(dq_zs, 0, [dq_zs[:, :].ap[0], [0, 2], [1, ng], [0, 32]])
                    nv = mk(dq_nib, 0,
                            [dq_nib[:, :].ap[0], [npc * 128, 2], [32, ng], [1, 32]])
                    wv = mk(dq_w, 0,
                            [dq_w[:, :].ap[0], [npc * 128, 2], [32, ng], [1, 32]])
                    sV.inc(v.tensor_tensor(wv, nv, scb, ALU.mult))
                    sV.w(v)
                    sV.inc(v.tensor_tensor(wv, wv, zsb, ALU.subtract))
                blk.vector(dd_vec)

                def dd_tr(t, npc=npc):
                    sV.w(t)
                    sA.w(t)
                    for j in range(2 * npc):
                        sM.inc(t.transpose(ptp[:, j * 128:(j + 1) * 128],
                                           dq_w[:, j * 128:(j + 1) * 128],
                                           ident[:, :]))
                blk.tensor(dd_tr)

                def dd_ev(sc, pp=pp, npc=npc, ot=ot):
                    sM.w(sc)
                    if pp % 16 == 0 and pp > 0 and ot == 0:
                        dS.w(sc)
                    for j in range(npc):
                        slot = (pp + j) % 16
                        sA.inc(sc.activation(
                            stg_d[:, slot * OPC + ot * 128:slot * OPC + ot * 128 + 128],
                            ptp[:, j * 128:(j + 1) * 128], AF.Copy))
                        sA.inc(sc.activation(
                            stg_d[:, (16 + slot) * OPC + ot * 128:
                                  (16 + slot) * OPC + ot * 128 + 128],
                            ptp[:, (npc + j) * 128:(npc + j + 1) * 128], AF.Copy))
                blk.scalar(dd_ev)

            if (pp + npc) % 16 == 0 or pp + npc == PC:
                nkk = ((pp + npc - 1) % 16) + 1
                kb0 = pp + npc - nkk

                def dd_fl(sy, nkk=nkk, kb0=kb0):
                    sA.w(sy)
                    dS.inc(sy.dma_start(
                        out=mk(wd_pc, kb0 * 128 * OPC,
                               [[OPC, 128], [128 * OPC, nkk], [1, OPC]]),
                        in_=mk(stg_d, 0,
                               [stg_d[:, :].ap[0], [OPC, nkk], [1, OPC]])), 16)
                    dS.inc(sy.dma_start(
                        out=mk(wd_pc, (PC + kb0) * 128 * OPC,
                               [[OPC, 128], [128 * OPC, nkk], [1, OPC]]),
                        in_=mk(stg_d, 16 * OPC,
                               [stg_d[:, :].ap[0], [OPC, nkk], [1, OPC]])), 16)
                blk.sync(dd_fl)

        ag_weight(wd_pc, wd_ga)
        CC_ALL = sCC.n

        def attn_group(gslot):
            def kv_load(g, gslot=gslot):
                g.wait_ge(sCC.h, CC_KV)
                sM.w(g)
                dS.w(g)
                with g.register(f"kr{gslot}") as kr, g.register(f"vr{gslot}") as vr:
                    g.reg_load(kr, kvoff[0:1, 2 * gslot:2 * gslot + 1])
                    g.reg_load(vr, kvoff[0:1, 2 * gslot + 1:2 * gslot + 2])
                    kreg = g.snap(kr, min_val=0, max_val=KVD - HD)
                    vreg = g.snap(vr, min_val=0, max_val=KVD - HD)
                    for dst, reg in ((kv_k, kreg), (kv_vw, vreg)):
                        dG.inc(g.dma_start(
                            out=mk(dst, 0, [dst[:, :].ap[0], [TC, NCORES], [1, TC]]),
                            in_=kv_ga[:, bass.ds(reg, HD), :].transpose([1, 0, 2]),
                            allow_slow_non_contiguous=True), 16)
            blk.gpsimd(kv_load)

            for b8 in range(0, ALLT // 128, 8):
                n8 = min(8, ALLT // 128 - b8)

                def v_tr(t, b8=b8, n8=n8):
                    dG.w(t)
                    sV.w(t)
                    sA.w(t)
                    for k in range(n8):
                        sM.inc(t.transpose(
                            ptp[:, k * 128:k * 128 + HD],
                            kv_vw[:, (b8 + k) * 128:(b8 + k + 1) * 128],
                            ident[0:HD, 0:HD]))
                blk.tensor(v_tr)

                def v_ev(v, b8=b8, n8=n8):
                    sM.w(v)
                    dst = mk(kv_vt, b8 * HD, [kv_vt[:, :].ap[0], [HD, n8], [1, HD]])
                    src = mk(ptp, 0, [ptp[:, :].ap[0], [128, n8], [1, HD]])
                    sV.inc(v.tensor_copy(dst, src))
                blk.vector(v_ev)

            for hh in range(NH // 2):
                h = gslot * (NH // 2) + hh
                bs = hh // (NH // 8)
                hp0 = (h * HD) % 128
                hti = (h * HD) // 128
                for j in range(TT):
                    ke = j + 1

                    def s_mm(t, bs=bs, j=j, ke=ke, hp0=hp0, hti=hti):
                        sA.w(t)
                        sV.w(t)
                        dG.w(t)
                        ins = None
                        lhs = qbuf[hp0:hp0 + HD,
                                   hti * TC + j * 128:hti * TC + (j + 1) * 128]
                        for s in range(2):
                            kbase = (2 * bs + s) * TC
                            done = 0
                            while done < ke * 128:
                                cur = s * ke * 128 + done
                                n = min(512 - (cur % 512), ke * 128 - done)
                                ins = t.matmul(
                                    pbig[:, cur:cur + n],
                                    lhs, kv_k[:, kbase + done:kbase + done + n],
                                    start=True, stop=True)
                                done += n
                        sM.inc(ins)
                    blk.tensor(s_mm)

                    def s_mask(v, ke=ke):
                        sM.w(v)
                        v.tensor_add(pbig[:, (ke - 1) * 128:ke * 128],
                                     pbig[:, (ke - 1) * 128:ke * 128],
                                     cmask[:, 0:128])
                        sV.inc(v.tensor_add(pbig[:, (2 * ke - 1) * 128:2 * ke * 128],
                                            pbig[:, (2 * ke - 1) * 128:2 * ke * 128],
                                            cmask[:, 128:256]))
                        sV.w(v)
                        sV.inc(v.reduce_max(sm_nm[:, 0:1], pbig[:, 0:2 * ke * 128],
                                            axis=AX.X, negate=True))
                        sV.w(v)
                        sV.inc(v.tensor_scalar_mul(sm_ns[:, 0:1], sm_nm[:, 0:1],
                                                   float(SCL)))
                    blk.vector(s_mask)

                    def s_exp(sc, ke=ke):
                        sV.w(sc)
                        sA.inc(sc.activation(p_sb[:, 0:2 * ke * 128],
                                             pbig[:, 0:2 * ke * 128], AF.Exp,
                                             bias=sm_ns[:, 0:1], scale=float(SCL),
                                             accum_out=sm_den[:, 0:1]))
                    blk.scalar(s_exp)

                    def s_rd(sc):
                        sA.w(sc)
                        sA.inc(sc.activation(sm_rd[:, 0:1], sm_den[:, 0:1], AF.Ln))
                        sA.w(sc)
                        sA.inc(sc.activation(sm_rd[:, 0:1], sm_rd[:, 0:1], AF.Exp,
                                             scale=-1.0))
                    blk.scalar(s_rd)

                    def s_norm(v, ke=ke):
                        sA.w(v)
                        sV.inc(v.tensor_scalar_mul(p_sb[:, 0:2 * ke * 128],
                                                   p_sb[:, 0:2 * ke * 128],
                                                   sm_rd[:, 0:1]))
                    blk.vector(s_norm)

                    nk = 2 * ke
                    for q8 in range(0, nk, 8):
                        n8 = min(8, nk - q8)

                        def p_tr(t, q8=q8, n8=n8):
                            sV.w(t)
                            sA.w(t)
                            for k in range(n8):
                                sM.inc(t.transpose(
                                    ptp[:, k * 128:(k + 1) * 128],
                                    p_sb[:, (q8 + k) * 128:(q8 + k + 1) * 128],
                                    ident[:, :]))
                        blk.tensor(p_tr)

                        def p_ev(sc, q8=q8, n8=n8):
                            sM.w(sc)
                            sA.inc(sc.activation(
                                pt_sb[:, q8 * 128:(q8 + n8) * 128],
                                ptp[:, 0:n8 * 128], AF.Copy))
                        blk.scalar(p_ev)

                    def av_mm(t, bs=bs, j=j, ke=ke):
                        sA.w(t)
                        sV.w(t)
                        kti = 0
                        ins = None
                        for s in range(2):
                            tb = (2 * bs + s) * TT
                            for kt in range(ke):
                                ins = t.matmul(
                                    pcx[0:HD, 0:128],
                                    kv_vt[:, (tb + kt) * HD:(tb + kt + 1) * HD],
                                    pt_sb[:, (s * ke + kt) * 128:(s * ke + kt + 1) * 128],
                                    start=(kti == 0), stop=(kti == 2 * ke - 1))
                                kti += 1
                        sM.inc(ins)
                    blk.tensor(av_mm)

                    def av_ev(v, j=j, hp0=hp0, hti=hti):
                        sM.w(v)
                        sV.inc(v.tensor_copy(
                            ctx_at(hti, j * 128, (j + 1) * 128, hp0, hp0 + HD),
                            pcx[0:HD, 0:128]))
                    blk.vector(av_ev)

        attn_group(0)
        if dbg:
            dbg_store(blk, d_kk, kv_k[:, :])
            dbg_store(blk, d_vt, kv_vt[:, :])
        attn_group(1)
        if dbg:
            dbg_store(blk, d_ctx[:, 0:LOT * TC], ctx_lo[:, :])
            if HIT:
                dbg_store(blk, d_ctx[:, LOT * TC:DT * TC], ctx_hi[:, 0:HIT * TC])

        gemm_blk(DT, wo_blk, None, qbuf, bias_fn=lambda w: bo_sb[:, w:w + 1],
                 rhs_at=ctx_at)

        if dbg:
            dbg_store(blk, d_y, qbuf[:, :])

        def xln2_slice(k):
            r = k // HPT
            lt = k % HPT
            base = 2 * GPC * r
            assert HPT >= 2 and HPT % 2 == 0
            half = HPT // 2
            if lt < half:
                off = base + 256 * lt
            else:
                off = base + 256 * (lt - half) + 1
            return mk(xlt, off, [xlt[:, :].ap[0], [2, 128]])

        for tt in range(TT):
            def p7_tr_gate(t):
                sA.w(t)
            blk.tensor(p7_tr_gate)
            tr_tile_to_feat(qbuf, lambda k, tt=tt: qbuf[:, k * TC + tt * 128:
                                                        k * TC + (tt + 1) * 128],
                            c_yt, lambda k: k * 128)

            def p7_stats(sc, tt=tt):
                sV.w(sc)
                sA.inc(sc.activation(junk[:, 0:D], c_yt[:, :], AF.Copy,
                                     accum_out=st_s1[:, 8:9]))
                sA.w(sc)
                sA.inc(sc.activation(junk[:, 0:D], c_yt[:, :], AF.Square,
                                     accum_out=st_s2[:, 8:9]))
            blk.scalar(p7_stats)
            ln_finish(8)

            def p7_aff(sc):
                sV.w(sc)
                sA.inc(sc.activation(junk[:, 0:D], c_yt[:, :], AF.Identity,
                                     scale=st_a[:, 8:9], bias=st_b[:, 8:9]))
            blk.scalar(p7_aff)

            def p7_x0(sy, tt=tt):
                sV.w(sy)
                dS.inc(sy.dma_start(out=c_x0[:, :],
                                    in_=x_own[tt * 128:(tt + 1) * 128, :]), 16)
            blk.sync(p7_x0)

            def p7_res(v):
                sA.w(v)
                dS.w(v)
                sV.inc(v.scalar_tensor_tensor(
                    c_x1[:, :], junk[:, 0:D], 1.0, c_x0[:, :],
                    ALU.mult, ALU.add, accum_out=st_s1[:, 9:10]))
            blk.vector(p7_res)

            def p7_sp(sy, tt=tt):
                sV.w(sy)
                dS.inc(sy.dma_start(out=x1_sp[tt * 128:(tt + 1) * 128, :],
                                    in_=c_x1[:, :]), 16)
            blk.sync(p7_sp)

            def p8_stats(sc):
                sV.w(sc)
                sA.inc(sc.activation(junk[:, 0:D], c_x1[:, :], AF.Square,
                                     accum_out=st_s2[:, 9:10]))
            blk.scalar(p8_stats)
            ln_finish(9)

            def p8_aff(sc):
                sV.w(sc)
                sA.inc(sc.activation(xlt[:, :], c_x1[:, :], AF.Identity,
                                     scale=st_a[:, 9:10], bias=st_b[:, 9:10]))
            blk.scalar(p8_aff)

            tr_tile_to_feat(xlt, xln2_slice,
                            xf, lambda k, tt=tt: k * TC + tt * 128)

        if dbg:
            dbg_store(blk, d_xf2, xf[:, :])

            def _dwg(sy):
                sy.wait_ge(sCC.h, CC_ALL)
                dS.inc(sy.dma_start(out=d_wg[:, :], in_=mk(
                    wg_ga, 0 * (IT * 2 * GPC * 128) + 0 * (2 * GPC * 128),
                    [[128, 2 * GPC], [1, 128]])), 16)
            blk.sync(_dwg)

        def ffn_gate_wait(eng):
            eng.wait_ge(sCC.h, CC_ALL)

        FFN_M0, FFN_A0, FFN_V0, FFN_S0 = sM.n, sA.n, sV.n, dS.n
        wg_bufs = (wg_b, wg_b2)
        wu_bufs = (wu_b, wu_b2)
        wd_bufs = (wd_b, wd_b2)
        hs_bufs = (hs_buf, hs_odd)
        snap_gmm, snap_sil, snap_hml, snap_dmm = {}, {}, {}, {}
        Lg, Ld = {}, {}

        def emit_gu_load(kg):
            pb = kg % 2
            need = snap_gmm.get(kg - 2, FFN_M0)

            def _s(sy, kg=kg, pb=pb, need=need):
                if kg < 2:
                    ffn_gate_wait(sy)
                sy.wait_ge(sM.h, need)
                for rt in range(HPT):
                    dS.inc(sy.dma_start(
                        out=mk(wg_bufs[pb], rt * 128,
                               [wg_bufs[pb][:, :].ap[0], [HPT * 128, NCORES],
                                [1, 128]]),
                        in_=mk(wg_ga, kg * (2 * GPC * 128) + rt * 128 * 128,
                               [[128, 128], [IT * 2 * GPC * 128, NCORES],
                                [1, 128]])), 16)
            blk.sync(_s)

            def _g(g, kg=kg, pb=pb, need=need):
                if kg < 2:
                    ffn_gate_wait(g)
                    g.wait_ge(dS.h, FFN_S0)
                g.wait_ge(sM.h, need)
                for rt in range(HPT):
                    dG.inc(g.dma_start(
                        out=mk(wu_bufs[pb], rt * 128,
                               [wu_bufs[pb][:, :].ap[0], [HPT * 128, NCORES],
                                [1, 128]]),
                        in_=mk(wu_ga, kg * (2 * GPC * 128) + rt * 128 * 128,
                               [[128, 128], [IT * 2 * GPC * 128, NCORES],
                                [1, 128]])), 16)
            blk.gpsimd(_g)
            Lg[kg] = (dS.n, dG.n)

        NCH = len([n for n in ICH if n > 0])
        DOWN = []
        kk = 0
        for ic, nich in enumerate(ICH):
            if nich == 0:
                continue
            for r in range(NCORES):
                DOWN.append((ic, r, kk, nich))
            kk += nich
        NDOWN = len(DOWN)
        dl_emitted = set()

        def emit_dload(di):
            ic, r, ic_base, nich = DOWN[di]
            dl_emitted.add(di)
            pb = di % 2
            sem = dS if pb == 0 else dG
            need = snap_dmm.get(di - 2, FFN_M0)

            def _l(e, di=di, r=r, ic_base=ic_base, nich=nich, pb=pb,
                   sem=sem, need=need):
                if di < 2:
                    ffn_gate_wait(e)
                    if pb == 1:
                        e.wait_ge(dS.h, FFN_S0)
                e.wait_ge(sM.h, need)
                sem.inc(e.dma_start(
                    out=mk(wd_bufs[pb], 0,
                           [wd_bufs[pb][:, :].ap[0], [OPC, nich], [1, OPC]]),
                    in_=mk(wd_ga, r * INTER * OPC + ic_base * 128 * OPC,
                           [[OPC, 128], [128 * OPC, nich], [1, OPC]])), 16)
            (blk.sync if pb == 0 else blk.gpsimd)(_l)
            Ld[di] = (sem, sem.n)

        emit_gu_load(0)
        emit_gu_load(1)
        gi = 0
        di_next = 0
        down_end_V = FFN_V0
        kk_glob = 0
        for ic in range(len(ICH)):
            nich = ICH[ic]
            if nich == 0:
                continue
            ic_base = kk_glob
            for kkl in range(nich):
                kg = ic_base + kkl
                for tc in range(NTC):
                    par = gi % 2
                    aneed = snap_sil.get(gi - 2, FFN_A0)
                    vneed = snap_hml.get(gi - 2, FFN_V0)
                    if par == 1:
                        vneed = max(vneed, down_end_V)

                    def f_gmm(t, kg=kg, tc=tc, par=par, pb=kg % 2,
                              aneed=aneed, vneed=vneed):
                        t.wait_ge(dS.h, Lg[kg][0])
                        t.wait_ge(dG.h, Lg[kg][1])
                        t.wait_ge(sA.h, aneed)
                        t.wait_ge(sV.h, vneed)
                        pg = pw0 if par == 0 else pcx
                        pu = pw1 if par == 0 else pbig
                        wgb, wub = wg_bufs[pb], wu_bufs[pb]
                        insu = None
                        for dt in range(DT):
                            t.matmul(
                                pg[:, 0:TCK],
                                wgb[:, dt * 128:(dt + 1) * 128],
                                xf[:, dt * TC + tc * TCK:dt * TC + (tc + 1) * TCK],
                                start=(dt == 0), stop=(dt == DT - 1))
                        for dt in range(DT):
                            insu = t.matmul(
                                pu[:, 0:TCK],
                                wub[:, dt * 128:(dt + 1) * 128],
                                xf[:, dt * TC + tc * TCK:dt * TC + (tc + 1) * TCK],
                                start=(dt == 0), stop=(dt == DT - 1))
                        sM.inc(insu)
                    blk.tensor(f_gmm)
                    snap_gmm[kg] = sM.n

                    def f_silu(sc, par=par):
                        sM.w(sc)
                        pg = pw0 if par == 0 else pcx
                        sA.inc(sc.activation(hs_bufs[par][:, 0:TCK],
                                             pg[:, 0:TCK], AF.Silu))
                    blk.scalar(f_silu)
                    snap_sil[gi] = sA.n

                    def f_hmul(v, kkl=kkl, tc=tc, par=par):
                        sA.w(v)
                        sM.w(v)
                        pu = pw1 if par == 0 else pbig
                        sV.inc(v.tensor_mul(
                            h_sb[:, kkl * TC + tc * TCK:kkl * TC + (tc + 1) * TCK],
                            hs_bufs[par][:, 0:TCK], pu[:, 0:TCK]))
                    blk.vector(f_hmul)
                    snap_hml[gi] = sV.n
                    gi += 1
                if kg + 2 < IT:
                    emit_gu_load(kg + 2)

            for d in (di_next, di_next + 1):
                if d < NDOWN and d not in dl_emitted:
                    emit_dload(d)
            for r in range(NCORES):
                di = di_next + r

                for tc in range(NTC):
                    def f_dmm(t, di=di, tc=tc, nich=nich):
                        sem, val = Ld[di]
                        t.wait_ge(sem.h, val)
                        sV.w(t)
                        ins = None
                        for ot in range(OTC):
                            for kkl in range(nich):
                                ins = t.matmul(
                                    pbig[:, ot * TCK:(ot + 1) * TCK],
                                    wd_bufs[di % 2][:, kkl * OPC + ot * 128:
                                                    kkl * OPC + (ot + 1) * 128],
                                    h_sb[:, kkl * TC + tc * TCK:kkl * TC + (tc + 1) * TCK],
                                    start=(kkl == 0), stop=(kkl == nich - 1))
                        sM.inc(ins)
                    blk.tensor(f_dmm)

                    def f_acc(v, ic=ic, r=r, tc=tc):
                        sM.w(v)
                        for ot in range(OTC):
                            dst = qbuf[:, (r * OTC + ot) * TC + tc * TCK:
                                       (r * OTC + ot) * TC + (tc + 1) * TCK]
                            if ic == 0:
                                ins = v.tensor_copy(dst, pbig[:, ot * TCK:(ot + 1) * TCK])
                            else:
                                ins = v.tensor_add(dst, dst,
                                                   pbig[:, ot * TCK:(ot + 1) * TCK])
                        sV.inc(ins)
                    blk.vector(f_acc)
                snap_dmm[di] = sM.n
                if di + 2 < NDOWN and (di + 2) not in dl_emitted:
                    emit_dload(di + 2)
            di_next += NCORES
            down_end_V = sV.n
            kk_glob += nich

        if dbg:
            dbg_store(blk, d_acc, qbuf[:, :])

        for tt in range(TT):
            tr_tile_to_feat(qbuf, lambda k, tt=tt: qbuf[:, k * TC + tt * 128:
                                                        k * TC + (tt + 1) * 128],
                            c_yt, lambda k: k * 128)

            def f_stats(sc):
                sV.w(sc)
                sA.inc(sc.activation(junk[:, 0:D], c_yt[:, :], AF.Copy,
                                     accum_out=st_s1[:, 10:11]))
                sA.w(sc)
                sA.inc(sc.activation(junk[:, 0:D], c_yt[:, :], AF.Square,
                                     accum_out=st_s2[:, 10:11]))
            blk.scalar(f_stats)
            ln_finish(10)

            def f_aff(sc):
                sV.w(sc)
                sA.inc(sc.activation(junk[:, 0:D], c_yt[:, :], AF.Identity,
                                     scale=st_a[:, 10:11], bias=st_b[:, 10:11]))
            blk.scalar(f_aff)

            def f_x1(sy, tt=tt):
                sV.w(sy)
                dS.inc(sy.dma_start(out=c_x1[:, :],
                                    in_=x1_sp[tt * 128:(tt + 1) * 128, :]), 16)
            blk.sync(f_x1)

            def f_out(v):
                sA.w(v)
                dS.w(v)
                sV.inc(v.scalar_tensor_tensor(
                    c_ot[:, :], junk[:, 0:D], 1.0, c_x1[:, :],
                    ALU.mult, ALU.add))
            blk.vector(f_out)

            def f_store(sy, tt=tt):
                sV.w(sy)
                dS.inc(sy.dma_start(out=out_d[tt * 128:(tt + 1) * 128, :],
                                    in_=c_ot[:, :]), 16)
            blk.sync(f_store)

        def fin(sy):
            dS.w(sy)
        blk.sync(fin)

    stk.close()
    return nc



def stage_inputs(cfg, inp):
    c = cfg_derive(cfg)
    D, HD, TC, TBC = c["D"], c["HD"], c["TC"], c["TBC"]
    DT, KVT, IT = c["DT"], c["KVT"], c["IT"]
    INTER, GPC, GGRP = c["INTER"], c["GPC"], c["GGRP"]
    OPC, DGRP = c["OPC"], c["DGRP"]

    x = np.asarray(inp["L0_attn_x"])
    wq = np.asarray(inp["L0_attn_Wq"])
    wkv = np.asarray(inp["L0_attn_Wkv"])
    wo = np.asarray(inp["L0_attn_Wo"])
    bo = np.asarray(inp["L0_attn_bo"])
    gq = np.asarray(inp["L0_ffn_gate_Wq"])
    uq = np.asarray(inp["L0_ffn_up_Wq"])
    dq = np.asarray(inp["L0_ffn_down_Wq"])
    gsc = np.asarray(inp["L0_ffn_gate_scale"])
    gzr = np.asarray(inp["L0_ffn_gate_zero"])
    usc = np.asarray(inp["L0_ffn_up_scale"])
    uzr = np.asarray(inp["L0_ffn_up_zero"])
    dsc = np.asarray(inp["L0_ffn_down_scale"])
    dzr = np.asarray(inp["L0_ffn_down_zero"])

    CC = np.ascontiguousarray
    wq_blk = CC(wq.reshape(D, DT, 128).transpose(1, 0, 2))
    wkv_blk = CC(wkv.reshape(D, KVT, 128).transpose(1, 0, 2))
    wo_blk = CC(wo.reshape(D, DT, 128).transpose(1, 0, 2))
    bo_st = CC(bo.reshape(DT, 128).T)
    ident = np.eye(128, dtype=ml_dtypes.bfloat16)
    perm_i = np.concatenate([np.arange(0, INTER, 2), np.arange(1, INTER, 2)])
    gq_p = gq[perm_i]
    uq_p = uq[perm_i]
    gsc_p = gsc[perm_i]
    gzr_p = gzr[perm_i]
    usc_p = usc[perm_i]
    uzr_p = uzr[perm_i]

    tri = np.triu(np.full((128, 128), NEG, np.float32), 1)
    full = np.full((128, 128), NEG, np.float32)
    zero = np.zeros((128, 128), np.float32)

    in_maps = []
    for cid in range(NCORES):
        bp, par = cid // 2, cid % 2
        xo = CC(x[bp].reshape(2 * TBC, 128, D)[par::2].reshape(TC, D))
        cm = np.concatenate([tri, full] if par == 0 else [zero, tri],
                            axis=1).astype(np.float32)
        kvo = (np.array([0, HD, 2 * HD, 3 * HD], np.int32)
               + np.int32(4 * HD * bp)).reshape(1, 4)
        ngr = GGRP
        m = dict(
            x_own=xo,
            wq_blk=wq_blk, wkv_blk=wkv_blk, wo_blk=wo_blk, bo_st=bo_st,
            ident_d=ident, cmask_d=CC(cm), kvoffs_d=kvo,
            gate_q=CC(gq_p[:, GPC * cid:GPC * (cid + 1)]),
            up_q=CC(uq_p[:, GPC * cid:GPC * (cid + 1)]),
            down_q=CC(dq[OPC * cid:OPC * (cid + 1), :]),
            g_sc_d=CC(gsc_p[:, ngr * cid:ngr * (cid + 1)]),
            g_zr_d=CC(gzr_p[:, ngr * cid:ngr * (cid + 1)]),
            u_sc_d=CC(usc_p[:, ngr * cid:ngr * (cid + 1)]),
            u_zr_d=CC(uzr_p[:, ngr * cid:ngr * (cid + 1)]),
            d_sc_d=CC(dsc[OPC * cid:OPC * (cid + 1), :]),
            d_zr_d=CC(dzr[OPC * cid:OPC * (cid + 1), :]),
        )
        in_maps.append(m)
    return in_maps


def assemble_out(cfg, shards):
    c = cfg_derive(cfg)
    D, TC, TBC = c["D"], c["TC"], c["TBC"]
    out = np.empty((NCORES // 2, 2 * TC, D), np.float32)
    for cid in range(NCORES):
        bp, par = cid // 2, cid % 2
        blkv = out[bp].reshape(2 * TBC, 128, D)
        blkv[par::2] = shards[cid].reshape(TBC, 128, D)
    return out


def kernel(**inputs):
    cfg = REAL
    in_maps = stage_inputs(cfg, inputs)
    nc = build_nc(cfg)
    res = run_bass_kernel_spmd(nc, in_maps, core_ids=list(range(NCORES)))
    shards = [np.asarray(res.results[i]["out"]) for i in range(NCORES)]
    return assemble_out(cfg, shards)


# version 09
# speedup vs baseline: 1.3164x; 1.1390x over previous
import numpy as np
import ml_dtypes
import concourse.bass as bass
import concourse.mybir as mybir
from concourse.bass_utils import run_bass_kernel_spmd

F32 = mybir.dt.float32
BF16 = mybir.dt.bfloat16
I32 = mybir.dt.int32
AF = mybir.ActivationFunctionType
ALU = mybir.AluOpType
AX = mybir.AxisListType

NCORES = 8
NEG = -1e9
EPS = 1e-5

REAL = dict(DT=32, HD=128, TBC=8, IT=86, NH=32)
TINY = dict(DT=16, HD=128, TBC=8, IT=40, NH=16)


def cfg_derive(cfg):
    c = dict(cfg)
    c["D"] = c["DT"] * 128
    c["NH"] = c.get("NH", 32)
    c["NG"] = 8
    assert c["NH"] % 8 == 0
    assert c["NH"] * c["HD"] == c["D"]
    c["KVD"] = 2 * c["HD"] * c["NG"]
    c["KVT"] = c["KVD"] // 128
    assert c["KVD"] % 128 == 0
    c["TC"] = c["TBC"] * 128
    c["SEQ"] = 2 * c["TC"]
    c["INTER"] = c["IT"] * 128
    assert (c["D"] // 2) % (NCORES * 32) == 0
    c["GPC"] = c["D"] // 2 // NCORES
    c["GGRP"] = c["GPC"] // 32
    c["HPT"] = 2 * c["GPC"] // 128
    assert (2 * c["GPC"]) % 128 == 0 and c["HPT"] >= 1
    c["OPC"] = c["D"] // NCORES
    c["OTC"] = c["OPC"] // 128
    assert c["OPC"] % 128 == 0
    c["PC"] = c["INTER"] // 2 // 128
    assert (c["INTER"] // 2) % 128 == 0
    c["DGRP"] = c["INTER"] // 64
    it = c["IT"]
    NCH = 16
    base, rem = it // NCH, it % NCH
    c["ICH"] = [base + (1 if i < rem else 0) for i in range(NCH)]
    return c


def mk(t, off, dims):
    return bass.AP(t, off, dims)


def build_nc(cfg, dbg=False):
    import contextlib

    c = cfg_derive(cfg)
    DT, HD, TBC, IT = c["DT"], c["HD"], c["TBC"], c["IT"]
    D, NH, NG, KVT, TC = c["D"], c["NH"], c["NG"], c["KVT"], c["TC"]
    KVD = c["KVD"]
    INTER, GPC, GGRP, HPT = c["INTER"], c["GPC"], c["GGRP"], c["HPT"]
    OPC, OTC, PC, ICH, DGRP = c["OPC"], c["OTC"], c["PC"], c["ICH"], c["DGRP"]
    TT = TBC
    NTC = max(TC // 512, 1)
    TCK = TC // NTC
    SCL = 1.0 / float(np.sqrt(HD))
    ALLT = NCORES * TC

    nc = bass.Bass("TRN2", target_bir_lowering=False, debug=False,
                   detect_race_conditions=True)

    x_own = nc.dram_tensor("x_own", [TC, D], F32, kind="ExternalInput")
    wq_blk = nc.dram_tensor("wq_blk", [DT, D, 128], F32, kind="ExternalInput")
    wkv_blk = nc.dram_tensor("wkv_blk", [KVT, D, 128], F32, kind="ExternalInput")
    wo_blk = nc.dram_tensor("wo_blk", [DT, D, 128], F32, kind="ExternalInput")
    bo_st = nc.dram_tensor("bo_st", [128, DT], F32, kind="ExternalInput")
    ident_d = nc.dram_tensor("ident_d", [128, 128], BF16, kind="ExternalInput")
    cmask_d = nc.dram_tensor("cmask_d", [128, 256], F32, kind="ExternalInput")
    kvoffs_d = nc.dram_tensor("kvoffs_d", [1, 4], I32, kind="ExternalInput")
    gate_q = nc.dram_tensor("gate_q", [INTER, GPC], I32, kind="ExternalInput")
    up_q = nc.dram_tensor("up_q", [INTER, GPC], I32, kind="ExternalInput")
    down_q = nc.dram_tensor("down_q", [OPC, INTER // 2], I32, kind="ExternalInput")
    g_sc_d = nc.dram_tensor("g_sc_d", [INTER, GGRP], F32, kind="ExternalInput")
    g_zr_d = nc.dram_tensor("g_zr_d", [INTER, GGRP], F32, kind="ExternalInput")
    u_sc_d = nc.dram_tensor("u_sc_d", [INTER, GGRP], F32, kind="ExternalInput")
    u_zr_d = nc.dram_tensor("u_zr_d", [INTER, GGRP], F32, kind="ExternalInput")
    d_sc_d = nc.dram_tensor("d_sc_d", [OPC, DGRP], F32, kind="ExternalInput")
    d_zr_d = nc.dram_tensor("d_zr_d", [OPC, DGRP], F32, kind="ExternalInput")
    out_d = nc.dram_tensor("out", [TC, D], F32, kind="ExternalOutput")

    kv_in = nc.dram_tensor("kv_in", [KVD, TC], BF16)
    kv_ga = nc.dram_tensor("kv_ga", [NCORES, KVD, TC], BF16)
    wg_pc = nc.dram_tensor("wg_pc", [IT, 2 * GPC, 128], BF16)
    wu_pc = nc.dram_tensor("wu_pc", [IT, 2 * GPC, 128], BF16)
    wg_ga = nc.dram_tensor("wg_ga", [NCORES, IT, 2 * GPC, 128], BF16)
    wu_ga = nc.dram_tensor("wu_ga", [NCORES, IT, 2 * GPC, 128], BF16)
    wd_pc = nc.dram_tensor("wd_pc", [INTER, OPC], BF16)
    wd_ga = nc.dram_tensor("wd_ga", [NCORES, INTER, OPC], BF16)
    x1_sp = nc.dram_tensor("x1_sp", [TC, D], BF16)
    if dbg:
        d_xf = nc.dram_tensor("d_xf", [128, DT * TC], BF16, kind="ExternalOutput")
        d_q = nc.dram_tensor("d_q", [128, DT * TC], BF16, kind="ExternalOutput")
        d_kv = nc.dram_tensor("d_kv", [128, KVT * TC], BF16, kind="ExternalOutput")
        d_kk = nc.dram_tensor("d_kk", [HD, ALLT], BF16, kind="ExternalOutput")
        d_vt = nc.dram_tensor("d_vt", [128, ALLT * HD // 128], BF16, kind="ExternalOutput")
        d_ctx = nc.dram_tensor("d_ctx", [128, DT * TC], BF16, kind="ExternalOutput")
        d_y = nc.dram_tensor("d_y", [128, DT * TC], BF16, kind="ExternalOutput")
        d_xf2 = nc.dram_tensor("d_xf2", [128, DT * TC], BF16, kind="ExternalOutput")
        d_acc = nc.dram_tensor("d_acc", [128, DT * TC], BF16, kind="ExternalOutput")
        d_wg = nc.dram_tensor("d_wg", [2 * GPC, 128], BF16, kind="ExternalOutput")

    KBy = 1024
    SM = 16512

    def at(name, shape, dtype, off):
        return nc.alloc_sbuf_tensor_at(name, shape, dtype, offset=off)

    ident = at("ident", [128, 128], BF16, SM)
    cmask = at("cmask", [128, 256], F32, SM + 256)
    bo_sb = at("bo_sb", [128, DT], F32, SM + 1280)
    kvoff = at("kvoff", [1, 4], I32, SM + 1472)
    st_s1 = at("st_s1", [128, 16], F32, SM + 1536)
    st_s2 = at("st_s2", [128, 16], F32, SM + 1600)
    st_a = at("st_a", [128, 16], F32, SM + 1664)
    st_b = at("st_b", [128, 16], F32, SM + 1728)
    st_t0 = at("st_t0", [128, 16], F32, SM + 1792)
    st_t1 = at("st_t1", [128, 16], F32, SM + 1856)
    sm_nm = at("sm_nm", [128, 1], F32, SM + 1920)
    sm_ns = at("sm_ns", [128, 1], F32, SM + 1952)
    sm_den = at("sm_den", [128, 1], F32, SM + 1984)
    sm_rd = at("sm_rd", [128, 1], F32, SM + 2016)
    eps_sb = at("eps_sb", [128, 1], F32, SM + 2048)
    hs_buf = at("hs_buf", [128, 512], F32, SM + 2080)
    p_sb = at("p_sb", [128, 2 * TT * 128], BF16, SM + 4128)
    pt_sb = at("pt_sb", [128, 2 * TT * 128], BF16, SM + 4128 + 2 * TT * 128 * 2)
    junk = at("junk", [128, D], BF16, SM + 4128 + 4 * TT * 128 * 2)
    RA = SM + 4128 + 4 * TT * 128 * 2 + 2 * D
    RA = (RA + 31) // 32 * 32

    xf = at("xf", [128, DT * TC], BF16, RA)
    _o = RA
    dq_gq = at("dq_gq", [128, 4 * max(GPC, 256)], I32, _o)
    _o += 4 * max(GPC, 256) * 4
    dq_nib = at("dq_nib", [128, 2 * max(GPC, 256)], I32, _o)
    _o += 2 * max(GPC, 256) * 4
    dq_w = at("dq_w", [128, 2 * max(GPC, 256)], BF16, _o)
    _o += 2 * max(GPC, 256) * 2
    dq_zs = at("dq_zs", [128, 64], F32, _o)
    _o += 256
    g_sc = at("g_sc", [128, IT * GGRP], F32, _o); _o += IT * GGRP * 4
    g_zr = at("g_zr", [128, IT * GGRP], F32, _o); _o += IT * GGRP * 4
    u_sc = at("u_sc", [128, IT * GGRP], F32, _o); _o += IT * GGRP * 4
    u_zr = at("u_zr", [128, IT * GGRP], F32, _o); _o += IT * GGRP * 4
    d_sc = at("d_sc", [128, OTC * DGRP], F32, _o); _o += OTC * DGRP * 4
    d_zr = at("d_zr", [128, OTC * DGRP], F32, _o); _o += OTC * DGRP * 4
    dq_dq = at("dq_dq", [128, OTC * 2 * 128], I32, _o); _o += OTC * 2 * 128 * 4
    stg_g = at("stg_g", [128, HPT * 16 * 128], BF16, _o)
    stg_d = at("stg_d", [128, 2 * 16 * OPC], BF16, _o)
    _o += max(HPT * 16 * 128 * 2, 2 * 16 * OPC * 2)
    kv_k = at("kv_k", [HD, ALLT], BF16, RA)
    kv_vw = at("kv_vw", [HD, ALLT], BF16, RA + ALLT * 2)
    kv_vt = at("kv_vt", [128, (ALLT // 128) * HD], BF16, RA + ALLT * 2)

    KVAREA = 2 * ALLT * 2
    XFSZ = DT * TC * 2
    HIT = min(DT, max(0, (XFSZ - KVAREA) // (TC * 2)))
    LOT = DT - HIT
    ctx_hi = at("ctx_hi", [128, max(HIT, 1) * TC], BF16, RA + KVAREA)
    MID = RA + max(KVAREA, XFSZ)
    x0t = at("x0t", [128, 2 * D], F32, MID)
    kvs = at("kvs", [128, KVT * TC], BF16, MID)
    ctx_lo = at("ctx_lo", [128, LOT * TC], BF16, MID)
    c_x0 = at("c_x0", [128, D], F32, MID)
    c_yt = at("c_yt", [128, D], BF16, MID + D * 4)
    c_x1 = at("c_x1", [128, D], BF16, MID + D * 4 + D * 2)
    c_ot = at("c_ot", [128, D], F32, MID)
    h_sb = at("h_sb", [128, max(ICH) * TC], BF16, MID)
    _f = MID + max(ICH) * TC * 2
    wg_b = at("wg_b", [128, D], BF16, _f); _f += D * 2
    wu_b = at("wu_b", [128, D], BF16, _f); _f += D * 2
    wd_b = at("wd_b", [128, max(ICH) * OPC], BF16, _f)
    _f += max(ICH) * OPC * 2
    MIDSZ = max(LOT * TC * 2, 2 * D * 4, KVT * TC * 2,
                D * 4 + 2 * D * 2, _f - MID)

    def ctx_at(dt, a, b, p0=0, p1=128):
        if dt < LOT:
            return ctx_lo[p0:p1, dt * TC + a:dt * TC + b]
        return ctx_hi[p0:p1, (dt - LOT) * TC + a:(dt - LOT) * TC + b]
    QB = MID + MIDSZ
    QB = (QB + 31) // 32 * 32
    qbuf = at("qbuf", [128, DT * TC], BF16, QB)
    TL = QB + DT * TC * 2
    xlt = at("xlt", [128, D], BF16, TL)
    wqb = at("wqb", [128, 2 * D], BF16, TL + D * 2)
    hs_odd = at("hs_odd", [128, 512], F32, SM + 4128)
    wd_b2 = at("wd_b2", [128, max(ICH) * OPC], BF16, TL)
    wg_b2 = at("wg_b2", [128, D], BF16, TL + D * 2)
    wu_b2 = at("wu_b2", [128, D], BF16, TL + D * 2 + D * 2)
    assert max(ICH) * OPC * 2 <= D * 2
    assert TL + D * 2 + 2 * D * 2 <= 229344, (
        f"sbuf overflow end={TL + D * 2 + 2 * D * 2}")
    assert _o <= 229344

    pbig = nc.alloc_psum_tensor("pbig", [128, 2048], F32)
    pw0 = nc.alloc_psum_tensor("pw0", [128, 512], F32)
    pw1 = nc.alloc_psum_tensor("pw1", [128, 512], F32)
    pcx = nc.alloc_psum_tensor("pcx", [128, 512], F32)
    ptp = nc.alloc_psum_tensor("ptp", [128, 1024], BF16)

    stk = contextlib.ExitStack()

    class SemC:
        def __init__(self, name):
            self.h = stk.enter_context(nc.semaphore(name))
            self.n = 0

        def inc(self, instr, k=1):
            instr.then_inc(self.h, k)
            self.n += k

        def w(self, eng):
            if self.n:
                eng.wait_ge(self.h, self.n)

    dS, dG, sM, sV, sA, sCC, sF = (SemC(x) for x in
                                   ("dS", "dG", "sM", "sV", "sA", "sCC", "sF"))
    dW = [SemC("dW0"), SemC("dW1")]

    def dbg_store(blkh, dst, src_ap):
        if not dbg:
            return

        def _d(sy):
            sA.w(sy)
            sV.w(sy)
            sM.w(sy)
            if isinstance(dst, bass.AP):
                out = dst
            else:
                out = dst[:, :] if dst.shape[0] <= 128 else dst.ap()
            dS.inc(sy.dma_start(out=out, in_=src_ap), 16)
        blkh.sync(_d)

    with nc.Block() as blk:
        def p0(sy):
            for dst, src in ((ident, ident_d), (cmask, cmask_d),
                             (bo_sb, bo_st), (kvoff, kvoffs_d)):
                dS.inc(sy.dma_start(out=dst[:, :], in_=src[:, :]), 16)
        blk.sync(p0)

        def p0v(v):
            sV.inc(v.memset(eps_sb[:, :], float(EPS)))
        blk.vector(p0v)

        def ln_finish(col):
            def _v(v):
                sA.w(v)
                invn = 1.0 / float(D)
                m = st_t0[:, col:col + 1]
                var = st_t1[:, col:col + 1]
                sV.inc(v.tensor_scalar_mul(m, st_s1[:, col:col + 1], invn))
                sV.w(v)
                sV.inc(v.tensor_scalar_mul(var, st_s2[:, col:col + 1], invn))
                sV.w(v)
                sV.inc(v.tensor_mul(st_a[:, col:col + 1], m, m))
                sV.w(v)
                sV.inc(v.tensor_sub(var, var, st_a[:, col:col + 1]))
            blk.vector(_v)

            def _s(sc):
                sV.w(sc)
                sA.inc(sc.activation(st_t1[:, col:col + 1], st_t1[:, col:col + 1],
                                     AF.Ln, bias=eps_sb[:, 0:1]))
                sA.w(sc)
                sA.inc(sc.activation(st_a[:, col:col + 1], st_t1[:, col:col + 1],
                                     AF.Exp, scale=-0.5))
            blk.scalar(_s)

            def _v2(v):
                sA.w(v)
                sV.inc(v.scalar_tensor_tensor(
                    st_b[:, col:col + 1], st_t0[:, col:col + 1], -1.0,
                    st_a[:, col:col + 1], ALU.mult, ALU.mult))
            blk.vector(_v2)

        def tr_tile_to_feat(src_t, slicer, dst_t, dst_col_of_k, wait_extra=None):
            for g8 in range(0, DT, 8):
                n8 = min(8, DT - g8)

                def _t(t, g8=g8, n8=n8):
                    sA.w(t)
                    sV.w(t)
                    if wait_extra is not None:
                        wait_extra(t)
                    for k in range(n8):
                        sM.inc(t.transpose(ptp[:, k * 128:(k + 1) * 128],
                                           slicer(g8 + k), ident[:, :]))
                blk.tensor(_t)

                def _v(v, g8=g8, n8=n8):
                    sM.w(v)
                    c0 = dst_col_of_k(g8)
                    cs = dst_col_of_k(g8 + 1) - c0
                    dst = mk(dst_t, c0, [dst_t[:, :].ap[0], [cs, n8], [1, 128]])
                    src = mk(ptp, 0, [ptp[:, :].ap[0], [128, n8], [1, 128]])
                    sV.inc(v.tensor_copy(dst, src))
                blk.vector(_v)

        for tt in range(TT):
            pb = tt % 2

            def p1_load(sy, tt=tt, pb=pb):
                sA.w(sy)
                dS.inc(sy.dma_start(out=x0t[:, pb * D:(pb + 1) * D],
                                    in_=x_own[tt * 128:(tt + 1) * 128, :]), 16)
            blk.sync(p1_load)

            def p1_stats(sc, tt=tt, pb=pb):
                dS.w(sc)
                xin = x0t[:, pb * D:(pb + 1) * D]
                sA.inc(sc.activation(junk[:, 0:D], xin, AF.Copy,
                                     accum_out=st_s1[:, tt:tt + 1]))
                sA.w(sc)
                sA.inc(sc.activation(junk[:, 0:D], xin, AF.Square,
                                     accum_out=st_s2[:, tt:tt + 1]))
            blk.scalar(p1_stats)
            ln_finish(tt)

            def p1_aff(sc, tt=tt, pb=pb):
                sV.w(sc)
                sA.inc(sc.activation(xlt[:, :], x0t[:, pb * D:(pb + 1) * D],
                                     AF.Identity, scale=st_a[:, tt:tt + 1],
                                     bias=st_b[:, tt:tt + 1]))
            blk.scalar(p1_aff)

            tr_tile_to_feat(xlt, lambda k: xlt[:, k * 128:(k + 1) * 128],
                            xf, lambda k, tt=tt: k * TC + tt * 128)

        def gemm_blk(nblk, wsrc, rhs_sb, dst_sb, bias_fn=None, rhs_at=None):
            if rhs_at is None:
                def rhs_at(dt, a, b, rhs_sb=rhs_sb):
                    return rhs_sb[:, dt * TC + a:dt * TC + b]
            for w in range(nblk):
                pb = w % 2

                def g_load(g, w=w, pb=pb):
                    sA.w(g)
                    dW[pb].inc(g.dma_start(
                        out=mk(wqb, pb * D, [wqb[:, :].ap[0], [128, DT], [1, 128]]),
                        in_=mk(wsrc, w * D * 128,
                               [[128, 128], [128 * 128, DT], [1, 128]])), 16)
                blk.gpsimd(g_load)

                for tc in range(NTC):
                    ps = pw0 if (w * NTC + tc) % 2 == 0 else pw1

                    def g_mm(t, w=w, tc=tc, ps=ps, pb=pb):
                        dW[pb].w(t)
                        sA.w(t)
                        sV.w(t)
                        ins = None
                        for dt in range(DT):
                            ins = t.matmul(
                                ps[:, 0:TCK],
                                wqb[:, pb * D + dt * 128:pb * D + (dt + 1) * 128],
                                rhs_at(dt, tc * TCK, (tc + 1) * TCK),
                                start=(dt == 0), stop=(dt == DT - 1))
                        sM.inc(ins)
                    blk.tensor(g_mm)

                    def g_ev(sc, w=w, tc=tc, ps=ps):
                        sM.w(sc)
                        fn = AF.Copy
                        kw = {}
                        if bias_fn is not None:
                            kw["bias"] = bias_fn(w)
                            fn = AF.Identity
                        sA.inc(sc.activation(
                            dst_sb[:, w * TC + tc * TCK:w * TC + (tc + 1) * TCK],
                            ps[:, 0:TCK], fn, **kw))
                    blk.scalar(g_ev)

        if dbg:
            dbg_store(blk, d_xf, xf[:, :])
        gemm_blk(DT, wq_blk, xf, qbuf)
        gemm_blk(KVT, wkv_blk, xf, kvs)
        if dbg:
            dbg_store(blk, d_q, qbuf[:, :])
            dbg_store(blk, d_kv, kvs[:, :])

        def p2_store_ag(g):
            sA.w(g)
            dG.inc(g.dma_start(
                out=mk(kv_in, 0, [[TC, 128], [128 * TC, KVT], [1, TC]]),
                in_=mk(kvs, 0, [kvs[:, :].ap[0], [TC, KVT], [1, TC]])), 16)
            dG.w(g)
            sCC.inc(g.collective_compute(
                "AllGather", ALU.bypass, replica_groups=[list(range(NCORES))],
                ins=[kv_in[:, :].opt()], outs=[kv_ga.ap().opt()]))
        blk.gpsimd(p2_store_ag)
        CC_KV = sCC.n

        def dq_gateup(qsrc, scs, zrs, piece):
            for it4 in range(0, IT, 4):
                nb = min(4, IT - it4)

                def d_load(sy, it4=it4, nb=nb):
                    sV.w(sy)
                    sM.w(sy)
                    dS.inc(sy.dma_start(
                        out=mk(dq_gq, 0, [dq_gq[:, :].ap[0], [GPC, nb], [1, GPC]]),
                        in_=mk(qsrc, it4 * 128 * GPC,
                               [[GPC, 128], [128 * GPC, nb], [1, GPC]])), 16)
                blk.sync(d_load)

                for k in range(nb):
                    it = it4 + k

                    def d_vec(v, it=it, k=k, scs=scs, zrs=zrs):
                        dS.w(v)
                        sA.w(v)
                        q = dq_gq[:, k * GPC:(k + 1) * GPC]
                        v.tensor_mul(dq_zs[:, 0:GGRP],
                                     zrs[:, it * GGRP:(it + 1) * GGRP],
                                     scs[:, it * GGRP:(it + 1) * GGRP])
                        v.tensor_scalar(dq_nib[:, 0:GPC], q, 4, 15,
                                        ALU.logical_shift_right, ALU.bitwise_and)
                        sV.inc(v.tensor_scalar(dq_nib[:, GPC:2 * GPC], q, 15, None,
                                               ALU.bitwise_and))
                        sV.w(v)
                        scb = mk(scs, it * GGRP,
                                 [scs[:, :].ap[0], [0, 2], [1, GGRP], [0, 32]])
                        zsb = mk(dq_zs, 0,
                                 [dq_zs[:, :].ap[0], [0, 2], [1, GGRP], [0, 32]])
                        nv = mk(dq_nib, 0,
                                [dq_nib[:, :].ap[0], [GPC, 2], [32, GGRP], [1, 32]])
                        wv = mk(dq_w, 0,
                                [dq_w[:, :].ap[0], [GPC, 2], [32, GGRP], [1, 32]])
                        sV.inc(v.tensor_tensor(wv, nv, scb, ALU.mult))
                        sV.w(v)
                        sV.inc(v.tensor_tensor(wv, wv, zsb, ALU.subtract))
                    blk.vector(d_vec)

                    def d_tr(t):
                        sV.w(t)
                        sA.w(t)
                        for rt in range(HPT):
                            sM.inc(t.transpose(ptp[:, rt * 128:(rt + 1) * 128],
                                               dq_w[:, rt * 128:(rt + 1) * 128],
                                               ident[:, :]))
                    blk.tensor(d_tr)

                    def d_ev(sc, it=it):
                        sM.w(sc)
                        if it % 16 == 0 and it > 0:
                            dS.w(sc)
                        kb = it % 16
                        dst = mk(stg_g, kb * 128,
                                 [stg_g[:, :].ap[0], [16 * 128, HPT], [1, 128]])
                        src = mk(ptp, 0, [ptp[:, :].ap[0], [128, HPT], [1, 128]])
                        sA.inc(sc.activation(dst, src, AF.Copy))
                    blk.scalar(d_ev)

                    if (it + 1) % 16 == 0 or it == IT - 1:
                        nkt = (it % 16) + 1
                        kb0 = it - nkt + 1

                        def d_fl(sy, nkt=nkt, kb0=kb0, piece=piece):
                            sA.w(sy)
                            for rt in range(HPT):
                                dS.inc(sy.dma_start(
                                    out=mk(piece,
                                           kb0 * (2 * GPC * 128) + rt * 128 * 128,
                                           [[128, 128], [2 * GPC * 128, nkt], [1, 128]]),
                                    in_=mk(stg_g, rt * 16 * 128,
                                           [stg_g[:, :].ap[0], [128, nkt], [1, 128]])), 16)
                        blk.sync(d_fl)

        def ag_weight(piece, gaout):
            def _r(sy):
                sy.wait_ge(dS.h, dS.n)
                sF.inc(sy.nop())
            blk.sync(_r)

            def _g(gg):
                sF.w(gg)
                sCC.inc(gg.collective_compute(
                    "AllGather", ALU.bypass, replica_groups=[list(range(NCORES))],
                    ins=[piece.ap().opt()], outs=[gaout.ap().opt()]))
            blk.gpsimd(_g)

        for engsec in (blk.sync, blk.scalar, blk.vector, blk.tensor):
            def _fence(e):
                dG.w(e)
                sA.w(e)
                sM.w(e)
                sV.w(e)
            engsec(_fence)

        def p3_scales(sy):
            for dst, src, nt, g in ((g_sc, g_sc_d, IT, GGRP), (g_zr, g_zr_d, IT, GGRP),
                                    (u_sc, u_sc_d, IT, GGRP), (u_zr, u_zr_d, IT, GGRP),
                                    (d_sc, d_sc_d, OTC, DGRP), (d_zr, d_zr_d, OTC, DGRP)):
                dS.inc(sy.dma_start(
                    out=mk(dst, 0, [dst[:, :].ap[0], [g, nt], [1, g]]),
                    in_=mk(src, 0, [[g, 128], [128 * g, nt], [1, g]])), 16)
        blk.sync(p3_scales)

        dq_gateup(gate_q, g_sc, g_zr, wg_pc)
        ag_weight(wg_pc, wg_ga)
        dq_gateup(up_q, u_sc, u_zr, wu_pc)
        ag_weight(wu_pc, wu_ga)

        for pp in range(0, PC, 2):
            npc = min(2, PC - pp)

            def dd_load(sy, pp=pp, npc=npc):
                sV.w(sy)
                sM.w(sy)
                dS.inc(sy.dma_start(
                    out=mk(dq_dq, 0,
                           [dq_dq[:, :].ap[0], [npc * 128, OTC], [1, npc * 128]]),
                    in_=mk(down_q, pp * 128,
                           [[INTER // 2, 128], [128 * (INTER // 2), OTC],
                            [1, npc * 128]])), 16)
            blk.sync(dd_load)

            for ot in range(OTC):
                def dd_vec(v, pp=pp, npc=npc, ot=ot):
                    dS.w(v)
                    sA.w(v)
                    q = dq_dq[:, ot * npc * 128:(ot + 1) * npc * 128]
                    gofs = ot * DGRP + 4 * pp
                    ng = 4 * npc
                    v.tensor_mul(dq_zs[:, 0:ng], d_zr[:, gofs:gofs + ng],
                                 d_sc[:, gofs:gofs + ng])
                    v.tensor_scalar(dq_nib[:, 0:npc * 128], q, 4, 15,
                                    ALU.logical_shift_right, ALU.bitwise_and)
                    sV.inc(v.tensor_scalar(dq_nib[:, npc * 128:2 * npc * 128], q,
                                           15, None, ALU.bitwise_and))
                    sV.w(v)
                    scb = mk(d_sc, gofs, [d_sc[:, :].ap[0], [0, 2], [1, ng], [0, 32]])
                    zsb = mk(dq_zs, 0, [dq_zs[:, :].ap[0], [0, 2], [1, ng], [0, 32]])
                    nv = mk(dq_nib, 0,
                            [dq_nib[:, :].ap[0], [npc * 128, 2], [32, ng], [1, 32]])
                    wv = mk(dq_w, 0,
                            [dq_w[:, :].ap[0], [npc * 128, 2], [32, ng], [1, 32]])
                    sV.inc(v.tensor_tensor(wv, nv, scb, ALU.mult))
                    sV.w(v)
                    sV.inc(v.tensor_tensor(wv, wv, zsb, ALU.subtract))
                blk.vector(dd_vec)

                def dd_tr(t, npc=npc):
                    sV.w(t)
                    sA.w(t)
                    for j in range(2 * npc):
                        sM.inc(t.transpose(ptp[:, j * 128:(j + 1) * 128],
                                           dq_w[:, j * 128:(j + 1) * 128],
                                           ident[:, :]))
                blk.tensor(dd_tr)

                def dd_ev(sc, pp=pp, npc=npc, ot=ot):
                    sM.w(sc)
                    if pp % 16 == 0 and pp > 0 and ot == 0:
                        dS.w(sc)
                    for j in range(npc):
                        slot = (pp + j) % 16
                        sA.inc(sc.activation(
                            stg_d[:, slot * OPC + ot * 128:slot * OPC + ot * 128 + 128],
                            ptp[:, j * 128:(j + 1) * 128], AF.Copy))
                        sA.inc(sc.activation(
                            stg_d[:, (16 + slot) * OPC + ot * 128:
                                  (16 + slot) * OPC + ot * 128 + 128],
                            ptp[:, (npc + j) * 128:(npc + j + 1) * 128], AF.Copy))
                blk.scalar(dd_ev)

            if (pp + npc) % 16 == 0 or pp + npc == PC:
                nkk = ((pp + npc - 1) % 16) + 1
                kb0 = pp + npc - nkk

                def dd_fl(sy, nkk=nkk, kb0=kb0):
                    sA.w(sy)
                    dS.inc(sy.dma_start(
                        out=mk(wd_pc, kb0 * 128 * OPC,
                               [[OPC, 128], [128 * OPC, nkk], [1, OPC]]),
                        in_=mk(stg_d, 0,
                               [stg_d[:, :].ap[0], [OPC, nkk], [1, OPC]])), 16)
                    dS.inc(sy.dma_start(
                        out=mk(wd_pc, (PC + kb0) * 128 * OPC,
                               [[OPC, 128], [128 * OPC, nkk], [1, OPC]]),
                        in_=mk(stg_d, 16 * OPC,
                               [stg_d[:, :].ap[0], [OPC, nkk], [1, OPC]])), 16)
                blk.sync(dd_fl)

        ag_weight(wd_pc, wd_ga)
        CC_ALL = sCC.n

        ATT_A0, ATT_V0, ATT_M0 = sA.n, sV.n, sM.n
        p_bufs = (p_sb, junk)
        den_cols = (sm_den, sm_nm)
        rd_cols = (sm_rd, sm_ns)
        aSmm, aExp, aNorm, aPtr, aPev, aAv, aAvev = {}, {}, {}, {}, {}, {}, {}
        aKe = {}

        aRd = {}

        def att_norm(v_i):
            def _r(sc, v_i=v_i):
                pv = v_i % 2
                sc.wait_ge(sA.h, aExp[v_i])
                sA.inc(sc.activation(rd_cols[pv][:, 0:1],
                                     den_cols[pv][:, 0:1], AF.Ln))
                sA.w(sc)
                sA.inc(sc.activation(rd_cols[pv][:, 0:1],
                                     rd_cols[pv][:, 0:1], AF.Exp, scale=-1.0))
            blk.scalar(_r)
            aRd[v_i] = sA.n

            def _n(v, v_i=v_i):
                v.wait_ge(sA.h, aRd[v_i])
                kep = aKe[v_i]
                pv = v_i % 2
                sV.inc(v.tensor_scalar_mul(
                    p_bufs[pv][:, 0:2 * kep * 128],
                    p_bufs[pv][:, 0:2 * kep * 128], rd_cols[pv][:, 0:1]))
            blk.vector(_n)
            aNorm[v_i] = sV.n

        def att_front(i, gslot, hh, j):
            ke = j + 1
            h = gslot * (NH // 2) + hh
            bs = hh // (NH // 8)
            hp0 = (h * HD) % 128
            hti = (h * HD) // 128

            def s_mm(t, i=i, bs=bs, j=j, ke=ke, hp0=hp0, hti=hti):
                t.wait_ge(sA.h, aExp.get(i - 1, ATT_A0))
                dG.w(t)
                ins = None
                lhs = qbuf[hp0:hp0 + HD,
                           hti * TC + j * 128:hti * TC + (j + 1) * 128]
                for s in range(2):
                    kbase = (2 * bs + s) * TC
                    done = 0
                    while done < ke * 128:
                        cur = s * ke * 128 + done
                        n = min(512 - (cur % 512), ke * 128 - done)
                        ins = t.matmul(
                            pbig[:, cur:cur + n],
                            lhs, kv_k[:, kbase + done:kbase + done + n],
                            start=True, stop=True)
                        done += n
                sM.inc(ins)
            blk.tensor(s_mm)
            aSmm[i] = sM.n

            aKe[i] = ke
            if i - 1 in aExp and i - 1 not in aNorm:
                att_norm(i - 1)

            def v_mask(v, i=i, ke=ke):
                v.wait_ge(sM.h, aSmm[i])
                v.tensor_add(pbig[:, (ke - 1) * 128:ke * 128],
                             pbig[:, (ke - 1) * 128:ke * 128],
                             cmask[:, 0:128])
                sV.inc(v.tensor_add(pbig[:, (2 * ke - 1) * 128:2 * ke * 128],
                                    pbig[:, (2 * ke - 1) * 128:2 * ke * 128],
                                    cmask[:, 128:256]))
            blk.vector(v_mask)
            aMask = sV.n

            def s_exp(sc, i=i, ke=ke, aMask=aMask):
                sc.wait_ge(sV.h, aMask)
                sc.wait_ge(sM.h, aPtr.get(i - 2, ATT_M0))
                sA.inc(sc.activation(p_bufs[i % 2][:, 0:2 * ke * 128],
                                     pbig[:, 0:2 * ke * 128], AF.Exp,
                                     scale=float(SCL),
                                     accum_out=den_cols[i % 2][:, 0:1]))
            blk.scalar(s_exp)
            aExp[i] = sA.n

        def att_tail(i, gslot, hh, j):
            ke = j + 1
            h = gslot * (NH // 2) + hh
            bs = hh // (NH // 8)
            hp0 = (h * HD) % 128
            hti = (h * HD) // 128
            nk = 2 * ke
            par = i % 2
            for q8 in range(0, nk, 8):
                n8 = min(8, nk - q8)

                def p_tr(t, i=i, q8=q8, n8=n8, par=par):
                    t.wait_ge(sA.h, ATT_A0)
                    t.wait_ge(sV.h, aNorm[i])
                    t.wait_ge(sV.h, aPev.get((i - 2) if q8 == 0 else i, ATT_V0))
                    for k in range(n8):
                        sM.inc(t.transpose(
                            ptp[:, k * 128:(k + 1) * 128],
                            p_bufs[par][:, (q8 + k) * 128:(q8 + k + 1) * 128],
                            ident[:, :]))
                blk.tensor(p_tr)
                aPtr[i] = sM.n

                def p_ev(v, i=i, q8=q8, n8=n8):
                    v.wait_ge(sM.h, aPtr[i])
                    v.wait_ge(sM.h, aAv.get(i - 1, ATT_M0))
                    sV.inc(v.tensor_copy(
                        pt_sb[:, q8 * 128:(q8 + n8) * 128],
                        ptp[:, 0:n8 * 128]))
                blk.vector(p_ev)
                aPev[i] = sV.n

            def av_mm(t, i=i, bs=bs, j=j, ke=ke):
                t.wait_ge(sV.h, aPev[i])
                t.wait_ge(sV.h, aAvev.get(i - 1, ATT_V0))
                kti = 0
                ins = None
                for s in range(2):
                    tb = (2 * bs + s) * TT
                    for kt in range(ke):
                        ins = t.matmul(
                            pcx[0:HD, 0:128],
                            kv_vt[:, (tb + kt) * HD:(tb + kt + 1) * HD],
                            pt_sb[:, (s * ke + kt) * 128:(s * ke + kt + 1) * 128],
                            start=(kti == 0), stop=(kti == 2 * ke - 1))
                        kti += 1
                sM.inc(ins)
            blk.tensor(av_mm)
            aAv[i] = sM.n

            def av_ev(v, i=i, j=j, hp0=hp0, hti=hti):
                v.wait_ge(sM.h, aAv[i])
                sV.inc(v.tensor_copy(
                    ctx_at(hti, j * 128, (j + 1) * 128, hp0, hp0 + HD),
                    pcx[0:HD, 0:128]))
            blk.vector(av_ev)
            aAvev[i] = sV.n

        def attn_group(gslot):
            def kv_load(g, gslot=gslot):
                g.wait_ge(sCC.h, CC_KV)
                sM.w(g)
                dS.w(g)
                with g.register(f"kr{gslot}") as kr, g.register(f"vr{gslot}") as vr:
                    g.reg_load(kr, kvoff[0:1, 2 * gslot:2 * gslot + 1])
                    g.reg_load(vr, kvoff[0:1, 2 * gslot + 1:2 * gslot + 2])
                    kreg = g.snap(kr, min_val=0, max_val=KVD - HD)
                    vreg = g.snap(vr, min_val=0, max_val=KVD - HD)
                    for dst, reg in ((kv_k, kreg), (kv_vw, vreg)):
                        dG.inc(g.dma_start(
                            out=mk(dst, 0, [dst[:, :].ap[0], [TC, NCORES], [1, TC]]),
                            in_=kv_ga[:, bass.ds(reg, HD), :].transpose([1, 0, 2]),
                            allow_slow_non_contiguous=True), 16)
            blk.gpsimd(kv_load)

            for b8 in range(0, ALLT // 128, 8):
                n8 = min(8, ALLT // 128 - b8)

                def v_tr(t, b8=b8, n8=n8):
                    dG.w(t)
                    sV.w(t)
                    sA.w(t)
                    for k in range(n8):
                        sM.inc(t.transpose(
                            ptp[:, k * 128:k * 128 + HD],
                            kv_vw[:, (b8 + k) * 128:(b8 + k + 1) * 128],
                            ident[0:HD, 0:HD]))
                blk.tensor(v_tr)

                def v_ev(v, b8=b8, n8=n8):
                    sM.w(v)
                    dst = mk(kv_vt, b8 * HD, [kv_vt[:, :].ap[0], [HD, n8], [1, HD]])
                    src = mk(ptp, 0, [ptp[:, :].ap[0], [128, n8], [1, HD]])
                    sV.inc(v.tensor_copy(dst, src))
                blk.vector(v_ev)

            base = gslot * (NH // 2) * TT
            pend = None
            for hh in range(NH // 2):
                for j in range(TT):
                    i = base + hh * TT + j
                    att_front(i, gslot, hh, j)
                    if pend is not None:
                        att_tail(*pend)
                    pend = (i, gslot, hh, j)
            att_norm(pend[0])
            att_tail(*pend)

        attn_group(0)
        if dbg:
            dbg_store(blk, d_kk, kv_k[:, :])
            dbg_store(blk, d_vt, kv_vt[:, :])
        attn_group(1)
        if dbg:
            dbg_store(blk, d_ctx[:, 0:LOT * TC], ctx_lo[:, :])
            if HIT:
                dbg_store(blk, d_ctx[:, LOT * TC:DT * TC], ctx_hi[:, 0:HIT * TC])

        gemm_blk(DT, wo_blk, None, qbuf, bias_fn=lambda w: bo_sb[:, w:w + 1],
                 rhs_at=ctx_at)

        if dbg:
            dbg_store(blk, d_y, qbuf[:, :])

        def xln2_slice(k):
            r = k // HPT
            lt = k % HPT
            base = 2 * GPC * r
            assert HPT >= 2 and HPT % 2 == 0
            half = HPT // 2
            if lt < half:
                off = base + 256 * lt
            else:
                off = base + 256 * (lt - half) + 1
            return mk(xlt, off, [xlt[:, :].ap[0], [2, 128]])

        for tt in range(TT):
            def p7_tr_gate(t):
                sA.w(t)
            blk.tensor(p7_tr_gate)
            tr_tile_to_feat(qbuf, lambda k, tt=tt: qbuf[:, k * TC + tt * 128:
                                                        k * TC + (tt + 1) * 128],
                            c_yt, lambda k: k * 128)

            def p7_stats(sc, tt=tt):
                sV.w(sc)
                sA.inc(sc.activation(junk[:, 0:D], c_yt[:, :], AF.Copy,
                                     accum_out=st_s1[:, 8:9]))
                sA.w(sc)
                sA.inc(sc.activation(junk[:, 0:D], c_yt[:, :], AF.Square,
                                     accum_out=st_s2[:, 8:9]))
            blk.scalar(p7_stats)
            ln_finish(8)

            def p7_aff(sc):
                sV.w(sc)
                sA.inc(sc.activation(junk[:, 0:D], c_yt[:, :], AF.Identity,
                                     scale=st_a[:, 8:9], bias=st_b[:, 8:9]))
            blk.scalar(p7_aff)

            def p7_x0(sy, tt=tt):
                sV.w(sy)
                dS.inc(sy.dma_start(out=c_x0[:, :],
                                    in_=x_own[tt * 128:(tt + 1) * 128, :]), 16)
            blk.sync(p7_x0)

            def p7_res(v):
                sA.w(v)
                dS.w(v)
                sV.inc(v.scalar_tensor_tensor(
                    c_x1[:, :], junk[:, 0:D], 1.0, c_x0[:, :],
                    ALU.mult, ALU.add, accum_out=st_s1[:, 9:10]))
            blk.vector(p7_res)

            def p7_sp(sy, tt=tt):
                sV.w(sy)
                dS.inc(sy.dma_start(out=x1_sp[tt * 128:(tt + 1) * 128, :],
                                    in_=c_x1[:, :]), 16)
            blk.sync(p7_sp)

            def p8_stats(sc):
                sV.w(sc)
                sA.inc(sc.activation(junk[:, 0:D], c_x1[:, :], AF.Square,
                                     accum_out=st_s2[:, 9:10]))
            blk.scalar(p8_stats)
            ln_finish(9)

            def p8_aff(sc):
                sV.w(sc)
                sA.inc(sc.activation(xlt[:, :], c_x1[:, :], AF.Identity,
                                     scale=st_a[:, 9:10], bias=st_b[:, 9:10]))
            blk.scalar(p8_aff)

            tr_tile_to_feat(xlt, xln2_slice,
                            xf, lambda k, tt=tt: k * TC + tt * 128)

        if dbg:
            dbg_store(blk, d_xf2, xf[:, :])

            def _dwg(sy):
                sy.wait_ge(sCC.h, CC_ALL)
                dS.inc(sy.dma_start(out=d_wg[:, :], in_=mk(
                    wg_ga, 0 * (IT * 2 * GPC * 128) + 0 * (2 * GPC * 128),
                    [[128, 2 * GPC], [1, 128]])), 16)
            blk.sync(_dwg)

        def ffn_gate_wait(eng):
            eng.wait_ge(sCC.h, CC_ALL)

        FFN_M0, FFN_A0, FFN_V0, FFN_S0 = sM.n, sA.n, sV.n, dS.n
        wg_bufs = (wg_b, wg_b2)
        wu_bufs = (wu_b, wu_b2)
        wd_bufs = (wd_b, wd_b2)
        hs_bufs = (hs_buf, hs_odd)
        snap_gmm, snap_sil, snap_hml, snap_dmm = {}, {}, {}, {}
        Lg, Ld = {}, {}

        def emit_gu_load(kg):
            pb = kg % 2
            need = snap_gmm.get(kg - 2, FFN_M0)

            def _s(sy, kg=kg, pb=pb, need=need):
                if kg < 2:
                    ffn_gate_wait(sy)
                sy.wait_ge(sM.h, need)
                for rt in range(HPT):
                    dS.inc(sy.dma_start(
                        out=mk(wg_bufs[pb], rt * 128,
                               [wg_bufs[pb][:, :].ap[0], [HPT * 128, NCORES],
                                [1, 128]]),
                        in_=mk(wg_ga, kg * (2 * GPC * 128) + rt * 128 * 128,
                               [[128, 128], [IT * 2 * GPC * 128, NCORES],
                                [1, 128]])), 16)
            blk.sync(_s)

            def _g(g, kg=kg, pb=pb, need=need):
                if kg < 2:
                    ffn_gate_wait(g)
                    g.wait_ge(dS.h, FFN_S0)
                g.wait_ge(sM.h, need)
                for rt in range(HPT):
                    dG.inc(g.dma_start(
                        out=mk(wu_bufs[pb], rt * 128,
                               [wu_bufs[pb][:, :].ap[0], [HPT * 128, NCORES],
                                [1, 128]]),
                        in_=mk(wu_ga, kg * (2 * GPC * 128) + rt * 128 * 128,
                               [[128, 128], [IT * 2 * GPC * 128, NCORES],
                                [1, 128]])), 16)
            blk.gpsimd(_g)
            Lg[kg] = (dS.n, dG.n)

        NCH = len([n for n in ICH if n > 0])
        DOWN = []
        kk = 0
        for ic, nich in enumerate(ICH):
            if nich == 0:
                continue
            for r in range(NCORES):
                DOWN.append((ic, r, kk, nich))
            kk += nich
        NDOWN = len(DOWN)
        dl_emitted = set()

        def emit_dload(di):
            ic, r, ic_base, nich = DOWN[di]
            dl_emitted.add(di)
            pb = di % 2
            sem = dS if pb == 0 else dG
            need = snap_dmm.get(di - 2, FFN_M0)

            def _l(e, di=di, r=r, ic_base=ic_base, nich=nich, pb=pb,
                   sem=sem, need=need):
                if di < 2:
                    ffn_gate_wait(e)
                    if pb == 1:
                        e.wait_ge(dS.h, FFN_S0)
                e.wait_ge(sM.h, need)
                sem.inc(e.dma_start(
                    out=mk(wd_bufs[pb], 0,
                           [wd_bufs[pb][:, :].ap[0], [OPC, nich], [1, OPC]]),
                    in_=mk(wd_ga, r * INTER * OPC + ic_base * 128 * OPC,
                           [[OPC, 128], [128 * OPC, nich], [1, OPC]])), 16)
            (blk.sync if pb == 0 else blk.gpsimd)(_l)
            Ld[di] = (sem, sem.n)

        emit_gu_load(0)
        emit_gu_load(1)
        gi = 0
        di_next = 0
        down_end_V = FFN_V0
        kk_glob = 0
        for ic in range(len(ICH)):
            nich = ICH[ic]
            if nich == 0:
                continue
            ic_base = kk_glob
            for kkl in range(nich):
                kg = ic_base + kkl
                for tc in range(NTC):
                    par = gi % 2
                    aneed = snap_sil.get(gi - 2, FFN_A0)
                    vneed = snap_hml.get(gi - 2, FFN_V0)
                    if par == 1:
                        vneed = max(vneed, down_end_V)

                    def f_gmm(t, kg=kg, tc=tc, par=par, pb=kg % 2,
                              aneed=aneed, vneed=vneed):
                        t.wait_ge(dS.h, Lg[kg][0])
                        t.wait_ge(dG.h, Lg[kg][1])
                        t.wait_ge(sA.h, aneed)
                        t.wait_ge(sV.h, vneed)
                        pg = pw0 if par == 0 else pcx
                        pu = pw1 if par == 0 else pbig
                        wgb, wub = wg_bufs[pb], wu_bufs[pb]
                        insu = None
                        for dt in range(DT):
                            t.matmul(
                                pg[:, 0:TCK],
                                wgb[:, dt * 128:(dt + 1) * 128],
                                xf[:, dt * TC + tc * TCK:dt * TC + (tc + 1) * TCK],
                                start=(dt == 0), stop=(dt == DT - 1))
                        for dt in range(DT):
                            insu = t.matmul(
                                pu[:, 0:TCK],
                                wub[:, dt * 128:(dt + 1) * 128],
                                xf[:, dt * TC + tc * TCK:dt * TC + (tc + 1) * TCK],
                                start=(dt == 0), stop=(dt == DT - 1))
                        sM.inc(insu)
                    blk.tensor(f_gmm)
                    snap_gmm[kg] = sM.n

                    def f_silu(sc, par=par):
                        sM.w(sc)
                        pg = pw0 if par == 0 else pcx
                        sA.inc(sc.activation(hs_bufs[par][:, 0:TCK],
                                             pg[:, 0:TCK], AF.Silu))
                    blk.scalar(f_silu)
                    snap_sil[gi] = sA.n

                    def f_hmul(v, kkl=kkl, tc=tc, par=par):
                        sA.w(v)
                        sM.w(v)
                        pu = pw1 if par == 0 else pbig
                        sV.inc(v.tensor_mul(
                            h_sb[:, kkl * TC + tc * TCK:kkl * TC + (tc + 1) * TCK],
                            hs_bufs[par][:, 0:TCK], pu[:, 0:TCK]))
                    blk.vector(f_hmul)
                    snap_hml[gi] = sV.n
                    gi += 1
                if kg + 2 < IT:
                    emit_gu_load(kg + 2)

            for d in (di_next, di_next + 1):
                if d < NDOWN and d not in dl_emitted:
                    emit_dload(d)
            for r in range(NCORES):
                di = di_next + r

                for tc in range(NTC):
                    def f_dmm(t, di=di, tc=tc, nich=nich):
                        sem, val = Ld[di]
                        t.wait_ge(sem.h, val)
                        sV.w(t)
                        ins = None
                        for ot in range(OTC):
                            for kkl in range(nich):
                                ins = t.matmul(
                                    pbig[:, ot * TCK:(ot + 1) * TCK],
                                    wd_bufs[di % 2][:, kkl * OPC + ot * 128:
                                                    kkl * OPC + (ot + 1) * 128],
                                    h_sb[:, kkl * TC + tc * TCK:kkl * TC + (tc + 1) * TCK],
                                    start=(kkl == 0), stop=(kkl == nich - 1))
                        sM.inc(ins)
                    blk.tensor(f_dmm)

                    def f_acc(v, ic=ic, r=r, tc=tc):
                        sM.w(v)
                        for ot in range(OTC):
                            dst = qbuf[:, (r * OTC + ot) * TC + tc * TCK:
                                       (r * OTC + ot) * TC + (tc + 1) * TCK]
                            if ic == 0:
                                ins = v.tensor_copy(dst, pbig[:, ot * TCK:(ot + 1) * TCK])
                            else:
                                ins = v.tensor_add(dst, dst,
                                                   pbig[:, ot * TCK:(ot + 1) * TCK])
                        sV.inc(ins)
                    blk.vector(f_acc)
                snap_dmm[di] = sM.n
                if di + 2 < NDOWN and (di + 2) not in dl_emitted:
                    emit_dload(di + 2)
            di_next += NCORES
            down_end_V = sV.n
            kk_glob += nich

        if dbg:
            dbg_store(blk, d_acc, qbuf[:, :])

        for tt in range(TT):
            tr_tile_to_feat(qbuf, lambda k, tt=tt: qbuf[:, k * TC + tt * 128:
                                                        k * TC + (tt + 1) * 128],
                            c_yt, lambda k: k * 128)

            def f_stats(sc):
                sV.w(sc)
                sA.inc(sc.activation(junk[:, 0:D], c_yt[:, :], AF.Copy,
                                     accum_out=st_s1[:, 10:11]))
                sA.w(sc)
                sA.inc(sc.activation(junk[:, 0:D], c_yt[:, :], AF.Square,
                                     accum_out=st_s2[:, 10:11]))
            blk.scalar(f_stats)
            ln_finish(10)

            def f_aff(sc):
                sV.w(sc)
                sA.inc(sc.activation(junk[:, 0:D], c_yt[:, :], AF.Identity,
                                     scale=st_a[:, 10:11], bias=st_b[:, 10:11]))
            blk.scalar(f_aff)

            def f_x1(sy, tt=tt):
                sV.w(sy)
                dS.inc(sy.dma_start(out=c_x1[:, :],
                                    in_=x1_sp[tt * 128:(tt + 1) * 128, :]), 16)
            blk.sync(f_x1)

            def f_out(v):
                sA.w(v)
                dS.w(v)
                sV.inc(v.scalar_tensor_tensor(
                    c_ot[:, :], junk[:, 0:D], 1.0, c_x1[:, :],
                    ALU.mult, ALU.add))
            blk.vector(f_out)

            def f_store(sy, tt=tt):
                sV.w(sy)
                dS.inc(sy.dma_start(out=out_d[tt * 128:(tt + 1) * 128, :],
                                    in_=c_ot[:, :]), 16)
            blk.sync(f_store)

        def fin(sy):
            dS.w(sy)
        blk.sync(fin)

    stk.close()
    return nc



def stage_inputs(cfg, inp):
    c = cfg_derive(cfg)
    D, HD, TC, TBC = c["D"], c["HD"], c["TC"], c["TBC"]
    DT, KVT, IT = c["DT"], c["KVT"], c["IT"]
    INTER, GPC, GGRP = c["INTER"], c["GPC"], c["GGRP"]
    OPC, DGRP = c["OPC"], c["DGRP"]

    x = np.asarray(inp["L0_attn_x"])
    wq = np.asarray(inp["L0_attn_Wq"])
    wkv = np.asarray(inp["L0_attn_Wkv"])
    wo = np.asarray(inp["L0_attn_Wo"])
    bo = np.asarray(inp["L0_attn_bo"])
    gq = np.asarray(inp["L0_ffn_gate_Wq"])
    uq = np.asarray(inp["L0_ffn_up_Wq"])
    dq = np.asarray(inp["L0_ffn_down_Wq"])
    gsc = np.asarray(inp["L0_ffn_gate_scale"])
    gzr = np.asarray(inp["L0_ffn_gate_zero"])
    usc = np.asarray(inp["L0_ffn_up_scale"])
    uzr = np.asarray(inp["L0_ffn_up_zero"])
    dsc = np.asarray(inp["L0_ffn_down_scale"])
    dzr = np.asarray(inp["L0_ffn_down_zero"])

    CC = np.ascontiguousarray
    wq_blk = CC(wq.reshape(D, DT, 128).transpose(1, 0, 2))
    wkv_blk = CC(wkv.reshape(D, KVT, 128).transpose(1, 0, 2))
    wo_blk = CC(wo.reshape(D, DT, 128).transpose(1, 0, 2))
    bo_st = CC(bo.reshape(DT, 128).T)
    ident = np.eye(128, dtype=ml_dtypes.bfloat16)
    perm_i = np.concatenate([np.arange(0, INTER, 2), np.arange(1, INTER, 2)])
    gq_p = gq[perm_i]
    uq_p = uq[perm_i]
    gsc_p = gsc[perm_i]
    gzr_p = gzr[perm_i]
    usc_p = usc[perm_i]
    uzr_p = uzr[perm_i]

    tri = np.triu(np.full((128, 128), NEG, np.float32), 1)
    full = np.full((128, 128), NEG, np.float32)
    zero = np.zeros((128, 128), np.float32)

    in_maps = []
    for cid in range(NCORES):
        bp, par = cid // 2, cid % 2
        xo = CC(x[bp].reshape(2 * TBC, 128, D)[par::2].reshape(TC, D))
        cm = np.concatenate([tri, full] if par == 0 else [zero, tri],
                            axis=1).astype(np.float32)
        kvo = (np.array([0, HD, 2 * HD, 3 * HD], np.int32)
               + np.int32(4 * HD * bp)).reshape(1, 4)
        ngr = GGRP
        m = dict(
            x_own=xo,
            wq_blk=wq_blk, wkv_blk=wkv_blk, wo_blk=wo_blk, bo_st=bo_st,
            ident_d=ident, cmask_d=CC(cm), kvoffs_d=kvo,
            gate_q=CC(gq_p[:, GPC * cid:GPC * (cid + 1)]),
            up_q=CC(uq_p[:, GPC * cid:GPC * (cid + 1)]),
            down_q=CC(dq[OPC * cid:OPC * (cid + 1), :]),
            g_sc_d=CC(gsc_p[:, ngr * cid:ngr * (cid + 1)]),
            g_zr_d=CC(gzr_p[:, ngr * cid:ngr * (cid + 1)]),
            u_sc_d=CC(usc_p[:, ngr * cid:ngr * (cid + 1)]),
            u_zr_d=CC(uzr_p[:, ngr * cid:ngr * (cid + 1)]),
            d_sc_d=CC(dsc[OPC * cid:OPC * (cid + 1), :]),
            d_zr_d=CC(dzr[OPC * cid:OPC * (cid + 1), :]),
        )
        in_maps.append(m)
    return in_maps


def assemble_out(cfg, shards):
    c = cfg_derive(cfg)
    D, TC, TBC = c["D"], c["TC"], c["TBC"]
    out = np.empty((NCORES // 2, 2 * TC, D), np.float32)
    for cid in range(NCORES):
        bp, par = cid // 2, cid % 2
        blkv = out[bp].reshape(2 * TBC, 128, D)
        blkv[par::2] = shards[cid].reshape(TBC, 128, D)
    return out


def kernel(**inputs):
    cfg = REAL
    in_maps = stage_inputs(cfg, inputs)
    nc = build_nc(cfg)
    res = run_bass_kernel_spmd(nc, in_maps, core_ids=list(range(NCORES)))
    shards = [np.asarray(res.results[i]["out"]) for i in range(NCORES)]
    return assemble_out(cfg, shards)


# version 10 (final)
# speedup vs baseline: 1.4570x; 1.1068x over previous
import numpy as np
import ml_dtypes
import concourse.bass as bass
import concourse.mybir as mybir
from concourse.bass_utils import run_bass_kernel_spmd

F32 = mybir.dt.float32
BF16 = mybir.dt.bfloat16
I32 = mybir.dt.int32
AF = mybir.ActivationFunctionType
ALU = mybir.AluOpType
AX = mybir.AxisListType

NCORES = 8
NEG = -1e9
EPS = 1e-5

REAL = dict(DT=32, HD=128, TBC=8, IT=86, NH=32)
TINY = dict(DT=16, HD=128, TBC=8, IT=40, NH=16)


def cfg_derive(cfg):
    c = dict(cfg)
    c["D"] = c["DT"] * 128
    c["NH"] = c.get("NH", 32)
    c["NG"] = 8
    assert c["NH"] % 8 == 0
    assert c["NH"] * c["HD"] == c["D"]
    c["KVD"] = 2 * c["HD"] * c["NG"]
    c["KVT"] = c["KVD"] // 128
    assert c["KVD"] % 128 == 0
    c["TC"] = c["TBC"] * 128
    c["SEQ"] = 2 * c["TC"]
    c["INTER"] = c["IT"] * 128
    assert (c["D"] // 2) % (NCORES * 32) == 0
    c["GPC"] = c["D"] // 2 // NCORES
    c["GGRP"] = c["GPC"] // 32
    c["HPT"] = 2 * c["GPC"] // 128
    assert (2 * c["GPC"]) % 128 == 0 and c["HPT"] >= 1
    c["OPC"] = c["D"] // NCORES
    c["OTC"] = c["OPC"] // 128
    assert c["OPC"] % 128 == 0
    c["PC"] = c["INTER"] // 2 // 128
    assert (c["INTER"] // 2) % 128 == 0
    c["DGRP"] = c["INTER"] // 64
    it = c["IT"]
    NCH = 16
    base, rem = it // NCH, it % NCH
    c["ICH"] = [base + (1 if i < rem else 0) for i in range(NCH)]
    return c


def mk(t, off, dims):
    return bass.AP(t, off, dims)


def build_nc(cfg, dbg=False):
    import contextlib

    c = cfg_derive(cfg)
    DT, HD, TBC, IT = c["DT"], c["HD"], c["TBC"], c["IT"]
    D, NH, NG, KVT, TC = c["D"], c["NH"], c["NG"], c["KVT"], c["TC"]
    KVD = c["KVD"]
    INTER, GPC, GGRP, HPT = c["INTER"], c["GPC"], c["GGRP"], c["HPT"]
    OPC, OTC, PC, ICH, DGRP = c["OPC"], c["OTC"], c["PC"], c["ICH"], c["DGRP"]
    TT = TBC
    NTC = max(TC // 512, 1)
    TCK = TC // NTC
    SCL = 1.0 / float(np.sqrt(HD))
    ALLT = NCORES * TC

    nc = bass.Bass("TRN2", target_bir_lowering=False, debug=False,
                   detect_race_conditions=True)

    x_own = nc.dram_tensor("x_own", [TC, D], F32, kind="ExternalInput")
    wq_blk = nc.dram_tensor("wq_blk", [DT, D, 128], F32, kind="ExternalInput")
    wkv_blk = nc.dram_tensor("wkv_blk", [KVT, D, 128], F32, kind="ExternalInput")
    wo_blk = nc.dram_tensor("wo_blk", [DT, D, 128], F32, kind="ExternalInput")
    bo_st = nc.dram_tensor("bo_st", [128, DT], F32, kind="ExternalInput")
    ident_d = nc.dram_tensor("ident_d", [128, 128], BF16, kind="ExternalInput")
    cmask_d = nc.dram_tensor("cmask_d", [128, 256], F32, kind="ExternalInput")
    kvoffs_d = nc.dram_tensor("kvoffs_d", [1, 4], I32, kind="ExternalInput")
    gate_q = nc.dram_tensor("gate_q", [INTER, GPC], I32, kind="ExternalInput")
    up_q = nc.dram_tensor("up_q", [INTER, GPC], I32, kind="ExternalInput")
    down_q = nc.dram_tensor("down_q", [OPC, INTER // 2], I32, kind="ExternalInput")
    g_sc_d = nc.dram_tensor("g_sc_d", [INTER, GGRP], F32, kind="ExternalInput")
    g_zr_d = nc.dram_tensor("g_zr_d", [INTER, GGRP], F32, kind="ExternalInput")
    u_sc_d = nc.dram_tensor("u_sc_d", [INTER, GGRP], F32, kind="ExternalInput")
    u_zr_d = nc.dram_tensor("u_zr_d", [INTER, GGRP], F32, kind="ExternalInput")
    d_sc_d = nc.dram_tensor("d_sc_d", [OPC, DGRP], F32, kind="ExternalInput")
    d_zr_d = nc.dram_tensor("d_zr_d", [OPC, DGRP], F32, kind="ExternalInput")
    out_d = nc.dram_tensor("out", [TC, D], F32, kind="ExternalOutput")

    kv_in = nc.dram_tensor("kv_in", [KVD, TC], BF16)
    kv_ga = nc.dram_tensor("kv_ga", [NCORES, KVD, TC], BF16)
    wg_pc = nc.dram_tensor("wg_pc", [IT, 2 * GPC, 128], BF16)
    wu_pc = nc.dram_tensor("wu_pc", [IT, 2 * GPC, 128], BF16)
    wg_ga = nc.dram_tensor("wg_ga", [NCORES, IT, 2 * GPC, 128], BF16)
    wu_ga = nc.dram_tensor("wu_ga", [NCORES, IT, 2 * GPC, 128], BF16)
    wd_pc = nc.dram_tensor("wd_pc", [INTER, OPC], BF16)
    wd_ga = nc.dram_tensor("wd_ga", [NCORES, INTER, OPC], BF16)
    x1_sp = nc.dram_tensor("x1_sp", [TC, D], BF16)
    if dbg:
        d_xf = nc.dram_tensor("d_xf", [128, DT * TC], BF16, kind="ExternalOutput")
        d_q = nc.dram_tensor("d_q", [128, DT * TC], BF16, kind="ExternalOutput")
        d_kv = nc.dram_tensor("d_kv", [128, KVT * TC], BF16, kind="ExternalOutput")
        d_kk = nc.dram_tensor("d_kk", [HD, ALLT], BF16, kind="ExternalOutput")
        d_vt = nc.dram_tensor("d_vt", [128, ALLT * HD // 128], BF16, kind="ExternalOutput")
        d_ctx = nc.dram_tensor("d_ctx", [128, DT * TC], BF16, kind="ExternalOutput")
        d_y = nc.dram_tensor("d_y", [128, DT * TC], BF16, kind="ExternalOutput")
        d_xf2 = nc.dram_tensor("d_xf2", [128, DT * TC], BF16, kind="ExternalOutput")
        d_acc = nc.dram_tensor("d_acc", [128, DT * TC], BF16, kind="ExternalOutput")
        d_wg = nc.dram_tensor("d_wg", [2 * GPC, 128], BF16, kind="ExternalOutput")

    KBy = 1024
    SM = 16512

    def at(name, shape, dtype, off):
        return nc.alloc_sbuf_tensor_at(name, shape, dtype, offset=off)

    ident = at("ident", [128, 128], BF16, SM)
    cmask = at("cmask", [128, 256], F32, SM + 256)
    bo_sb = at("bo_sb", [128, DT], F32, SM + 1280)
    kvoff = at("kvoff", [1, 4], I32, SM + 1472)
    st_s1 = at("st_s1", [128, 16], F32, SM + 1536)
    st_s2 = at("st_s2", [128, 16], F32, SM + 1600)
    st_a = at("st_a", [128, 16], F32, SM + 1664)
    st_b = at("st_b", [128, 16], F32, SM + 1728)
    st_t0 = at("st_t0", [128, 16], F32, SM + 1792)
    st_t1 = at("st_t1", [128, 16], F32, SM + 1856)
    sm_nm = at("sm_nm", [128, 1], F32, SM + 1920)
    sm_ns = at("sm_ns", [128, 1], F32, SM + 1952)
    sm_den = at("sm_den", [128, 1], F32, SM + 1984)
    sm_rd = at("sm_rd", [128, 1], F32, SM + 2016)
    eps_sb = at("eps_sb", [128, 1], F32, SM + 2048)
    hs_buf = at("hs_buf", [128, 512], F32, SM + 2080)
    p_sb = at("p_sb", [128, 2 * TT * 128], BF16, SM + 4128)
    pt_sb = at("pt_sb", [128, 2 * TT * 128], BF16, SM + 4128 + 2 * TT * 128 * 2)
    junk = at("junk", [128, D], BF16, SM + 4128 + 4 * TT * 128 * 2)
    RA = SM + 4128 + 4 * TT * 128 * 2 + 2 * D
    RA = (RA + 31) // 32 * 32

    xf = at("xf", [128, DT * TC], BF16, RA)
    _o = RA
    dq_gq = at("dq_gq", [128, 4 * max(GPC, 256)], I32, _o)
    _o += 4 * max(GPC, 256) * 4
    dq_nib = at("dq_nib", [128, 2 * max(GPC, 256)], I32, _o)
    _o += 2 * max(GPC, 256) * 4
    dq_w = at("dq_w", [128, 2 * max(GPC, 256)], BF16, _o)
    _o += 2 * max(GPC, 256) * 2
    dq_zs = at("dq_zs", [128, 64], F32, _o)
    _o += 256
    g_sc = at("g_sc", [128, IT * GGRP], F32, _o); _o += IT * GGRP * 4
    g_zr = at("g_zr", [128, IT * GGRP], F32, _o); _o += IT * GGRP * 4
    u_sc = at("u_sc", [128, IT * GGRP], F32, _o); _o += IT * GGRP * 4
    u_zr = at("u_zr", [128, IT * GGRP], F32, _o); _o += IT * GGRP * 4
    d_sc = at("d_sc", [128, OTC * DGRP], F32, _o); _o += OTC * DGRP * 4
    d_zr = at("d_zr", [128, OTC * DGRP], F32, _o); _o += OTC * DGRP * 4
    dq_dq = at("dq_dq", [128, OTC * 2 * 128], I32, _o); _o += OTC * 2 * 128 * 4
    stg_g = at("stg_g", [128, HPT * 16 * 128], BF16, _o)
    stg_d = at("stg_d", [128, 2 * 16 * OPC], BF16, _o)
    _o += max(HPT * 16 * 128 * 2, 2 * 16 * OPC * 2)
    kv_k = at("kv_k", [HD, ALLT], BF16, RA)
    kv_vw = at("kv_vw", [HD, ALLT], BF16, RA + ALLT * 2)
    kv_vt = at("kv_vt", [128, (ALLT // 128) * HD], BF16, RA + ALLT * 2)

    KVAREA = 2 * ALLT * 2
    XFSZ = DT * TC * 2
    HIT = min(DT, max(0, (XFSZ - KVAREA) // (TC * 2)))
    LOT = DT - HIT
    ctx_hi = at("ctx_hi", [128, max(HIT, 1) * TC], BF16, RA + KVAREA)
    MID = RA + max(KVAREA, XFSZ)
    x0t = at("x0t", [128, 2 * D], F32, MID)
    kvs = at("kvs", [128, KVT * TC], BF16, MID)
    ctx_lo = at("ctx_lo", [128, LOT * TC], BF16, MID)
    c_x0 = at("c_x0", [128, D], F32, MID)
    c_yt = at("c_yt", [128, D], BF16, MID + D * 4)
    c_x1 = at("c_x1", [128, D], BF16, MID + D * 4 + D * 2)
    c_ot = at("c_ot", [128, D], F32, MID)
    h_sb = at("h_sb", [128, max(ICH) * TC], BF16, MID)
    _f = MID + max(ICH) * TC * 2
    wg_b = at("wg_b", [128, D], BF16, _f); _f += D * 2
    wu_b = at("wu_b", [128, D], BF16, _f); _f += D * 2
    wd_b = at("wd_b", [128, max(ICH) * OPC], BF16, _f)
    _f += max(ICH) * OPC * 2
    MIDSZ = max(LOT * TC * 2, 2 * D * 4, KVT * TC * 2,
                D * 4 + 2 * D * 2, _f - MID)

    def ctx_at(dt, a, b, p0=0, p1=128):
        if dt < LOT:
            return ctx_lo[p0:p1, dt * TC + a:dt * TC + b]
        return ctx_hi[p0:p1, (dt - LOT) * TC + a:(dt - LOT) * TC + b]
    QB = MID + MIDSZ
    QB = (QB + 31) // 32 * 32
    qbuf = at("qbuf", [128, DT * TC], BF16, QB)
    TL = QB + DT * TC * 2
    xlt = at("xlt", [128, D], BF16, TL)
    wqb = at("wqb", [128, 2 * D], BF16, TL + D * 2)
    hs_odd = at("hs_odd", [128, 512], F32, SM + 4128)
    wd_b2 = at("wd_b2", [128, max(ICH) * OPC], BF16, TL)
    wg_b2 = at("wg_b2", [128, D], BF16, TL + D * 2)
    wu_b2 = at("wu_b2", [128, D], BF16, TL + D * 2 + D * 2)
    assert max(ICH) * OPC * 2 <= D * 2
    assert TL + D * 2 + 2 * D * 2 <= 229344, (
        f"sbuf overflow end={TL + D * 2 + 2 * D * 2}")
    assert _o <= 229344

    pbig = nc.alloc_psum_tensor("pbig", [128, 2048], F32)
    pw0 = nc.alloc_psum_tensor("pw0", [128, 512], F32)
    pw1 = nc.alloc_psum_tensor("pw1", [128, 512], F32)
    pcx = nc.alloc_psum_tensor("pcx", [128, 512], F32)
    ptp = nc.alloc_psum_tensor("ptp", [128, 1024], BF16)

    stk = contextlib.ExitStack()

    class SemC:
        def __init__(self, name):
            self.h = stk.enter_context(nc.semaphore(name))
            self.n = 0

        def inc(self, instr, k=1):
            instr.then_inc(self.h, k)
            self.n += k

        def w(self, eng):
            if self.n:
                eng.wait_ge(self.h, self.n)

    dS, dG, sM, sV, sA, sCC, sF = (SemC(x) for x in
                                   ("dS", "dG", "sM", "sV", "sA", "sCC", "sF"))
    dW = [SemC("dW0"), SemC("dW1")]

    def dbg_store(blkh, dst, src_ap):
        if not dbg:
            return

        def _d(sy):
            sA.w(sy)
            sV.w(sy)
            sM.w(sy)
            if isinstance(dst, bass.AP):
                out = dst
            else:
                out = dst[:, :] if dst.shape[0] <= 128 else dst.ap()
            dS.inc(sy.dma_start(out=out, in_=src_ap), 16)
        blkh.sync(_d)

    with nc.Block() as blk:
        def p0(sy):
            for dst, src in ((ident, ident_d), (cmask, cmask_d),
                             (bo_sb, bo_st), (kvoff, kvoffs_d)):
                dS.inc(sy.dma_start(out=dst[:, :], in_=src[:, :]), 16)
        blk.sync(p0)

        def p0v(v):
            sV.inc(v.memset(eps_sb[:, :], float(EPS)))
        blk.vector(p0v)

        def ln_finish(col):
            def _v(v):
                sA.w(v)
                invn = 1.0 / float(D)
                m = st_t0[:, col:col + 1]
                var = st_t1[:, col:col + 1]
                sV.inc(v.tensor_scalar_mul(m, st_s1[:, col:col + 1], invn))
                sV.w(v)
                sV.inc(v.tensor_scalar_mul(var, st_s2[:, col:col + 1], invn))
                sV.w(v)
                sV.inc(v.tensor_mul(st_a[:, col:col + 1], m, m))
                sV.w(v)
                sV.inc(v.tensor_sub(var, var, st_a[:, col:col + 1]))
            blk.vector(_v)

            def _s(sc):
                sV.w(sc)
                sA.inc(sc.activation(st_t1[:, col:col + 1], st_t1[:, col:col + 1],
                                     AF.Ln, bias=eps_sb[:, 0:1]))
                sA.w(sc)
                sA.inc(sc.activation(st_a[:, col:col + 1], st_t1[:, col:col + 1],
                                     AF.Exp, scale=-0.5))
            blk.scalar(_s)

            def _v2(v):
                sA.w(v)
                sV.inc(v.scalar_tensor_tensor(
                    st_b[:, col:col + 1], st_t0[:, col:col + 1], -1.0,
                    st_a[:, col:col + 1], ALU.mult, ALU.mult))
            blk.vector(_v2)

        def tr_tile_to_feat(src_t, slicer, dst_t, dst_col_of_k, wait_extra=None):
            for g8 in range(0, DT, 8):
                n8 = min(8, DT - g8)

                def _t(t, g8=g8, n8=n8):
                    sA.w(t)
                    sV.w(t)
                    if wait_extra is not None:
                        wait_extra(t)
                    for k in range(n8):
                        sM.inc(t.transpose(ptp[:, k * 128:(k + 1) * 128],
                                           slicer(g8 + k), ident[:, :]))
                blk.tensor(_t)

                def _v(v, g8=g8, n8=n8):
                    sM.w(v)
                    c0 = dst_col_of_k(g8)
                    cs = dst_col_of_k(g8 + 1) - c0
                    dst = mk(dst_t, c0, [dst_t[:, :].ap[0], [cs, n8], [1, 128]])
                    src = mk(ptp, 0, [ptp[:, :].ap[0], [128, n8], [1, 128]])
                    sV.inc(v.tensor_copy(dst, src))
                blk.vector(_v)

        for tt in range(TT):
            pb = tt % 2

            def p1_load(sy, tt=tt, pb=pb):
                sA.w(sy)
                dS.inc(sy.dma_start(out=x0t[:, pb * D:(pb + 1) * D],
                                    in_=x_own[tt * 128:(tt + 1) * 128, :]), 16)
            blk.sync(p1_load)

            def p1_stats(sc, tt=tt, pb=pb):
                dS.w(sc)
                xin = x0t[:, pb * D:(pb + 1) * D]
                sA.inc(sc.activation(junk[:, 0:D], xin, AF.Copy,
                                     accum_out=st_s1[:, tt:tt + 1]))
                sA.w(sc)
                sA.inc(sc.activation(junk[:, 0:D], xin, AF.Square,
                                     accum_out=st_s2[:, tt:tt + 1]))
            blk.scalar(p1_stats)
            ln_finish(tt)

            def p1_aff(sc, tt=tt, pb=pb):
                sV.w(sc)
                sA.inc(sc.activation(xlt[:, :], x0t[:, pb * D:(pb + 1) * D],
                                     AF.Identity, scale=st_a[:, tt:tt + 1],
                                     bias=st_b[:, tt:tt + 1]))
            blk.scalar(p1_aff)

            tr_tile_to_feat(xlt, lambda k: xlt[:, k * 128:(k + 1) * 128],
                            xf, lambda k, tt=tt: k * TC + tt * 128)

        def gemm_blk(nblk, wsrc, rhs_sb, dst_sb, bias_fn=None, rhs_at=None):
            if rhs_at is None:
                def rhs_at(dt, a, b, rhs_sb=rhs_sb):
                    return rhs_sb[:, dt * TC + a:dt * TC + b]
            GA0, GV0, GM0 = sA.n, sV.n, sM.n
            evs, mms = {}, {}
            for w in range(nblk):
                pb = w % 2

                def g_load(g, w=w, pb=pb):
                    g.wait_ge(sM.h, mms.get(w - 2, GM0))
                    dW[pb].inc(g.dma_start(
                        out=mk(wqb, pb * D, [wqb[:, :].ap[0], [128, DT], [1, 128]]),
                        in_=mk(wsrc, w * D * 128,
                               [[128, 128], [128 * 128, DT], [1, 128]])), 16)
                blk.gpsimd(g_load)

                for tc in range(NTC):
                    idx = w * NTC + tc
                    ps = pw0 if idx % 2 == 0 else pw1

                    def g_mm(t, w=w, tc=tc, ps=ps, pb=pb, idx=idx):
                        dW[pb].w(t)
                        t.wait_ge(sA.h, evs.get(idx - 2, GA0))
                        t.wait_ge(sV.h, GV0)
                        ins = None
                        for dt in range(DT):
                            ins = t.matmul(
                                ps[:, 0:TCK],
                                wqb[:, pb * D + dt * 128:pb * D + (dt + 1) * 128],
                                rhs_at(dt, tc * TCK, (tc + 1) * TCK),
                                start=(dt == 0), stop=(dt == DT - 1))
                        sM.inc(ins)
                    blk.tensor(g_mm)
                    mms[w] = sM.n

                    def g_ev(sc, w=w, tc=tc, ps=ps):
                        sM.w(sc)
                        fn = AF.Copy
                        kw = {}
                        if bias_fn is not None:
                            kw["bias"] = bias_fn(w)
                            fn = AF.Identity
                        sA.inc(sc.activation(
                            dst_sb[:, w * TC + tc * TCK:w * TC + (tc + 1) * TCK],
                            ps[:, 0:TCK], fn, **kw))
                    blk.scalar(g_ev)
                    evs[idx] = sA.n

        if dbg:
            dbg_store(blk, d_xf, xf[:, :])
        gemm_blk(DT, wq_blk, xf, qbuf)
        gemm_blk(KVT, wkv_blk, xf, kvs)
        if dbg:
            dbg_store(blk, d_q, qbuf[:, :])
            dbg_store(blk, d_kv, kvs[:, :])

        def p2_store_ag(g):
            sA.w(g)
            dG.inc(g.dma_start(
                out=mk(kv_in, 0, [[TC, 128], [128 * TC, KVT], [1, TC]]),
                in_=mk(kvs, 0, [kvs[:, :].ap[0], [TC, KVT], [1, TC]])), 16)
            dG.w(g)
            sCC.inc(g.collective_compute(
                "AllGather", ALU.bypass, replica_groups=[list(range(NCORES))],
                ins=[kv_in[:, :].opt()], outs=[kv_ga.ap().opt()]))
        blk.gpsimd(p2_store_ag)
        CC_KV = sCC.n

        def dq_gateup(qsrc, scs, zrs, piece):
            for it4 in range(0, IT, 4):
                nb = min(4, IT - it4)

                def d_load(sy, it4=it4, nb=nb):
                    sV.w(sy)
                    sM.w(sy)
                    dS.inc(sy.dma_start(
                        out=mk(dq_gq, 0, [dq_gq[:, :].ap[0], [GPC, nb], [1, GPC]]),
                        in_=mk(qsrc, it4 * 128 * GPC,
                               [[GPC, 128], [128 * GPC, nb], [1, GPC]])), 16)
                blk.sync(d_load)

                for k in range(nb):
                    it = it4 + k

                    def d_vec(v, it=it, k=k, scs=scs, zrs=zrs):
                        dS.w(v)
                        sA.w(v)
                        q = dq_gq[:, k * GPC:(k + 1) * GPC]
                        v.tensor_mul(dq_zs[:, 0:GGRP],
                                     zrs[:, it * GGRP:(it + 1) * GGRP],
                                     scs[:, it * GGRP:(it + 1) * GGRP])
                        v.tensor_scalar(dq_nib[:, 0:GPC], q, 4, 15,
                                        ALU.logical_shift_right, ALU.bitwise_and)
                        sV.inc(v.tensor_scalar(dq_nib[:, GPC:2 * GPC], q, 15, None,
                                               ALU.bitwise_and))
                        sV.w(v)
                        scb = mk(scs, it * GGRP,
                                 [scs[:, :].ap[0], [0, 2], [1, GGRP], [0, 32]])
                        zsb = mk(dq_zs, 0,
                                 [dq_zs[:, :].ap[0], [0, 2], [1, GGRP], [0, 32]])
                        nv = mk(dq_nib, 0,
                                [dq_nib[:, :].ap[0], [GPC, 2], [32, GGRP], [1, 32]])
                        wv = mk(dq_w, 0,
                                [dq_w[:, :].ap[0], [GPC, 2], [32, GGRP], [1, 32]])
                        sV.inc(v.tensor_tensor(wv, nv, scb, ALU.mult))
                        sV.w(v)
                        sV.inc(v.tensor_tensor(wv, wv, zsb, ALU.subtract))
                    blk.vector(d_vec)

                    def d_tr(t):
                        sV.w(t)
                        sA.w(t)
                        for rt in range(HPT):
                            sM.inc(t.transpose(ptp[:, rt * 128:(rt + 1) * 128],
                                               dq_w[:, rt * 128:(rt + 1) * 128],
                                               ident[:, :]))
                    blk.tensor(d_tr)

                    def d_ev(sc, it=it):
                        sM.w(sc)
                        if it % 16 == 0 and it > 0:
                            dS.w(sc)
                        kb = it % 16
                        dst = mk(stg_g, kb * 128,
                                 [stg_g[:, :].ap[0], [16 * 128, HPT], [1, 128]])
                        src = mk(ptp, 0, [ptp[:, :].ap[0], [128, HPT], [1, 128]])
                        sA.inc(sc.activation(dst, src, AF.Copy))
                    blk.scalar(d_ev)

                    if (it + 1) % 16 == 0 or it == IT - 1:
                        nkt = (it % 16) + 1
                        kb0 = it - nkt + 1

                        def d_fl(sy, nkt=nkt, kb0=kb0, piece=piece):
                            sA.w(sy)
                            for rt in range(HPT):
                                dS.inc(sy.dma_start(
                                    out=mk(piece,
                                           kb0 * (2 * GPC * 128) + rt * 128 * 128,
                                           [[128, 128], [2 * GPC * 128, nkt], [1, 128]]),
                                    in_=mk(stg_g, rt * 16 * 128,
                                           [stg_g[:, :].ap[0], [128, nkt], [1, 128]])), 16)
                        blk.sync(d_fl)

        def ag_weight(piece, gaout):
            def _r(sy):
                sy.wait_ge(dS.h, dS.n)
                sF.inc(sy.nop())
            blk.sync(_r)

            def _g(gg):
                sF.w(gg)
                sCC.inc(gg.collective_compute(
                    "AllGather", ALU.bypass, replica_groups=[list(range(NCORES))],
                    ins=[piece.ap().opt()], outs=[gaout.ap().opt()]))
            blk.gpsimd(_g)

        for engsec in (blk.sync, blk.scalar, blk.vector, blk.tensor):
            def _fence(e):
                dG.w(e)
                sA.w(e)
                sM.w(e)
                sV.w(e)
            engsec(_fence)

        def p3_scales(sy):
            for dst, src, nt, g in ((g_sc, g_sc_d, IT, GGRP), (g_zr, g_zr_d, IT, GGRP),
                                    (u_sc, u_sc_d, IT, GGRP), (u_zr, u_zr_d, IT, GGRP),
                                    (d_sc, d_sc_d, OTC, DGRP), (d_zr, d_zr_d, OTC, DGRP)):
                dS.inc(sy.dma_start(
                    out=mk(dst, 0, [dst[:, :].ap[0], [g, nt], [1, g]]),
                    in_=mk(src, 0, [[g, 128], [128 * g, nt], [1, g]])), 16)
        blk.sync(p3_scales)

        dq_gateup(gate_q, g_sc, g_zr, wg_pc)
        ag_weight(wg_pc, wg_ga)
        dq_gateup(up_q, u_sc, u_zr, wu_pc)
        ag_weight(wu_pc, wu_ga)

        for pp in range(0, PC, 2):
            npc = min(2, PC - pp)

            def dd_load(sy, pp=pp, npc=npc):
                sV.w(sy)
                sM.w(sy)
                dS.inc(sy.dma_start(
                    out=mk(dq_dq, 0,
                           [dq_dq[:, :].ap[0], [npc * 128, OTC], [1, npc * 128]]),
                    in_=mk(down_q, pp * 128,
                           [[INTER // 2, 128], [128 * (INTER // 2), OTC],
                            [1, npc * 128]])), 16)
            blk.sync(dd_load)

            for ot in range(OTC):
                def dd_vec(v, pp=pp, npc=npc, ot=ot):
                    dS.w(v)
                    sA.w(v)
                    q = dq_dq[:, ot * npc * 128:(ot + 1) * npc * 128]
                    gofs = ot * DGRP + 4 * pp
                    ng = 4 * npc
                    v.tensor_mul(dq_zs[:, 0:ng], d_zr[:, gofs:gofs + ng],
                                 d_sc[:, gofs:gofs + ng])
                    v.tensor_scalar(dq_nib[:, 0:npc * 128], q, 4, 15,
                                    ALU.logical_shift_right, ALU.bitwise_and)
                    sV.inc(v.tensor_scalar(dq_nib[:, npc * 128:2 * npc * 128], q,
                                           15, None, ALU.bitwise_and))
                    sV.w(v)
                    scb = mk(d_sc, gofs, [d_sc[:, :].ap[0], [0, 2], [1, ng], [0, 32]])
                    zsb = mk(dq_zs, 0, [dq_zs[:, :].ap[0], [0, 2], [1, ng], [0, 32]])
                    nv = mk(dq_nib, 0,
                            [dq_nib[:, :].ap[0], [npc * 128, 2], [32, ng], [1, 32]])
                    wv = mk(dq_w, 0,
                            [dq_w[:, :].ap[0], [npc * 128, 2], [32, ng], [1, 32]])
                    sV.inc(v.tensor_tensor(wv, nv, scb, ALU.mult))
                    sV.w(v)
                    sV.inc(v.tensor_tensor(wv, wv, zsb, ALU.subtract))
                blk.vector(dd_vec)

                def dd_tr(t, npc=npc):
                    sV.w(t)
                    sA.w(t)
                    for j in range(2 * npc):
                        sM.inc(t.transpose(ptp[:, j * 128:(j + 1) * 128],
                                           dq_w[:, j * 128:(j + 1) * 128],
                                           ident[:, :]))
                blk.tensor(dd_tr)

                def dd_ev(sc, pp=pp, npc=npc, ot=ot):
                    sM.w(sc)
                    if pp % 16 == 0 and pp > 0 and ot == 0:
                        dS.w(sc)
                    for j in range(npc):
                        slot = (pp + j) % 16
                        sA.inc(sc.activation(
                            stg_d[:, slot * OPC + ot * 128:slot * OPC + ot * 128 + 128],
                            ptp[:, j * 128:(j + 1) * 128], AF.Copy))
                        sA.inc(sc.activation(
                            stg_d[:, (16 + slot) * OPC + ot * 128:
                                  (16 + slot) * OPC + ot * 128 + 128],
                            ptp[:, (npc + j) * 128:(npc + j + 1) * 128], AF.Copy))
                blk.scalar(dd_ev)

            if (pp + npc) % 16 == 0 or pp + npc == PC:
                nkk = ((pp + npc - 1) % 16) + 1
                kb0 = pp + npc - nkk

                def dd_fl(sy, nkk=nkk, kb0=kb0):
                    sA.w(sy)
                    dS.inc(sy.dma_start(
                        out=mk(wd_pc, kb0 * 128 * OPC,
                               [[OPC, 128], [128 * OPC, nkk], [1, OPC]]),
                        in_=mk(stg_d, 0,
                               [stg_d[:, :].ap[0], [OPC, nkk], [1, OPC]])), 16)
                    dS.inc(sy.dma_start(
                        out=mk(wd_pc, (PC + kb0) * 128 * OPC,
                               [[OPC, 128], [128 * OPC, nkk], [1, OPC]]),
                        in_=mk(stg_d, 16 * OPC,
                               [stg_d[:, :].ap[0], [OPC, nkk], [1, OPC]])), 16)
                blk.sync(dd_fl)

        ag_weight(wd_pc, wd_ga)
        CC_ALL = sCC.n

        ATT_A0, ATT_V0, ATT_M0 = sA.n, sV.n, sM.n
        p_bufs = (p_sb, junk)
        den_cols = (sm_den, sm_nm)
        rd_cols = (sm_rd, sm_ns)
        aSmm, aExp, aNorm, aPtr, aPev, aAv, aAvev = {}, {}, {}, {}, {}, {}, {}
        aKe = {}

        aRd = {}

        def att_norm(v_i):
            def _r(sc, v_i=v_i):
                pv = v_i % 2
                sc.wait_ge(sA.h, aExp[v_i])
                sA.inc(sc.activation(rd_cols[pv][:, 0:1],
                                     den_cols[pv][:, 0:1], AF.Ln))
                sA.w(sc)
                sA.inc(sc.activation(rd_cols[pv][:, 0:1],
                                     rd_cols[pv][:, 0:1], AF.Exp, scale=-1.0))
            blk.scalar(_r)
            aRd[v_i] = sA.n

            def _n(v, v_i=v_i):
                v.wait_ge(sA.h, aRd[v_i])
                kep = aKe[v_i]
                pv = v_i % 2
                sV.inc(v.tensor_scalar_mul(
                    p_bufs[pv][:, 0:2 * kep * 128],
                    p_bufs[pv][:, 0:2 * kep * 128], rd_cols[pv][:, 0:1]))
            blk.vector(_n)
            aNorm[v_i] = sV.n

        def att_front(i, gslot, hh, j):
            ke = j + 1
            h = gslot * (NH // 2) + hh
            bs = hh // (NH // 8)
            hp0 = (h * HD) % 128
            hti = (h * HD) // 128

            def s_mm(t, i=i, bs=bs, j=j, ke=ke, hp0=hp0, hti=hti):
                t.wait_ge(sA.h, aExp.get(i - 1, ATT_A0))
                dG.w(t)
                ins = None
                lhs = qbuf[hp0:hp0 + HD,
                           hti * TC + j * 128:hti * TC + (j + 1) * 128]
                for s in range(2):
                    kbase = (2 * bs + s) * TC
                    done = 0
                    while done < ke * 128:
                        cur = s * ke * 128 + done
                        n = min(512 - (cur % 512), ke * 128 - done)
                        ins = t.matmul(
                            pbig[:, cur:cur + n],
                            lhs, kv_k[:, kbase + done:kbase + done + n],
                            start=True, stop=True)
                        done += n
                sM.inc(ins)
            blk.tensor(s_mm)
            aSmm[i] = sM.n

            aKe[i] = ke
            if i - 1 in aExp and i - 1 not in aNorm:
                att_norm(i - 1)

            def v_mask(v, i=i, ke=ke):
                v.wait_ge(sM.h, aSmm[i])
                v.tensor_add(pbig[:, (ke - 1) * 128:ke * 128],
                             pbig[:, (ke - 1) * 128:ke * 128],
                             cmask[:, 0:128])
                sV.inc(v.tensor_add(pbig[:, (2 * ke - 1) * 128:2 * ke * 128],
                                    pbig[:, (2 * ke - 1) * 128:2 * ke * 128],
                                    cmask[:, 128:256]))
            blk.vector(v_mask)
            aMask = sV.n

            def s_exp(sc, i=i, ke=ke, aMask=aMask):
                sc.wait_ge(sV.h, aMask)
                sc.wait_ge(sM.h, aPtr.get(i - 2, ATT_M0))
                sA.inc(sc.activation(p_bufs[i % 2][:, 0:2 * ke * 128],
                                     pbig[:, 0:2 * ke * 128], AF.Exp,
                                     scale=float(SCL),
                                     accum_out=den_cols[i % 2][:, 0:1]))
            blk.scalar(s_exp)
            aExp[i] = sA.n

        def att_tail(i, gslot, hh, j):
            ke = j + 1
            h = gslot * (NH // 2) + hh
            bs = hh // (NH // 8)
            hp0 = (h * HD) % 128
            hti = (h * HD) // 128
            nk = 2 * ke
            par = i % 2
            for q8 in range(0, nk, 8):
                n8 = min(8, nk - q8)

                def p_tr(t, i=i, q8=q8, n8=n8, par=par):
                    t.wait_ge(sA.h, ATT_A0)
                    t.wait_ge(sV.h, aNorm[i])
                    t.wait_ge(sV.h, aPev.get((i - 2) if q8 == 0 else i, ATT_V0))
                    for k in range(n8):
                        sM.inc(t.transpose(
                            ptp[:, k * 128:(k + 1) * 128],
                            p_bufs[par][:, (q8 + k) * 128:(q8 + k + 1) * 128],
                            ident[:, :]))
                blk.tensor(p_tr)
                aPtr[i] = sM.n

                def p_ev(v, i=i, q8=q8, n8=n8):
                    v.wait_ge(sM.h, aPtr[i])
                    v.wait_ge(sM.h, aAv.get(i - 1, ATT_M0))
                    sV.inc(v.tensor_copy(
                        pt_sb[:, q8 * 128:(q8 + n8) * 128],
                        ptp[:, 0:n8 * 128]))
                blk.vector(p_ev)
                aPev[i] = sV.n

            def av_mm(t, i=i, bs=bs, j=j, ke=ke):
                t.wait_ge(sV.h, aPev[i])
                t.wait_ge(sV.h, aAvev.get(i - 1, ATT_V0))
                kti = 0
                ins = None
                for s in range(2):
                    tb = (2 * bs + s) * TT
                    for kt in range(ke):
                        ins = t.matmul(
                            pcx[0:HD, 0:128],
                            kv_vt[:, (tb + kt) * HD:(tb + kt + 1) * HD],
                            pt_sb[:, (s * ke + kt) * 128:(s * ke + kt + 1) * 128],
                            start=(kti == 0), stop=(kti == 2 * ke - 1))
                        kti += 1
                sM.inc(ins)
            blk.tensor(av_mm)
            aAv[i] = sM.n

            def av_ev(v, i=i, j=j, hp0=hp0, hti=hti):
                v.wait_ge(sM.h, aAv[i])
                sV.inc(v.tensor_copy(
                    ctx_at(hti, j * 128, (j + 1) * 128, hp0, hp0 + HD),
                    pcx[0:HD, 0:128]))
            blk.vector(av_ev)
            aAvev[i] = sV.n

        def attn_group(gslot):
            def kv_load(g, gslot=gslot):
                g.wait_ge(sCC.h, CC_KV)
                sM.w(g)
                dS.w(g)
                with g.register(f"kr{gslot}") as kr, g.register(f"vr{gslot}") as vr:
                    g.reg_load(kr, kvoff[0:1, 2 * gslot:2 * gslot + 1])
                    g.reg_load(vr, kvoff[0:1, 2 * gslot + 1:2 * gslot + 2])
                    kreg = g.snap(kr, min_val=0, max_val=KVD - HD)
                    vreg = g.snap(vr, min_val=0, max_val=KVD - HD)
                    for dst, reg in ((kv_k, kreg), (kv_vw, vreg)):
                        dG.inc(g.dma_start(
                            out=mk(dst, 0, [dst[:, :].ap[0], [TC, NCORES], [1, TC]]),
                            in_=kv_ga[:, bass.ds(reg, HD), :].transpose([1, 0, 2]),
                            allow_slow_non_contiguous=True), 16)
            blk.gpsimd(kv_load)

            for b8 in range(0, ALLT // 128, 8):
                n8 = min(8, ALLT // 128 - b8)

                def v_tr(t, b8=b8, n8=n8):
                    dG.w(t)
                    sV.w(t)
                    sA.w(t)
                    for k in range(n8):
                        sM.inc(t.transpose(
                            ptp[:, k * 128:k * 128 + HD],
                            kv_vw[:, (b8 + k) * 128:(b8 + k + 1) * 128],
                            ident[0:HD, 0:HD]))
                blk.tensor(v_tr)

                def v_ev(v, b8=b8, n8=n8):
                    sM.w(v)
                    dst = mk(kv_vt, b8 * HD, [kv_vt[:, :].ap[0], [HD, n8], [1, HD]])
                    src = mk(ptp, 0, [ptp[:, :].ap[0], [128, n8], [1, HD]])
                    sV.inc(v.tensor_copy(dst, src))
                blk.vector(v_ev)

            base = gslot * (NH // 2) * TT
            pend = None
            for hh in range(NH // 2):
                for j in range(TT):
                    i = base + hh * TT + j
                    att_front(i, gslot, hh, j)
                    if pend is not None:
                        att_tail(*pend)
                    pend = (i, gslot, hh, j)
            att_norm(pend[0])
            att_tail(*pend)

        attn_group(0)
        if dbg:
            dbg_store(blk, d_kk, kv_k[:, :])
            dbg_store(blk, d_vt, kv_vt[:, :])
        attn_group(1)
        if dbg:
            dbg_store(blk, d_ctx[:, 0:LOT * TC], ctx_lo[:, :])
            if HIT:
                dbg_store(blk, d_ctx[:, LOT * TC:DT * TC], ctx_hi[:, 0:HIT * TC])

        gemm_blk(DT, wo_blk, None, qbuf, bias_fn=lambda w: bo_sb[:, w:w + 1],
                 rhs_at=ctx_at)

        if dbg:
            dbg_store(blk, d_y, qbuf[:, :])

        def xln2_slice(k):
            r = k // HPT
            lt = k % HPT
            base = 2 * GPC * r
            assert HPT >= 2 and HPT % 2 == 0
            half = HPT // 2
            if lt < half:
                off = base + 256 * lt
            else:
                off = base + 256 * (lt - half) + 1
            return mk(xlt, off, [xlt[:, :].ap[0], [2, 128]])

        for tt in range(TT):
            def p7_tr_gate(t):
                sA.w(t)
            blk.tensor(p7_tr_gate)
            tr_tile_to_feat(qbuf, lambda k, tt=tt: qbuf[:, k * TC + tt * 128:
                                                        k * TC + (tt + 1) * 128],
                            c_yt, lambda k: k * 128)

            def p7_stats(sc, tt=tt):
                sV.w(sc)
                sA.inc(sc.activation(junk[:, 0:D], c_yt[:, :], AF.Copy,
                                     accum_out=st_s1[:, 8:9]))
                sA.w(sc)
                sA.inc(sc.activation(junk[:, 0:D], c_yt[:, :], AF.Square,
                                     accum_out=st_s2[:, 8:9]))
            blk.scalar(p7_stats)
            ln_finish(8)

            def p7_aff(sc):
                sV.w(sc)
                sA.inc(sc.activation(junk[:, 0:D], c_yt[:, :], AF.Identity,
                                     scale=st_a[:, 8:9], bias=st_b[:, 8:9]))
            blk.scalar(p7_aff)

            def p7_x0(sy, tt=tt):
                sV.w(sy)
                dS.inc(sy.dma_start(out=c_x0[:, :],
                                    in_=x_own[tt * 128:(tt + 1) * 128, :]), 16)
            blk.sync(p7_x0)

            def p7_res(v):
                sA.w(v)
                dS.w(v)
                sV.inc(v.scalar_tensor_tensor(
                    c_x1[:, :], junk[:, 0:D], 1.0, c_x0[:, :],
                    ALU.mult, ALU.add, accum_out=st_s1[:, 9:10]))
            blk.vector(p7_res)

            def p7_sp(sy, tt=tt):
                sV.w(sy)
                dS.inc(sy.dma_start(out=x1_sp[tt * 128:(tt + 1) * 128, :],
                                    in_=c_x1[:, :]), 16)
            blk.sync(p7_sp)

            def p8_stats(sc):
                sV.w(sc)
                sA.inc(sc.activation(junk[:, 0:D], c_x1[:, :], AF.Square,
                                     accum_out=st_s2[:, 9:10]))
            blk.scalar(p8_stats)
            ln_finish(9)

            def p8_aff(sc):
                sV.w(sc)
                sA.inc(sc.activation(xlt[:, :], c_x1[:, :], AF.Identity,
                                     scale=st_a[:, 9:10], bias=st_b[:, 9:10]))
            blk.scalar(p8_aff)

            tr_tile_to_feat(xlt, xln2_slice,
                            xf, lambda k, tt=tt: k * TC + tt * 128)

        if dbg:
            dbg_store(blk, d_xf2, xf[:, :])

            def _dwg(sy):
                sy.wait_ge(sCC.h, CC_ALL)
                dS.inc(sy.dma_start(out=d_wg[:, :], in_=mk(
                    wg_ga, 0 * (IT * 2 * GPC * 128) + 0 * (2 * GPC * 128),
                    [[128, 2 * GPC], [1, 128]])), 16)
            blk.sync(_dwg)

        def ffn_gate_wait(eng):
            eng.wait_ge(sCC.h, CC_ALL)

        FFN_M0, FFN_A0, FFN_V0, FFN_S0 = sM.n, sA.n, sV.n, dS.n
        wg_bufs = (wg_b, wg_b2)
        wu_bufs = (wu_b, wu_b2)
        wd_bufs = (wd_b, wd_b2)
        hs_bufs = (hs_buf, hs_odd)
        snap_gmm, snap_sil, snap_hml, snap_dmm = {}, {}, {}, {}
        Lg, Ld = {}, {}

        def emit_gu_load(kg):
            pb = kg % 2
            need = snap_gmm.get(kg - 2, FFN_M0)

            def _s(sy, kg=kg, pb=pb, need=need):
                if kg < 2:
                    ffn_gate_wait(sy)
                sy.wait_ge(sM.h, need)
                for rt in range(HPT):
                    dS.inc(sy.dma_start(
                        out=mk(wg_bufs[pb], rt * 128,
                               [wg_bufs[pb][:, :].ap[0], [HPT * 128, NCORES],
                                [1, 128]]),
                        in_=mk(wg_ga, kg * (2 * GPC * 128) + rt * 128 * 128,
                               [[128, 128], [IT * 2 * GPC * 128, NCORES],
                                [1, 128]])), 16)
            blk.sync(_s)

            def _g(g, kg=kg, pb=pb, need=need):
                if kg < 2:
                    ffn_gate_wait(g)
                    g.wait_ge(dS.h, FFN_S0)
                g.wait_ge(sM.h, need)
                for rt in range(HPT):
                    dG.inc(g.dma_start(
                        out=mk(wu_bufs[pb], rt * 128,
                               [wu_bufs[pb][:, :].ap[0], [HPT * 128, NCORES],
                                [1, 128]]),
                        in_=mk(wu_ga, kg * (2 * GPC * 128) + rt * 128 * 128,
                               [[128, 128], [IT * 2 * GPC * 128, NCORES],
                                [1, 128]])), 16)
            blk.gpsimd(_g)
            Lg[kg] = (dS.n, dG.n)

        NCH = len([n for n in ICH if n > 0])
        DOWN = []
        kk = 0
        for ic, nich in enumerate(ICH):
            if nich == 0:
                continue
            for r in range(NCORES):
                DOWN.append((ic, r, kk, nich))
            kk += nich
        NDOWN = len(DOWN)
        dl_emitted = set()

        def emit_dload(di):
            ic, r, ic_base, nich = DOWN[di]
            dl_emitted.add(di)
            pb = di % 2
            sem = dS if pb == 0 else dG
            need = snap_dmm.get(di - 2, FFN_M0)

            def _l(e, di=di, r=r, ic_base=ic_base, nich=nich, pb=pb,
                   sem=sem, need=need):
                if di < 2:
                    ffn_gate_wait(e)
                    if pb == 1:
                        e.wait_ge(dS.h, FFN_S0)
                e.wait_ge(sM.h, need)
                sem.inc(e.dma_start(
                    out=mk(wd_bufs[pb], 0,
                           [wd_bufs[pb][:, :].ap[0], [OPC, nich], [1, OPC]]),
                    in_=mk(wd_ga, r * INTER * OPC + ic_base * 128 * OPC,
                           [[OPC, 128], [128 * OPC, nich], [1, OPC]])), 16)
            (blk.sync if pb == 0 else blk.gpsimd)(_l)
            Ld[di] = (sem, sem.n)

        emit_gu_load(0)
        emit_gu_load(1)
        gi = 0
        di_next = 0
        down_end_V = FFN_V0
        kk_glob = 0
        for ic in range(len(ICH)):
            nich = ICH[ic]
            if nich == 0:
                continue
            ic_base = kk_glob
            for kkl in range(nich):
                kg = ic_base + kkl
                for tc in range(NTC):
                    par = gi % 2
                    aneed = snap_sil.get(gi - 2, FFN_A0)
                    vneed = snap_hml.get(gi - 2, FFN_V0)
                    if par == 1:
                        vneed = max(vneed, down_end_V)

                    def f_gmm(t, kg=kg, tc=tc, par=par, pb=kg % 2,
                              aneed=aneed, vneed=vneed):
                        t.wait_ge(dS.h, Lg[kg][0])
                        t.wait_ge(dG.h, Lg[kg][1])
                        t.wait_ge(sA.h, aneed)
                        t.wait_ge(sV.h, vneed)
                        pg = pw0 if par == 0 else pcx
                        pu = pw1 if par == 0 else pbig
                        wgb, wub = wg_bufs[pb], wu_bufs[pb]
                        insu = None
                        for dt in range(DT):
                            t.matmul(
                                pg[:, 0:TCK],
                                wgb[:, dt * 128:(dt + 1) * 128],
                                xf[:, dt * TC + tc * TCK:dt * TC + (tc + 1) * TCK],
                                start=(dt == 0), stop=(dt == DT - 1))
                        for dt in range(DT):
                            insu = t.matmul(
                                pu[:, 0:TCK],
                                wub[:, dt * 128:(dt + 1) * 128],
                                xf[:, dt * TC + tc * TCK:dt * TC + (tc + 1) * TCK],
                                start=(dt == 0), stop=(dt == DT - 1))
                        sM.inc(insu)
                    blk.tensor(f_gmm)
                    snap_gmm[kg] = sM.n

                    def f_silu(sc, par=par):
                        sM.w(sc)
                        pg = pw0 if par == 0 else pcx
                        sA.inc(sc.activation(hs_bufs[par][:, 0:TCK],
                                             pg[:, 0:TCK], AF.Silu))
                    blk.scalar(f_silu)
                    snap_sil[gi] = sA.n

                    def f_hmul(v, kkl=kkl, tc=tc, par=par):
                        sA.w(v)
                        sM.w(v)
                        pu = pw1 if par == 0 else pbig
                        sV.inc(v.tensor_mul(
                            h_sb[:, kkl * TC + tc * TCK:kkl * TC + (tc + 1) * TCK],
                            hs_bufs[par][:, 0:TCK], pu[:, 0:TCK]))
                    blk.vector(f_hmul)
                    snap_hml[gi] = sV.n
                    gi += 1
                if kg + 2 < IT:
                    emit_gu_load(kg + 2)

            for d in (di_next, di_next + 1):
                if d < NDOWN and d not in dl_emitted:
                    emit_dload(d)
            for r in range(NCORES):
                di = di_next + r

                for tc in range(NTC):
                    def f_dmm(t, di=di, tc=tc, nich=nich):
                        sem, val = Ld[di]
                        t.wait_ge(sem.h, val)
                        sV.w(t)
                        ins = None
                        for ot in range(OTC):
                            for kkl in range(nich):
                                ins = t.matmul(
                                    pbig[:, ot * TCK:(ot + 1) * TCK],
                                    wd_bufs[di % 2][:, kkl * OPC + ot * 128:
                                                    kkl * OPC + (ot + 1) * 128],
                                    h_sb[:, kkl * TC + tc * TCK:kkl * TC + (tc + 1) * TCK],
                                    start=(kkl == 0), stop=(kkl == nich - 1))
                        sM.inc(ins)
                    blk.tensor(f_dmm)

                    def f_acc(v, ic=ic, r=r, tc=tc):
                        sM.w(v)
                        for ot in range(OTC):
                            dst = qbuf[:, (r * OTC + ot) * TC + tc * TCK:
                                       (r * OTC + ot) * TC + (tc + 1) * TCK]
                            if ic == 0:
                                ins = v.tensor_copy(dst, pbig[:, ot * TCK:(ot + 1) * TCK])
                            else:
                                ins = v.tensor_add(dst, dst,
                                                   pbig[:, ot * TCK:(ot + 1) * TCK])
                        sV.inc(ins)
                    blk.vector(f_acc)
                snap_dmm[di] = sM.n
                if di + 2 < NDOWN and (di + 2) not in dl_emitted:
                    emit_dload(di + 2)
            di_next += NCORES
            down_end_V = sV.n
            kk_glob += nich

        if dbg:
            dbg_store(blk, d_acc, qbuf[:, :])

        for tt in range(TT):
            tr_tile_to_feat(qbuf, lambda k, tt=tt: qbuf[:, k * TC + tt * 128:
                                                        k * TC + (tt + 1) * 128],
                            c_yt, lambda k: k * 128)

            def f_stats(sc):
                sV.w(sc)
                sA.inc(sc.activation(junk[:, 0:D], c_yt[:, :], AF.Copy,
                                     accum_out=st_s1[:, 10:11]))
                sA.w(sc)
                sA.inc(sc.activation(junk[:, 0:D], c_yt[:, :], AF.Square,
                                     accum_out=st_s2[:, 10:11]))
            blk.scalar(f_stats)
            ln_finish(10)

            def f_aff(sc):
                sV.w(sc)
                sA.inc(sc.activation(junk[:, 0:D], c_yt[:, :], AF.Identity,
                                     scale=st_a[:, 10:11], bias=st_b[:, 10:11]))
            blk.scalar(f_aff)

            def f_x1(sy, tt=tt):
                sV.w(sy)
                dS.inc(sy.dma_start(out=c_x1[:, :],
                                    in_=x1_sp[tt * 128:(tt + 1) * 128, :]), 16)
            blk.sync(f_x1)

            def f_out(v):
                sA.w(v)
                dS.w(v)
                sV.inc(v.scalar_tensor_tensor(
                    c_ot[:, :], junk[:, 0:D], 1.0, c_x1[:, :],
                    ALU.mult, ALU.add))
            blk.vector(f_out)

            def f_store(sy, tt=tt):
                sV.w(sy)
                dS.inc(sy.dma_start(out=out_d[tt * 128:(tt + 1) * 128, :],
                                    in_=c_ot[:, :]), 16)
            blk.sync(f_store)

        def fin(sy):
            dS.w(sy)
        blk.sync(fin)

    stk.close()
    return nc



def stage_inputs(cfg, inp):
    c = cfg_derive(cfg)
    D, HD, TC, TBC = c["D"], c["HD"], c["TC"], c["TBC"]
    DT, KVT, IT = c["DT"], c["KVT"], c["IT"]
    INTER, GPC, GGRP = c["INTER"], c["GPC"], c["GGRP"]
    OPC, DGRP = c["OPC"], c["DGRP"]

    x = np.asarray(inp["L0_attn_x"])
    wq = np.asarray(inp["L0_attn_Wq"])
    wkv = np.asarray(inp["L0_attn_Wkv"])
    wo = np.asarray(inp["L0_attn_Wo"])
    bo = np.asarray(inp["L0_attn_bo"])
    gq = np.asarray(inp["L0_ffn_gate_Wq"])
    uq = np.asarray(inp["L0_ffn_up_Wq"])
    dq = np.asarray(inp["L0_ffn_down_Wq"])
    gsc = np.asarray(inp["L0_ffn_gate_scale"])
    gzr = np.asarray(inp["L0_ffn_gate_zero"])
    usc = np.asarray(inp["L0_ffn_up_scale"])
    uzr = np.asarray(inp["L0_ffn_up_zero"])
    dsc = np.asarray(inp["L0_ffn_down_scale"])
    dzr = np.asarray(inp["L0_ffn_down_zero"])

    CC = np.ascontiguousarray
    wq_blk = CC(wq.reshape(D, DT, 128).transpose(1, 0, 2))
    wkv_blk = CC(wkv.reshape(D, KVT, 128).transpose(1, 0, 2))
    wo_blk = CC(wo.reshape(D, DT, 128).transpose(1, 0, 2))
    bo_st = CC(bo.reshape(DT, 128).T)
    ident = np.eye(128, dtype=ml_dtypes.bfloat16)
    perm_i = np.concatenate([np.arange(0, INTER, 2), np.arange(1, INTER, 2)])
    gq_p = gq[perm_i]
    uq_p = uq[perm_i]
    gsc_p = gsc[perm_i]
    gzr_p = gzr[perm_i]
    usc_p = usc[perm_i]
    uzr_p = uzr[perm_i]

    tri = np.triu(np.full((128, 128), NEG, np.float32), 1)
    full = np.full((128, 128), NEG, np.float32)
    zero = np.zeros((128, 128), np.float32)

    in_maps = []
    for cid in range(NCORES):
        bp, par = cid // 2, cid % 2
        xo = CC(x[bp].reshape(2 * TBC, 128, D)[par::2].reshape(TC, D))
        cm = np.concatenate([tri, full] if par == 0 else [zero, tri],
                            axis=1).astype(np.float32)
        kvo = (np.array([0, HD, 2 * HD, 3 * HD], np.int32)
               + np.int32(4 * HD * bp)).reshape(1, 4)
        ngr = GGRP
        m = dict(
            x_own=xo,
            wq_blk=wq_blk, wkv_blk=wkv_blk, wo_blk=wo_blk, bo_st=bo_st,
            ident_d=ident, cmask_d=CC(cm), kvoffs_d=kvo,
            gate_q=CC(gq_p[:, GPC * cid:GPC * (cid + 1)]),
            up_q=CC(uq_p[:, GPC * cid:GPC * (cid + 1)]),
            down_q=CC(dq[OPC * cid:OPC * (cid + 1), :]),
            g_sc_d=CC(gsc_p[:, ngr * cid:ngr * (cid + 1)]),
            g_zr_d=CC(gzr_p[:, ngr * cid:ngr * (cid + 1)]),
            u_sc_d=CC(usc_p[:, ngr * cid:ngr * (cid + 1)]),
            u_zr_d=CC(uzr_p[:, ngr * cid:ngr * (cid + 1)]),
            d_sc_d=CC(dsc[OPC * cid:OPC * (cid + 1), :]),
            d_zr_d=CC(dzr[OPC * cid:OPC * (cid + 1), :]),
        )
        in_maps.append(m)
    return in_maps


def assemble_out(cfg, shards):
    c = cfg_derive(cfg)
    D, TC, TBC = c["D"], c["TC"], c["TBC"]
    out = np.empty((NCORES // 2, 2 * TC, D), np.float32)
    for cid in range(NCORES):
        bp, par = cid // 2, cid % 2
        blkv = out[bp].reshape(2 * TBC, 128, D)
        blkv[par::2] = shards[cid].reshape(TBC, 128, D)
    return out


def kernel(**inputs):
    cfg = REAL
    in_maps = stage_inputs(cfg, inputs)
    nc = build_nc(cfg)
    res = run_bass_kernel_spmd(nc, in_maps, core_ids=list(range(NCORES)))
    shards = [np.asarray(res.results[i]["out"]) for i in range(NCORES)]
    return assemble_out(cfg, shards)
